# version 11
# baseline (speedup 1.0000x reference)
import math
from contextlib import ExitStack

import numpy as np
import ml_dtypes

import concourse.bass as bass
import concourse.mybir as mybir
from concourse.bass_utils import run_bass_kernel_spmd

F32 = mybir.dt.float32
BF16 = mybir.dt.bfloat16
U8 = mybir.dt.uint8
AF = mybir.ActivationFunctionType
ALU = mybir.AluOpType
bfdt = ml_dtypes.bfloat16

NCORES = 8
DM = 2048
DFF = 5632
NDC = DM // 128
NFC = DFF // 128
TOK = 1024
HALF = 512
NSUB = HALF // 128
NH, DH, CH = 16, 128, 64
NCH = HALF // CH
EPS_LN = 1e-5
SCALE = float(DH) ** 0.5

NF4 = [float(bfdt(v)) for v in (-1.0, -0.6962, -0.5251, -0.3949, -0.2844, -0.1848,
                                -0.0911, 0.0, 0.0796, 0.1609, 0.2461, 0.3379,
                                0.4407, 0.5626, 0.723, 1.0)]
MID = [(NF4[j] + NF4[j + 1]) * 0.5 for j in range(15)]

W12_SH = DFF // 4
W12_FC = W12_SH // 128
W3_SH = DM // 8

ENGINES = ("sync", "tensor", "vector", "scalar", "gpsimd")


class Sem:
    __slots__ = ("h", "n", "name")

    def __init__(self, h, name):
        self.h = h
        self.n = 0
        self.name = name


class B:
    def __init__(self, nc):
        self.nc = nc
        self.q = {e: [] for e in ENGINES}
        self.last_wait = {e: {} for e in ENGINES}

    def emit(self, eng, fn, waits=(), inc=None, k=1):
        ws = []
        lw = self.last_wait[eng]
        for s, v in waits:
            if s is None or v is None or v <= 0:
                continue
            if lw.get(s.name, -1) >= v:
                continue
            lw[s.name] = v
            ws.append((s.h, v))
        tok = None
        if inc is not None:
            inc.n += k
            tok = inc.n
        self.q[eng].append((ws, fn, None if inc is None else inc.h, k))
        return tok

    def replay(self, block):
        for e in ENGINES:
            ops = self.q[e]

            def run(eng, ops=ops):
                for ws, fn, inch, k in ops:
                    for h, v in ws:
                        eng.wait_ge(h, v)
                    ins = fn(eng)
                    if inch is not None:
                        ins.then_inc(inch, k)

            getattr(block, e)(run)


def ts_(i, n=1):
    return slice(i * 128, (i + n) * 128)


NOACT = False
NOMM = False
GV_PLAIN_COPY = False
NO_TRANSPOSE = False
TR_MODE = "normal"
QK_TRANSPOSES = False
BETA_SUBS = 4
BETA_IN_Q2 = False
LN_SKIP = set()


def build_forward(stage="full", fused=False):
    nc = bass.Bass("TRN2", target_bir_lowering=False)

    x_in = nc.dram_tensor("x_in", [DM, TOK], F32, kind="ExternalInput").ap()
    if fused:
        rw12 = nc.dram_tensor("rw12", [W12_SH, DM], BF16, kind="ExternalInput").ap()
        cm12 = nc.dram_tensor("cm12", [1, 4], F32, kind="ExternalInput").ap()
        cnmid = nc.dram_tensor("cnmid", [128, 15], F32, kind="ExternalInput").ap()
        cm3 = nc.dram_tensor("cm3", [1, 4], F32, kind="ExternalInput").ap()
        ccin = nc.dram_tensor("ccin", [1, 4], F32).ap()
        ccout = nc.dram_tensor("ccout", [1, 4], F32).ap()
        rw3 = nc.dram_tensor("rw3", [W3_SH, DFF], BF16, kind="ExternalInput").ap()
        crow88 = nc.dram_tensor("crow88", [1, 88], F32, kind="ExternalInput").ap()
        cone88 = nc.dram_tensor("cone88", [88, 1], F32, kind="ExternalInput").ap()
        ag1s = nc.dram_tensor("ag1s", [W12_SH, DM], BF16).ap()
        ag2s = nc.dram_tensor("ag2s", [W3_SH, DFF], BF16).ap()
        wag12 = nc.dram_tensor("wag12", [8 * W12_SH, DM], BF16).ap()
        wag3 = nc.dram_tensor("wag3", [8 * W3_SH, DFF], BF16).ap()
        dq12 = nc.dram_tensor("dq12", [W12_SH, 32], BF16).ap()
        rc12 = nc.dram_tensor("rc12", [W12_SH, 32], F32).ap()
        dq3 = nc.dram_tensor("dq3", [W3_SH, 88], BF16).ap()
        rc3 = nc.dram_tensor("rc3", [W3_SH, 88], F32).ap()
    else:
        wag12 = nc.dram_tensor("wag12", [8 * W12_SH, DM],
                               BF16, kind="ExternalInput").ap()
        wag3 = nc.dram_tensor("wag3", [8 * W3_SH, DFF],
                              BF16, kind="ExternalInput").ap()
    wq = nc.dram_tensor("wq", [DM, DM], F32, kind="ExternalInput").ap()
    wk = nc.dram_tensor("wk", [DM, DM], F32, kind="ExternalInput").ap()
    wv = nc.dram_tensor("wv", [DM, DM], F32, kind="ExternalInput").ap()
    wog = nc.dram_tensor("wog", [DM, DM], F32, kind="ExternalInput").ap()
    wo = nc.dram_tensor("wo", [DM, DM], F32, kind="ExternalInput").ap()
    wbt = nc.dram_tensor("wbt", [DM, NH], F32, kind="ExternalInput").ap()
    qb = nc.dram_tensor("qb", [128, NDC], F32, kind="ExternalInput").ap()
    kb = nc.dram_tensor("kb", [128, NDC], F32, kind="ExternalInput").ap()
    vb = nc.dram_tensor("vb", [128, NDC], F32, kind="ExternalInput").ap()
    ob = nc.dram_tensor("ob", [128, NDC], F32, kind="ExternalInput").ap()
    bb = nc.dram_tensor("bb", [1, NH], F32, kind="ExternalInput").ap()
    cmask = nc.dram_tensor("cmask", [128, 128], F32, kind="ExternalInput").ap()
    cones_col = nc.dram_tensor("cones_col", [128, 1], BF16, kind="ExternalInput").ap()
    cones_row = nc.dram_tensor("cones_row", [1, 128], F32, kind="ExternalInput").ap()
    cident = nc.dram_tensor("cident", [128, 128], BF16, kind="ExternalInput").ap()

    y_out = nc.dram_tensor("y_out", [DM, TOK], F32, kind="ExternalOutput").ap()
    wcache = nc.dram_tensor("wcache", [5 * DM, DM], BF16).ap()

    with ExitStack() as ctx:
        E = ctx.enter_context

        bbs = E(nc.sbuf_tensor("bbs", [128, NH], F32))
        wbres = E(nc.sbuf_tensor("wbres", [128, NDC * NH], BF16))
        bscr = E(nc.sbuf_tensor("bscr", [128, NH], F32))
        bscr2 = E(nc.sbuf_tensor("bscr2", [128, NH], F32))
        beta = E(nc.sbuf_tensor("beta", [128, NSUB * NH], F32))
        xh = E(nc.sbuf_tensor("xh", [128, NDC * HALF], F32))
        xbf = E(nc.sbuf_tensor("xbf", [128, NDC * HALF], BF16))
        xln = E(nc.sbuf_tensor("xln", [128, NDC * HALF], BF16))
        arena = E(nc.sbuf_tensor("arena", [128, NFC * HALF], BF16))
        wbA = E(nc.sbuf_tensor("wbA", [128, 2 * DM], BF16))
        wbB = E(nc.sbuf_tensor("wbB", [128, 2 * DM], BF16))
        wb3 = E(nc.sbuf_tensor("wb3", [128, 2 * DFF], BF16))
        wf32 = E(nc.sbuf_tensor("wf32", [128, 2 * DM], F32))
        sc1 = E(nc.sbuf_tensor("sc1", [1, 4 * HALF], F32))
        bcB = E(nc.sbuf_tensor("bcB", [128, HALF], BF16))
        bcC = E(nc.sbuf_tensor("bcC", [128, HALF], BF16))
        lntmp = E(nc.sbuf_tensor("lntmp", [128, HALF], BF16))
        qbs = E(nc.sbuf_tensor("qbs", [128, NDC], F32))
        kbs = E(nc.sbuf_tensor("kbs", [128, NDC], F32))
        vbs = E(nc.sbuf_tensor("vbs", [128, NDC], F32))
        obs = E(nc.sbuf_tensor("obs", [128, NDC], F32))
        msk = E(nc.sbuf_tensor("msk", [128, 128], F32))
        onc = E(nc.sbuf_tensor("onc", [128, 1], BF16))
        onr = E(nc.sbuf_tensor("onr", [1, 128], F32))
        idn = E(nc.sbuf_tensor("idn", [128, 128], BF16))
        q2 = E(nc.sbuf_tensor("q2", [128, HALF], BF16))
        vt2 = E(nc.sbuf_tensor("vt2", [128, 2 * HALF], BF16))
        mskball = E(nc.sbuf_tensor("mskball", [128, NSUB * NH * 128], BF16))

        if fused:
            sqq = E(nc.sbuf_tensor("sqq", [88, 6 * 256], F32))
            dqs = E(nc.sbuf_tensor("dqs", [88, 256], BF16))
            cents = E(nc.sbuf_tensor("cents", [88, 256], BF16))
            smalls = E(nc.sbuf_tensor("smalls", [88, 16], F32))
            s11 = E(nc.sbuf_tensor("s11", [1, 8], F32))
            s11b = E(nc.sbuf_tensor("s11b", [1, 2], BF16))
            r88 = E(nc.sbuf_tensor("r88", [1, 88], F32))
            c88 = E(nc.sbuf_tensor("c88", [88, 1], F32))
            m12row = E(nc.sbuf_tensor("m12row", [1, 4], F32))
            m3row = E(nc.sbuf_tensor("m3row", [1, 4], F32))
            nmid = E(nc.sbuf_tensor("nmid", [128, 15], F32))
        PS1 = E(nc.psum_tensor("PS1", [128, 1024], F32))
        PS2 = E(nc.psum_tensor("PS2", [128, 1024], F32))
        PS3 = E(nc.psum_tensor("PS3", [128, 1024], F32))
        PT = E(nc.psum_tensor("PT", [128, 2048], BF16))

        b = B(nc)
        semnames = ("dX", "dC", "dP", "sBR", "sWA0", "sWA1", "sWB0", "sWB1",
                    "sW30", "sW31", "sWF0", "sWF1", "sCA", "sCB", "eT", "eV", "eA",
                    "sQ0", "sQ1", "sR0", "sR1", "dQW", "dO", "ccS", "eP",
                    "sWA2", "sWB2")
        sem = {n: Sem(E(nc.semaphore(n)), n) for n in semnames}
        dX, dC, dP, sBR = sem["dX"], sem["dC"], sem["dP"], sem["sBR"]
        sWA = (sem["sWA0"], sem["sWA1"])
        sWB = (sem["sWB0"], sem["sWB1"])
        sW3 = (sem["sW30"], sem["sW31"])
        sWF = (sem["sWF0"], sem["sWF1"])
        sCC = (sem["sCA"], sem["sCB"])
        eT, eV, eA = sem["eT"], sem["eV"], sem["eA"]
        sQ = (sem["sQ0"], sem["sQ1"])
        sR = (sem["sR0"], sem["sR1"])
        dQW, dO, ccS = sem["dQW"], sem["dO"], sem["ccS"]

        def dma(queue, out, in_, s, waits=()):
            return b.emit(queue, lambda e, o=out, i=in_: e.dma_start(out=o, in_=i),
                          waits=waits, inc=s, k=16)

        def mm(out, lhsT, rhs, start, stop, waits=(), inc=None):
            return b.emit("tensor",
                          lambda e, o=out, l=lhsT, r=rhs, st=start, sp=stop:
                          e.matmul(o, l, r, start=st, stop=sp),
                          waits=waits, inc=inc)

        def vop(fn, waits=()):
            return b.emit("vector", fn, waits=waits, inc=eV)

        def aop(fn, waits=()):
            return b.emit("scalar", fn, waits=waits, inc=eA)

        def xh_(dc):
            return xh[:, dc * HALF:(dc + 1) * HALF]

        def xbf_(dc):
            return xbf[:, dc * HALF:(dc + 1) * HALF]

        def xln_(dc):
            return xln[:, dc * HALF:(dc + 1) * HALF]

        def ar_(fc):
            return arena[:, fc * HALF:(fc + 1) * HALF]

        def slab_tile(buf, off, idx, width):
            return buf[:, off:off + NDC * width].rearrange(
                "p (c f) -> p c f", f=width)[:, idx, :]

        dma("sync", msk[:], cmask, dC)
        dma("sync", onc[:], cones_col, dC)
        dma("sync", onr[:], cones_row, dC)
        dma("sync", qbs[:], qb, dC)
        dma("sync", kbs[:], kb, dC)
        dma("sync", vbs[:], vb, dC)
        dma("sync", obs[:], ob, dC)
        dma("sync", idn[:], cident, dC)
        t_wbres = dma("gpsimd", wbres[:].rearrange("p (c n) -> p c n", n=NH),
                      wbt.rearrange("(c p) n -> p c n", p=128), sBR)
        t_bbrow = dma("sync", sc1[0:1, 0:NH], bb, dC)
        t_consts = t_bbrow
        for pi_, wsrc_ in enumerate((wq, wk, wog, wv, wo)):
            dma("gpsimd", wcache[DM * pi_:DM * (pi_ + 1), :], wsrc_,
                sem["sWF0"])
        cbar = [(dC, dC.n), (sBR, sBR.n)]
        for _e in ENGINES:
            b.emit(_e, lambda e: e.nop(), waits=cbar)
        t = mm(PS3[:, 0:NH], onr[:], sc1[0:1, 0:NH], True, True, inc=eT)
        t_bbs = vop(lambda e: e.tensor_copy(bbs[:], PS3[:, 0:NH]), waits=[(eT, t)])

        if fused:
            dma("sync", r88[:], crow88, dC)
            dma("sync", c88[:], cone88, dC)
            dma("sync", m12row[:], cm12, dC)
            dma("sync", m3row[:], cm3, dC)
            dma("sync", nmid[:], cnmid, dC)
            qt_ = (lambda: arena[0:88, 0:16384],
                   lambda: xh[:].bitcast(BF16)[0:88, 0:16384])
            scalfs = (wf32[0:88, 0:2048], wf32[0:88, 2048:4096])

            def q_fullpass(src, scaldst, t0=None, t0p=None):
                tprev = [t0, t0p]
                tv = tvp = None
                for t in range(8):
                    par = t % 2
                    w = ([(eV, tprev[par])] if tprev[par] is not None
                         else [(dC, dC.n)])
                    tl = dma("sync" if par == 0 else "scalar", qt_[par](),
                             src.rearrange("a bx -> (a bx)")
                             .rearrange("(t p f) -> t p f", t=8, p=88)[t, :, :],
                             sQ[par], waits=w)
                    fn = (lambda e, par=par, t=t: e.tensor_reduce(
                        scaldst[:, t * 256:(t + 1) * 256],
                        qt_[par]().rearrange("p (c z) -> p c z", z=64),
                        axis=mybir.AxisListType.X, op=ALU.max,
                        apply_absolute_value=True))
                    tv = vop(fn, waits=[(sQ[par], tl)])
                    tprev[par] = tv
                return tv, tvp

            def q_shard_scalers(rsrc, ntiles, dst, t0=None):
                tprev = [t0, t0]
                tv = None
                for t in range(ntiles):
                    par = t % 2
                    w = ([(eV, tprev[par])] if tprev[par] is not None
                         else [(dC, dC.n)])
                    tl = dma("sync" if par == 0 else "scalar", qt_[par](),
                             rsrc.rearrange("a bx -> (a bx)")
                             .rearrange("(t p f) -> t p f", t=ntiles, p=88)[t, :, :],
                             sQ[par], waits=w)
                    tv = vop(lambda e, par=par, t=t: e.tensor_reduce(
                        dst[:, t * 256:(t + 1) * 256],
                        qt_[par]().rearrange("p (c z) -> p c z", z=64),
                        axis=mybir.AxisListType.X, op=ALU.max,
                        apply_absolute_value=True), waits=[(sQ[par], tl)])
                    tprev[par] = tv
                return tv

            scal12k = wf32[0:88, 0:512]
            scal3k = wf32[0:88, 512:768]
            ts12 = q_shard_scalers(rw12, 2, scal12k)
            ts3 = q_shard_scalers(rw3, 1, scal3k, t0=ts12)

            rs12 = smalls[:, 12:13]
            rs3 = smalls[:, 13:14]
            t1 = vop(lambda e: e.tensor_reduce(rs12, scal12k,
                                               axis=mybir.AxisListType.X,
                                               op=ALU.add), waits=[(eV, ts12)])
            t1b = vop(lambda e: e.tensor_reduce(rs3, scal3k,
                                                axis=mybir.AxisListType.X,
                                                op=ALU.add), waits=[(eV, ts3)])
            t2 = mm(PS3[0:1, 0:1], c88[:], rs12, True, True,
                    waits=[(eV, t1)], inc=eT)
            t2b = mm(PS3[0:1, 1:2], c88[:], rs3, True, True,
                     waits=[(eV, t1b)], inc=eT)
            t3 = vop(lambda e: e.tensor_copy(s11[0:1, 0:1], PS3[0:1, 0:1]),
                     waits=[(eT, t2)])
            t3b = vop(lambda e: e.tensor_copy(s11[0:1, 1:2], PS3[0:1, 1:2]),
                      waits=[(eT, t2b)])
            t4 = vop(lambda e: e.tensor_scalar(s11[0:1, 4:8], m12row[:],
                                               s11[0:1, 0:1], None, op0=ALU.mult),
                     waits=[(eV, t3)])
            t5 = vop(lambda e: e.scalar_tensor_tensor(s11[0:1, 4:8], m3row[:],
                                                      s11[0:1, 1:2],
                                                      s11[0:1, 4:8],
                                                      op0=ALU.mult, op1=ALU.add),
                     waits=[(eV, t4), (eV, t3b)])
            tcw = dma("gpsimd", ccin, s11[0:1, 4:8], dQW, waits=[(eV, t5)])
            tcc = b.emit("gpsimd",
                         lambda e: e.collective_compute(
                             "AllReduce", ALU.add,
                             replica_groups=[list(range(8))],
                             ins=[ccin.opt()], outs=[ccout.opt()]),
                         waits=[(dQW, tcw)], inc=ccS, k=1)
            tcr = dma("gpsimd", s11[0:1, 4:8], ccout, dQW, waits=[(ccS, tcc)])

            def q_mean(mrow, mcol_off, s11_off):
                t6 = vop(lambda e: e.tensor_tensor(s11b[0:1, 0:4].bitcast(F32)
                                                   [0:1, 0:2][0:1, 0:2]
                                                   if False else
                                                   smalls[0:1, 8:12],
                                                   s11[0:1, 4:8], mrow[:],
                                                   op=ALU.mult),
                         waits=[(dQW, tcr)])
                t7 = vop(lambda e: e.tensor_reduce(s11[0:1, s11_off:s11_off + 1],
                                                   smalls[0:1, 8:12],
                                                   axis=mybir.AxisListType.X,
                                                   op=ALU.add), waits=[(eV, t6)])
                t8 = vop(lambda e: e.tensor_scalar(s11[0:1, s11_off:s11_off + 1],
                                                   s11[0:1, s11_off:s11_off + 1],
                                                   1.0 / 180224.0, None,
                                                   op0=ALU.mult), waits=[(eV, t7)])
                t9 = vop(lambda e: e.tensor_copy(s11b[0:1, s11_off:s11_off + 1],
                                                 s11[0:1, s11_off:s11_off + 1]),
                         waits=[(eV, t8)])
                t10 = vop(lambda e: e.tensor_copy(s11[0:1, 2 + s11_off:3 + s11_off],
                                                  s11b[0:1, s11_off:s11_off + 1]),
                          waits=[(eV, t9)])
                t11 = mm(PS3[0:88, 1:2], r88[:], s11[0:1, 2 + s11_off:3 + s11_off],
                         True, True, waits=[(eV, t10)], inc=eT)
                t12 = vop(lambda e: e.tensor_copy(smalls[:, mcol_off:mcol_off + 1],
                                                  PS3[0:88, 1:2]), waits=[(eT, t11)])
                return t12

            tm12 = q_mean(m12row, 0, 0)
            tm3 = q_mean(m3row, 2, 1)
            tf3 = tm3

            def q_scaler_stage(scal_res, ntiles, mcol_off, tmean, dq_d, rc_d,
                               tprevload):
                mcol = smalls[:, mcol_off:mcol_off + 1]
                S = lambda a: sqq[:, a * 256:(a + 1) * 256]
                yv, tv, Fv, bb2, rcs = S(1), S(2), S(3), S(4), S(5)
                m1 = smalls[:, 8:9]
                qf = smalls[:, 9:10]
                rqf = smalls[:, 10:11]
                tlast = tprevload
                for st in range(ntiles):
                    scal = scal_res[:, st * 256:(st + 1) * 256]
                    t1 = tlast
                    t2 = vop(lambda e, scal=scal: e.tensor_scalar(
                        cents[:], scal, mcol, None, op0=ALU.subtract),
                        waits=[(eV, tmean), (dQW, dQW.n)] +
                              ([(eV, tlast)] if tlast is not None else []))
                    t3 = vop(lambda e: e.tensor_reduce(m1, cents[:],
                                                       axis=mybir.AxisListType.X,
                                                       op=ALU.max,
                                                       apply_absolute_value=True),
                             waits=[(eV, t2)])
                    t4 = vop(lambda e: e.reciprocal(qf, m1), waits=[(eV, t3)])
                    t5 = vop(lambda e: e.tensor_scalar(qf, qf, 128.0, None,
                                                       op0=ALU.mult),
                             waits=[(eV, t4)])
                    t6 = vop(lambda e: e.tensor_scalar(yv, cents[:], qf, None,
                                                       op0=ALU.mult),
                             waits=[(eV, t5)])
                    t7 = vop(lambda e: e.tensor_scalar(tv, yv, 128.5, None,
                                                       op0=ALU.add),
                             waits=[(eV, t6)])
                    t8 = vop(lambda e: e.memset(Fv, 0.0), waits=[(eV, t7)])
                    tk = t8
                    for k in range(8, -1, -1):
                        ka = vop(lambda e, k=k: e.tensor_scalar(
                            bb2, tv, float(2 ** k), None, op0=ALU.is_ge),
                            waits=[(eV, tk)])
                        kb = vop(lambda e, k=k: e.scalar_tensor_tensor(
                            tv, bb2, float(-(2 ** k)), tv, op0=ALU.mult,
                            op1=ALU.add), waits=[(eV, ka)])
                        tk = vop(lambda e, k=k: e.scalar_tensor_tensor(
                            Fv, bb2, float(2 ** k), Fv, op0=ALU.mult,
                            op1=ALU.add), waits=[(eV, kb)])
                    t9 = vop(lambda e: e.tensor_scalar(Fv, Fv, -128.0, 127.0,
                                                       op0=ALU.add, op1=ALU.min),
                             waits=[(eV, tk)])
                    t10 = vop(lambda e: e.tensor_scalar(Fv, Fv, -128.0, None,
                                                        op0=ALU.max),
                              waits=[(eV, t9)])
                    t11 = vop(lambda e: e.reciprocal(rqf, qf), waits=[(eV, t10)])
                    t12 = vop(lambda e: e.tensor_scalar(dqs[:], Fv, rqf, None,
                                                        op0=ALU.mult),
                              waits=[(eV, t11)])
                    t13 = vop(lambda e: e.tensor_scalar(dqs[:], dqs[:], mcol, None,
                                                        op0=ALU.add),
                              waits=[(eV, t12)])
                    t14 = vop(lambda e, scal=scal: e.reciprocal(rcs, scal),
                              waits=[(eV, t13)])
                    dma("gpsimd", dq_d.rearrange("a bx -> (a bx)")
                        [st * 22528:(st + 1) * 22528]
                        .rearrange("(p c) -> p c", p=88), dqs[:], dQW,
                        waits=[(eV, t13)])
                    dma("gpsimd", rc_d.rearrange("a bx -> (a bx)")
                        [st * 22528:(st + 1) * 22528]
                        .rearrange("(p c) -> p c", p=88), rcs, dQW,
                        waits=[(eV, t14)])
                    tlast = t14
                return tlast

            tq12 = q_scaler_stage(scal12k, 2, 0, tm12, dq12, rc12, tm3)
            tq3 = q_scaler_stage(scal3k, 1, 2, tm3, dq3, rc3, tq12)
            ndqw = dQW.n

            eP = sem["eP"]

            def q_row_stage(rsrc, nrt, width, dq_d, rc_d, agdst, t_prev,
                            s_prev=None, tp_prev=None, agcb=None):
                nbl = width // 64
                ndc = width // 128
                hw = width // 2
                hb = nbl // 2
                wrow_ = lambda par: wb3[:, par * width:(par + 1) * width]
                rbf_ = xbf[:, 0:width]
                accf_ = arena[:].bitcast(F32)[:, 0:width]
                tmpf_ = xh[:, 0:width]
                wdq_ = xln[:, 0:width]
                slab_ = mskball[:, 0:width]
                dqr_ = lambda par: q2[:, par * nbl:(par + 1) * nbl]
                rcr_ = lambda par: vt2[:].bitcast(F32)[:, par * nbl:(par + 1) * nbl]
                tev_bank = [t_prev, t_prev]
                tstores = [s_prev] if s_prev else []
                tlw = [t_prev, t_prev]
                tlwp = [tp_prev, tp_prev]

                C0 = float(NF4[0]) + sum(GAP) * 0.5
                sgn_ok = False

                def halfops(eng, inc_s, c0, c1, b0, b1, par, tok):
                    emitq = (lambda fn, waits: b.emit(eng, fn, waits=waits,
                                                      inc=inc_s))
                    nb = b1 - b0
                    t1 = emitq(lambda e, par=par: e.tensor_tensor(
                        rbf_[:, c0:c1].rearrange("p (c z) -> p c z", z=64),
                        wrow_(par)[:, c0:c1].rearrange("p (c z) -> p c z", z=64),
                        rcr_(par)[:, b0:b1].unsqueeze(2)
                        .broadcast_to([128, nb, 64]), op=ALU.mult),
                        [(sR[par], tok)])
                    if sgn_ok:
                        tmpb = (xln[:, width:2 * width],
                                xln[:, 2 * width:3 * width])
                        t2 = emitq(lambda e: e.memset(accf_[:, c0:c1], C0),
                                   [(inc_s, t1)])
                        tk = t2
                        tstt = {}
                        tsg = {}
                        for j in range(15):
                            wsg = [(eV, t1)]
                            if j >= 2:
                                wsg.append((eV, tstt[j - 2]))
                            tsg[j] = aop(lambda e, j=j: e.activation(
                                tmpb[j % 2][:, c0:c1],
                                rbf_[:, c0:c1], AF.Sign,
                                bias=nmid[:, j:j + 1]),
                                waits=wsg)
                            tk = emitq(lambda e, j=j: e.scalar_tensor_tensor(
                                accf_[:, c0:c1],
                                tmpb[j % 2][:, c0:c1], GAP[j] * 0.5,
                                accf_[:, c0:c1], op0=ALU.mult, op1=ALU.add),
                                [(eA, tsg[j]), (inc_s, tk)])
                            tstt[j] = tk
                    else:
                        t2 = emitq(lambda e: e.memset(accf_[:, c0:c1],
                                                      float(NF4[0])),
                                   [(inc_s, t1)])
                        tk = t2
                        for j in range(15):
                            ja = emitq(lambda e, j=j: e.tensor_scalar(
                                tmpf_[:, c0:c1], rbf_[:, c0:c1], MID[j], GAP[j],
                                op0=ALU.is_gt, op1=ALU.mult), [(inc_s, tk)])
                            tk = emitq(lambda e, j=j: e.tensor_tensor(
                                accf_[:, c0:c1], accf_[:, c0:c1],
                                tmpf_[:, c0:c1], op=ALU.add), [(inc_s, ja)])
                    t3 = emitq(lambda e, par=par: e.tensor_tensor(
                        wdq_[:, c0:c1].rearrange("p (c z) -> p c z", z=64),
                        accf_[:, c0:c1].rearrange("p (c z) -> p c z", z=64),
                        dqr_(par)[:, b0:b1].unsqueeze(2)
                        .broadcast_to([128, nb, 64]), op=ALU.mult), [(inc_s, tk)])
                    return t3

                for t in range(nrt):
                    par = t % 2
                    w = [(dQW, ndqw)]
                    if tlw[par] is not None:
                        w.append((eV, tlw[par]))
                    if tlwp[par] is not None:
                        w.append((eP, tlwp[par]))
                    dma("sync", wrow_(par), rsrc[128 * t:128 * (t + 1), :],
                        sR[par], waits=w)
                    dma("sync", dqr_(par), dq_d[128 * t:128 * (t + 1), :], sR[par])
                    dma("sync", rcr_(par), rc_d[128 * t:128 * (t + 1), :], sR[par])
                    tok = sR[par].n
                    t3v = halfops("vector", eV, 0, width, 0, nbl, par, tok)
                    t3p = None
                    tlw[par] = t3v
                    tlwp[par] = None
                    tev = None
                    first_evac = True
                    for dc in range(ndc):
                        bank = dc % 2
                        slot = bank * 1024
                        w2 = [(eV, t3v)]
                        if tev_bank[bank] is not None:
                            w2.append((eV, tev_bank[bank]))
                        tt_ = b.emit("tensor",
                                     lambda e, dc=dc, slot=slot: e.transpose(
                                         PT[:, slot:slot + 128],
                                         wdq_[:, dc * 128:(dc + 1) * 128], idn[:]),
                                     waits=w2, inc=eT)
                        w3_ = [(eT, tt_)]
                        if first_evac and tstores:
                            w3_.append((dO, tstores[-1]))
                            first_evac = False
                        tev = vop(lambda e, dc=dc, slot=slot: e.tensor_copy(
                            slab_[:, dc * 128:(dc + 1) * 128],
                            PT[:, slot:slot + 128]), waits=w3_)
                        tev_bank[bank] = tev
                    ts_ = dma("gpsimd", agdst[128 * t:128 * (t + 1), :],
                              slab_[:, 0:width], dO, waits=[(eV, tev)])
                    tstores.append(ts_)
                    if agcb is not None:
                        agcb(t, ts_)
                return tstores, tlw, tlwp

            def ag1cb(t, ts_):
                b.emit("gpsimd",
                       lambda e, t=t: e.collective_compute(
                           "AllGather", ALU.bypass,
                           replica_groups=[list(range(8))],
                           ins=[ag1s[128 * t:128 * (t + 1), :].opt()],
                           outs=[wag12[1024 * t:1024 * (t + 1), :].opt()]),
                       waits=[(dO, ts_)], inc=ccS, k=1)

            def ag2cb(t, ts_):
                b.emit("gpsimd",
                       lambda e, t=t: e.collective_compute(
                           "AllGather", ALU.bypass,
                           replica_groups=[list(range(8))],
                           ins=[ag2s[128 * t:128 * (t + 1), :].opt()],
                           outs=[wag3[1024 * t:1024 * (t + 1), :].opt()]),
                       waits=[(dO, ts_)], inc=ccS, k=1)

            st1, tl1, tp1 = q_row_stage(rw12, W12_FC, DM, dq12, rc12, ag1s, tq3,
                                        agcb=ag1cb)
            q_row_stage(rw3, 2, DFF, dq3, rc3, ag2s, eV.n, s_prev=st1[-1],
                        tp_prev=max(t for t in tp1 if t is not None)
                        if any(tp1) else None, agcb=ag2cb)
            qbar = [(ss_, ss_.n) for ss_ in sem.values()]
            for _e in ENGINES:
                b.emit(_e, lambda e: e.nop(), waits=qbar)



        def halfprog(h):
            if h > 0:
                bar = [(ss, ss.n) for ss in sem.values()]
                for eng in ("sync", "tensor", "vector", "scalar", "gpsimd"):
                    b.emit(eng, lambda e: e.nop(), waits=bar)
            for dc in range(NDC):
                dma("sync", xh_(dc), x_in[ts_(dc), h * HALF:(h + 1) * HALF], dX)
            xtok = [(dX, dX.n)] * NDC

            def layernorm(xtok, extra_xbf=()):
                tb = []
                for dc in range(NDC):
                    w = [xtok[dc]] + list(extra_xbf)
                    tb.append(vop(lambda e, dc=dc: e.tensor_copy(xbf_(dc), xh_(dc)),
                                  waits=w))
                if "stats" in LN_SKIP:
                    tstat = b.emit("tensor", lambda e: e.nop(),
                                   waits=[(eV, tb[-1])], inc=eT)
                else:
                    tsq_consumed = None
                    for dc in range(NDC):
                        w = [(eV, tb[dc])]
                        if tsq_consumed is not None:
                            w.append((eT, tsq_consumed))
                        tq = vop(lambda e, dc=dc: e.tensor_tensor(
                            lntmp[:], xbf_(dc), xbf_(dc), op=ALU.mult), waits=w)
                        mm(PS1[0:1, 0:512], onc[:], xbf_(dc), dc == 0, dc == NDC - 1,
                           waits=[(eV, tb[dc])])
                        tsq_consumed = mm(PS2[0:1, 0:512], onc[:], lntmp[:],
                                          dc == 0, dc == NDC - 1,
                                          waits=[(eV, tq)], inc=eT)
                    tstat = tsq_consumed
                S = lambda a: sc1[0:1, a * 512:(a + 1) * 512]
                mean, ex2, m2, ve = S(0), S(1), S(2), S(3)
                r, t_, mrs = S(2), S(1), S(1)
                if "scal" in LN_SKIP:
                    tz0 = vop(lambda e: e.memset(bcB[:], 1.0), waits=[(eT, tstat)])
                    tBc = tCc = vop(lambda e: e.memset(bcC[:], 0.0),
                                    waits=[(eV, tz0)])
                    txl = []
                    if "apply" in LN_SKIP:
                        return tb
                    prev = tCc
                    for dc in range(NDC):
                        ta = vop(lambda e, dc=dc: e.tensor_tensor(
                            lntmp[:], xbf_(dc), bcB[:], op=ALU.mult),
                            waits=[(eV, tBc), (eV, prev)])
                        prev = vop(lambda e, dc=dc: e.tensor_tensor(
                            xln_(dc), lntmp[:], bcC[:], op=ALU.subtract),
                            waits=[(eV, ta)])
                        txl.append(prev)
                    return txl
                t1 = vop(lambda e: e.tensor_copy(mean, PS1[0:1, 0:512]),
                         waits=[(eT, tstat)])
                t2 = vop(lambda e: e.tensor_copy(ex2, PS2[0:1, 0:512]),
                         waits=[(eT, tstat)])
                t3 = vop(lambda e: e.tensor_scalar(mean, mean, 1.0 / DM, None,
                                                   op0=ALU.mult), waits=[(eV, t1)])
                t4 = vop(lambda e: e.tensor_scalar(ex2, ex2, 1.0 / DM, None,
                                                   op0=ALU.mult), waits=[(eV, t2)])
                t5 = vop(lambda e: e.tensor_tensor(m2, mean, mean, op=ALU.mult),
                         waits=[(eV, t3)])
                t6 = vop(lambda e: e.scalar_tensor_tensor(ve, ex2, EPS_LN, m2,
                                                          op0=ALU.add,
                                                          op1=ALU.subtract),
                         waits=[(eV, t4), (eV, t5)])
                if "act" in LN_SKIP:
                    t8v = vop(lambda e: e.memset(r, 1.0), waits=[(eV, t6)])
                    last = (eV, t8v)
                else:
                    t7 = aop(lambda e: e.activation(t_, ve, AF.Ln), waits=[(eV, t6)])
                    t8 = aop(lambda e: e.activation(r, t_, AF.Exp, scale=-0.5),
                             waits=[(eA, t7)])
                    last = (eA, t8)
                for _ in range(0 if "newton" in LN_SKIP else 2):
                    ta = vop(lambda e: e.tensor_tensor(t_, r, r, op=ALU.mult),
                             waits=[last])
                    tb2 = vop(lambda e: e.tensor_tensor(t_, t_, ve, op=ALU.mult),
                              waits=[(eV, ta)])
                    tc = vop(lambda e: e.tensor_scalar(t_, t_, -0.5, 1.5,
                                                       op0=ALU.mult, op1=ALU.add),
                             waits=[(eV, tb2)])
                    td = vop(lambda e: e.tensor_tensor(r, r, t_, op=ALU.mult),
                             waits=[(eV, tc)])
                    last = (eV, td)
                tm = vop(lambda e: e.tensor_tensor(mrs, mean, r, op=ALU.mult),
                         waits=[last])
                tB = mm(PS3[:, 0:512], onr[:], r, True, True,
                        waits=[(eV, tm)], inc=eT)
                tC = mm(PS3[:, 512:1024], onr[:], mrs, True, True,
                        waits=[(eV, tm)], inc=eT)
                tBc = vop(lambda e: e.tensor_copy(bcB[:], PS3[:, 0:512]),
                          waits=[(eT, tB)])
                tCc = vop(lambda e: e.tensor_copy(bcC[:], PS3[:, 512:1024]),
                          waits=[(eT, tC)])
                txl = []
                prev = tCc
                for dc in range(NDC):
                    ta = vop(lambda e, dc=dc: e.tensor_tensor(lntmp[:], xbf_(dc),
                                                              bcB[:], op=ALU.mult),
                             waits=[(eV, tBc), (eV, prev)])
                    prev = vop(lambda e, dc=dc: e.tensor_tensor(
                        xln_(dc), lntmp[:], bcC[:], op=ALU.subtract),
                        waits=[(eV, ta)])
                    txl.append(prev)
                return txl

            def ffn(txl, extra_arena=(), extra_wb3=(), extra_wAB=()):
                tslabA = [None, None]
                tsilu = [None] * NFC
                tmmA = [None] * NFC
                for fc in range(NFC):
                    r_, lfc = fc // W12_FC, fc % W12_FC
                    par = fc % 2
                    w = ([(eT, tmmA[fc - 2])] if fc >= 2 else
                         list(extra_wAB) + [(sCC[0], sCC[0].n)])
                    tslabA[par] = dma("sync", wbA[:, par * DM:(par + 1) * DM],
                                      wag12[(lfc * 8 + r_) * 128:
                                            (lfc * 8 + r_) * 128 + 128, :],
                                      sWA[par], waits=w)
                    bank = PS1[:, par * 512:(par + 1) * 512]
                    for dc in range(NDC):
                        wts = [(sWA[par], tslabA[par]), (eV, txl[dc])]
                        if dc == 0 and fc >= 2 and tsilu[fc - 2] is not None:
                            wts.append((eA, tsilu[fc - 2]))
                        tk = mm(bank, slab_tile(wbA, par * DM, dc, 128),
                                xln_(dc), dc == 0, dc == NDC - 1, waits=wts,
                                inc=eT if dc == NDC - 1 else None)
                    tmmA[fc] = tk
                    wts = [(eT, tmmA[fc])] + list(extra_arena)
                    tsilu[fc] = aop(lambda e, fc=fc, bank=bank: e.activation(
                        ar_(fc), bank, AF.Silu), waits=wts)
                tmmB = [None] * NFC
                th = [None] * NFC
                for fc in range(NFC):
                    r_, lfc = fc // W12_FC, fc % W12_FC
                    par = fc % 2
                    w = ([(eT, tmmB[fc - 2])] if fc >= 2 else
                         list(extra_wAB) + [(sCC[1], sCC[1].n)])
                    tsb = dma("sync", wbB[:, par * DM:(par + 1) * DM],
                              wag12[(lfc * 8 + r_ + 4) * 128:
                                    (lfc * 8 + r_ + 4) * 128 + 128, :],
                              sWB[par], waits=w)
                    bank = PS2[:, par * 512:(par + 1) * 512]
                    for dc in range(NDC):
                        wts = [(sWB[par], tsb), (eV, txl[dc])]
                        if dc == 0 and fc >= 2 and th[fc - 2] is not None:
                            wts.append((eV, th[fc - 2]))
                        tk = mm(bank, slab_tile(wbB, par * DM, dc, 128),
                                xln_(dc), dc == 0, dc == NDC - 1, waits=wts,
                                inc=eT if dc == NDC - 1 else None)
                    tmmB[fc] = tk
                    th[fc] = vop(lambda e, fc=fc, bank=bank: e.tensor_tensor(
                        ar_(fc), ar_(fc), bank, op=ALU.mult),
                        waits=[(eT, tmmB[fc]), (eA, tsilu[fc])])
                tmmC = [None] * NDC
                txn = [None] * NDC
                for dc in range(NDC):
                    r_, dcl = dc // 2, dc % 2
                    par = dc % 2
                    w = ([(eT, tmmC[dc - 2])] if dc >= 2 else []) + list(extra_wb3)
                    ts3 = dma("sync", wb3[:, par * DFF:(par + 1) * DFF],
                              wag3[(dcl * 8 + r_) * 128:
                                   (dcl * 8 + r_) * 128 + 128, :],
                              sW3[par], waits=w)
                    bank = PS3[:, par * 512:(par + 1) * 512]
                    for fc in range(NFC):
                        wts = [(sW3[par], ts3), (eV, th[fc])]
                        if fc == 0 and dc >= 2 and txn[dc - 2] is not None:
                            wts.append((eV, txn[dc - 2]))
                        tk = mm(bank,
                                wb3[:, par * DFF:(par + 1) * DFF].rearrange(
                                    "p (c f) -> p c f", f=128)[:, fc, :],
                                ar_(fc), fc == 0, fc == NFC - 1, waits=wts,
                                inc=eT if fc == NFC - 1 else None)
                    tmmC[dc] = tk
                    txn[dc] = vop(lambda e, dc=dc, bank=bank: e.scalar_tensor_tensor(
                        xh_(dc), bank, 0.5, xh_(dc), op0=ALU.mult, op1=ALU.add),
                        waits=[(eT, tmmC[dc])])
                return txn, tmmC[NDC - 1]

            def attention(txl, extra_arena, t_wAB_free, sub="all"):
                if sub == "ln2":
                    tdump = []
                    for dc in range(NDC):
                        tdump.append(vop(lambda e, dc=dc: e.tensor_copy(
                            xh_(dc), xln_(dc)),
                            waits=[(eV, eV.n), (eA, eA.n), (eT, eT.n)]))
                    return tdump, eT.n, eT.n
                qn_ = lambda oc: arena[:, 0 * 8192 + oc * HALF: 0 * 8192 + (oc + 1) * HALF]
                gate_ = lambda oc: arena[:, 8192 + oc * HALF: 8192 + (oc + 1) * HALF]
                kn_ = lambda oc: xbf_(oc)
                gv_ = lambda sub: wb3[:, sub * DM:(sub + 1) * DM]

                tbeta = []
                tprev = None
                for bsub in range(min(NSUB, BETA_SUBS)):
                    if not NOMM:
                        for dc in range(NDC):
                            w = [(eV, txl[dc])]
                            if dc == 0 and tprev is not None:
                                w.append((eV, tprev))
                            tz = mm(PS3[:, 0:NH],
                                    xln[:, dc * HALF + bsub * 128:dc * HALF + (bsub + 1) * 128],
                                    wbres[:].rearrange("p (c n) -> p c n", n=NH)[:, dc, :],
                                    dc == 0, dc == NDC - 1,
                                    waits=w, inc=eT if dc == NDC - 1 else None)
                    bsl = beta[:, bsub * NH:(bsub + 1) * NH]
                    if NOMM:
                        t0 = vop(lambda e, bsl=bsl: e.memset(bsl, 0.25),
                                 waits=([(eV, tprev)] if tprev else []))
                    else:
                        t0 = vop(lambda e, bsl=bsl: e.tensor_tensor(
                            bsl, PS3[:, 0:NH], bbs[:], op=ALU.add),
                            waits=[(eT, tz), (eV, t_bbs)])
                    if NOACT:
                        tprev = t0
                        tbeta.append(tprev)
                        continue
                    ta1 = aop(lambda e, bsl=bsl: e.activation(bscr[:], bsl, AF.Abs),
                              waits=[(eV, t0)])
                    ta2 = aop(lambda e: e.activation(bscr2[:], bscr[:], AF.Exp,
                                                     scale=-1.0), waits=[(eA, ta1)])
                    ta3 = aop(lambda e: e.activation(bscr[:], bscr2[:], AF.Ln,
                                                     bias=1.0), waits=[(eA, ta2)])
                    tprev = vop(lambda e, bsl=bsl: e.scalar_tensor_tensor(
                        bsl, bsl, 0.0, bscr[:], op0=ALU.max, op1=ALU.add),
                        waits=[(eA, ta3)])
                    tbeta.append(tprev)
                t_beta_all = tprev

                if sub == "beta":
                    tdump = []
                    for dc in range(NDC):
                        tdump.append(vop(lambda e, dc=dc: e.tensor_copy(
                            xh_(dc), xln_(dc)),
                            waits=[(eV, eV.n), (eA, eA.n), (eT, eT.n)]))
                    return tdump, eT.n, eT.n

                passes = [("q", wq, qbs, qn_), ("k", wk, kbs, kn_),
                          ("og", wog, None, gate_), ("v", wv, vbs, None)]
                if sub == "qk":
                    passes = passes[:2]
                SLAB4 = (lambda: wbA[:, 0:DM], lambda: wbA[:, DM:2 * DM],
                         lambda: wbB[:, 0:DM], lambda: wbB[:, DM:2 * DM])
                SSEM4 = (sWA[0], sWA[1], sWB[0], sWB[1])
                BANK4 = (lambda: PS2[:, 0:512], lambda: PS2[:, 512:1024],
                         lambda: PS1[:, 0:512], lambda: PS1[:, 512:1024])
                tslab = [None] * 4
                tconv = [None] * 4
                tgrp = [None] * 4
                tevac = [None] * 4
                normq = {}
                last_tss = [None]
                last_tn = [None]
                tgv = [None] * (NSUB * NH)
                tnorm_done = {}
                for pi, (pname, wsrc, bias, dst) in enumerate(passes):
                    for oc in range(NDC):
                        par = oc % 4
                        wpar = oc % 2
                        gi = pi * NDC + oc
                        w = [(eT, tgrp[par]) if tgrp[par] is not None
                             else (eT, t_wAB_free),
                             (sem["sWF0"], 80)]
                        tl = dma("scalar" if par < 2 else "sync", SLAB4[par](),
                                 wcache[DM * pi + 128 * oc:
                                        DM * pi + 128 * (oc + 1), :],
                                 SSEM4[par], waits=w)
                        slab = SLAB4[par]()
                        tslab_tok = (SSEM4[par], tl)
                        bank = BANK4[par]()
                        for kc in range(NDC):
                            wts = [tslab_tok, (eV, txl[kc])]
                            if kc == 0 and tevac[par] is not None:
                                wts.append(tevac[par])
                            tg = mm(bank, slab.rearrange("p (c f) -> p c f",
                                                         f=128)[:, kc, :],
                                    xln_(kc), kc == 0, kc == NDC - 1, waits=wts,
                                    inc=eT if kc == NDC - 1 else None)
                        tgrp[par] = tg
                        if pname in ("q", "k"):
                            dtile = dst(oc)
                            wts = [(eT, tg)]
                            if pname == "q":
                                wts += list(extra_arena)
                            te = aop(lambda e, dtile=dtile, bank=bank, bias=bias,
                                     oc=oc: e.activation(
                                dtile, bank, AF.Identity, bias=bias[:, oc:oc + 1]),
                                waits=wts)

                            def norm_stage1(st):
                                tss = mm(PS3[0:1, 512:1024], onc[:], q2[:],
                                         True, True,
                                         waits=[(eV, st["tq2"])], inc=eT)
                                st["tss"] = tss
                                S = lambda a: sc1[0:1, a * 512:(a + 1) * 512]
                                ss, tt = S(0), S(1)
                                rr = S(2 + st["oc"] % 2)
                                t1 = vop(lambda e, ss=ss: e.tensor_copy(
                                    ss, PS3[0:1, 512:1024]), waits=[(eT, tss)])
                                t7 = aop(lambda e, ss=ss, tt=tt: e.activation(
                                    tt, ss, AF.Ln), waits=[(eV, t1)])
                                t8 = aop(lambda e, rr=rr, tt=tt: e.activation(
                                    rr, tt, AF.Exp, scale=-0.5), waits=[(eA, t7)])
                                last = (eA, t8)
                                for _ in range(2):
                                    ta = vop(lambda e, rr=rr, tt=tt:
                                             e.tensor_tensor(tt, rr, rr,
                                                             op=ALU.mult),
                                             waits=[last])
                                    tb2 = vop(lambda e, ss=ss, tt=tt:
                                              e.tensor_tensor(tt, tt, ss,
                                                              op=ALU.mult),
                                              waits=[(eV, ta)])
                                    tc2 = vop(lambda e, tt=tt: e.tensor_scalar(
                                        tt, tt, -0.5, 1.5, op0=ALU.mult,
                                        op1=ALU.add), waits=[(eV, tb2)])
                                    td = vop(lambda e, rr=rr, tt=tt:
                                             e.tensor_tensor(rr, rr, tt,
                                                             op=ALU.mult),
                                             waits=[(eV, tc2)])
                                    last = (eV, td)
                                st["rr"] = rr
                                st["tnewt"] = last

                            def norm_stage2(st):
                                w_ = [st["tnewt"]]
                                if last_tn[0] is not None:
                                    w_.append((eV, last_tn[0]))
                                tfb = mm(PS3[:, 0:512], onr[:], st["rr"],
                                         True, True, waits=w_, inc=eT)
                                tn = vop(lambda e, dtile=st["dtile"]:
                                         e.tensor_tensor(
                                             dtile, dtile, PS3[:, 0:512],
                                             op=ALU.mult), waits=[(eT, tfb)])
                                tnorm_done[(st["pname"], st["oc"])] = tn
                                last_tn[0] = tn
                                return tn

                            if oc >= 1:
                                norm_stage1(normq[oc - 1])
                                last_tss[0] = normq[oc - 1]["tss"]
                            tq2 = vop(lambda e, dtile=dtile: e.tensor_tensor(
                                q2[:], dtile, dtile, op=ALU.mult),
                                waits=[(eA, te)] +
                                      ([(eT, last_tss[0])]
                                       if last_tss[0] is not None else []))
                            normq[oc] = {"dtile": dtile, "tq2": tq2, "oc": oc,
                                         "pname": pname}
                            if oc >= 2:
                                tevac[(oc - 2) % 4] = (eV,
                                                       norm_stage2(normq[oc - 2]))
                            if oc == NDC - 1:
                                norm_stage1(normq[oc])
                                last_tss[0] = normq[oc]["tss"]
                                tevac[(oc - 1) % 4] = (eV,
                                                       norm_stage2(normq[oc - 1]))
                                tevac[oc % 4] = (eV, norm_stage2(normq[oc]))
                                normq.clear()
                        elif pname == "og":
                            te = aop(lambda e, oc=oc, bank=bank: e.activation(
                                gate_(oc), bank, AF.Silu),
                                waits=[(eT, tg)] + list(extra_arena))
                            tevac[par] = (eA, te)
                        else:
                            vp = oc % 2
                            te = aop(lambda e, bank=bank, oc=oc, vp=vp: e.activation(
                                vt2[:, vp * HALF:(vp + 1) * HALF], bank, AF.Identity,
                                bias=vbs[:, oc:oc + 1]), waits=[(eT, tg)])
                            for vsub in range(NSUB):
                                g = oc * NSUB + vsub
                                slot = (g % 2) * 1024
                                w2 = [(eA, te)]
                                if g >= 2:
                                    w2.append((eV, tgv[g - 2]))
                                if NO_TRANSPOSE:
                                    tt_ = b.emit("tensor", lambda e: e.nop(),
                                                 waits=w2, inc=eT)
                                else:
                                    if TR_MODE == "fixed_in":
                                        tt_ = b.emit("tensor",
                                                     lambda e, slot=slot:
                                                     e.transpose(PT[:, slot:slot + 128],
                                                                 idn[:], idn[:]),
                                                     waits=w2, inc=eT)
                                    elif TR_MODE == "slot0":
                                        tt_ = b.emit("tensor",
                                                     lambda e, vsub=vsub, vp=vp:
                                                     e.transpose(
                                                         PT[:, 0:128],
                                                         vt2[:, vp * HALF + vsub * 128:
                                                             vp * HALF + (vsub + 1) * 128],
                                                         idn[:]),
                                                     waits=w2, inc=eT)
                                    else:
                                        tt_ = b.emit("tensor",
                                                     lambda e, vsub=vsub, vp=vp, slot=slot:
                                                     e.transpose(
                                                         PT[:, slot:slot + 128],
                                                         vt2[:, vp * HALF + vsub * 128:
                                                             vp * HALF + (vsub + 1) * 128],
                                                         idn[:]),
                                                     waits=w2, inc=eT)
                                if GV_PLAIN_COPY:
                                    tgv[oc * NSUB + vsub] = vop(
                                        lambda e, vsub=vsub, oc=oc, slot=slot:
                                        e.tensor_copy(
                                            gv_(vsub)[:, oc * 128:(oc + 1) * 128],
                                            PT[:, slot:slot + 128]),
                                        waits=[(eT, tt_), (eV, t_beta_all)])
                                else:
                                    tgv[oc * NSUB + vsub] = vop(
                                        lambda e, vsub=vsub, oc=oc, slot=slot:
                                        e.tensor_scalar(
                                            gv_(vsub)[:, oc * 128:(oc + 1) * 128],
                                            PT[:, slot:slot + 128],
                                            beta[:, vsub * NH + oc: vsub * NH + oc + 1],
                                            None, op0=ALU.mult),
                                        waits=[(eT, tt_), (eV, t_beta_all)])
                            tevac[par] = (eA, te)
                t_gv_all = eV.n
                t_kq_all = eV.n

                if sub == "qk" and QK_TRANSPOSES:
                    tlast = None
                    for i_ in range(8):
                        tlast = b.emit("tensor",
                                       lambda e, i_=i_: e.transpose(
                                           PT[:, i_ * 128:(i_ + 1) * 128],
                                           idn[:], idn[:]),
                                       waits=[(eT, eT.n), (eV, eV.n), (eA, eA.n)],
                                       inc=eT)
                if sub in ("proj", "qk"):
                    tdump = []
                    for dc in range(NDC):
                        tdump.append(vop(lambda e, dc=dc: e.tensor_copy(
                            xh_(dc), qn_(dc)),
                            waits=[(eV, eV.n), (eA, eA.n), (eT, eT.n)]))
                    return tdump, eT.n, eT.n

                t_projV, t_projA = eV.n, eA.n
                tmsk = [[None, None] for _ in range(NSUB)]
                for sp in range(NSUB):
                    for ic in range(2):
                        ps = PS1 if ic == 0 else PS2
                        for hh in range(NH):
                            w = [(eV, tnorm_done[("q", hh)]),
                                 (eV, tnorm_done[("k", hh)])]
                            if hh == 0 and sp == 0:
                                w += [(eV, t_projV), (eA, t_projA)]
                            if hh == 0 and sp >= 1:
                                w.append((eV, tmsk[sp - 1][ic]))
                            tsc = mm(ps[:, hh * CH:(hh + 1) * CH],
                                     kn_(hh)[:, sp * 128:(sp + 1) * 128],
                                     qn_(hh)[:, sp * 128 + ic * CH:
                                             sp * 128 + (ic + 1) * CH],
                                     True, True, waits=w,
                                     inc=eT if hh == NH - 1 else None)
                        mb = msk[:, ic * CH:(ic + 1) * CH].unsqueeze(1) \
                            .broadcast_to([128, NH, CH])
                        tmsk[sp][ic] = vop(
                            lambda e, sp=sp, ic=ic, ps=ps, mb=mb: e.tensor_tensor(
                                mskball[:, sp * NH * 128:(sp + 1) * NH * 128]
                                .rearrange("p (n i) -> p n i", i=128)
                                [:, :, ic * CH:(ic + 1) * CH],
                                ps[:, 0:NH * CH].rearrange("p (n i) -> p n i", i=CH),
                                mb, op=ALU.mult), waits=[(eT, tsc)])
                t_sc_last = eT.n

                tgat = [None] * NH
                for hh in range(NH):
                    par = hh % 2
                    bank = PS3[:, par * 512:(par + 1) * 512]
                    for sp in range(NSUB):
                        w = [(eV, tmsk[sp][0]), (eV, tmsk[sp][1]), (eV, t_gv_all)]
                        if sp == 0 and hh < 2:
                            w.append((eV, t_projV))
                        if sp == 0 and hh >= 2 and tgat[hh - 2] is not None:
                            w.append((eV, tgat[hh - 2]))
                        ta = mm(bank[:, sp * 128:(sp + 1) * 128],
                                gv_(sp)[:, hh * 128:(hh + 1) * 128],
                                mskball[:, sp * NH * 128 + hh * 128:
                                        sp * NH * 128 + (hh + 1) * 128],
                                True, True, waits=w,
                                inc=eT if sp == NSUB - 1 else None)
                    tgat[hh] = vop(lambda e, hh=hh, bank=bank: e.tensor_tensor(
                        gate_(hh), bank, gate_(hh), op=ALU.mult),
                        waits=[(eT, ta)])

                if sub == "score":
                    tdump = []
                    for dc in range(NDC):
                        tdump.append(vop(lambda e, dc=dc: e.tensor_copy(
                            xh_(dc), gate_(dc)),
                            waits=[(eV, eV.n), (eA, eA.n), (eT, eT.n)]))
                    return tdump, eT.n, eT.n

                tslab2 = [None, None]
                tgrp2 = [None] * 4
                txn = [None] * NDC
                for oc in range(NDC):
                    par = oc % 4
                    wpar = oc % 2
                    w = [(eT, tgrp2[par]) if tgrp2[par] is not None
                         else (eT, tgrp[par])]
                    tl = dma("scalar" if par < 2 else "sync", SLAB4[par](),
                             wcache[DM * 4 + 128 * oc:DM * 4 + 128 * (oc + 1), :],
                             SSEM4[par], waits=w)
                    slab = SLAB4[par]()
                    ttok = (SSEM4[par], tl)
                    bank = BANK4[par]()
                    for kc in range(NDC):
                        w = [ttok, (eV, tgat[kc])]
                        if kc == 0 and oc >= 4 and txn[oc - 4] is not None:
                            w.append((eV, txn[oc - 4]))
                        tg = mm(bank, slab.rearrange("p (c f) -> p c f",
                                                     f=128)[:, kc, :],
                                gate_(kc), kc == 0, kc == NDC - 1, waits=w,
                                inc=eT if kc == NDC - 1 else None)
                    tgrp2[par] = tg
                    txn[oc] = vop(lambda e, oc=oc, bank=bank: e.scalar_tensor_tensor(
                        xh_(oc), bank, obs[:, oc:oc + 1], xh_(oc),
                        op0=ALU.add, op1=ALU.add),
                        waits=[(eT, tg)])
                t_arena_free = eT.n
                return txn, t_sc_last, t_arena_free

            txl = layernorm(xtok)
            if stage == "ln1":
                tst = []
                for dc in range(NDC):
                    t = vop(lambda e, dc=dc: e.tensor_copy(xh_(dc), xln_(dc)),
                            waits=[(eV, txl[dc])])
                    tst.append(t)
                txn = tst
            elif stage == "betaonly":
                txn, t_sc_last, t_ar_free = attention(
                    txl, extra_arena=[], t_wAB_free=0, sub="beta")
            else:
                txn, t_ffn_last = ffn(txl)
                if stage not in ("ffn1",):
                    txl2 = layernorm([(eV, t) for t in txn],
                                     extra_xbf=[(eT, t_ffn_last)])
                    asub = {"attnp": "proj", "attns": "score", "attnb": "beta",
                            "attnq": "qk", "attnl": "ln2"}.get(stage, "all")
                    txn, t_sc_last, t_ar_free = attention(
                        txl2, extra_arena=[(eT, t_ffn_last)],
                        t_wAB_free=t_ffn_last, sub=asub)
                    if stage == "full" and asub == "all":
                        txl3 = layernorm([(eV, t) for t in txn],
                                         extra_xbf=[(eT, t_sc_last)])
                        txn, _ = ffn(txl3,
                                     extra_arena=[(eT, t_ar_free)],
                                     extra_wb3=[(eT, t_ar_free)],
                                     extra_wAB=[(eT, t_ar_free),
                                                (sem["sCA"], sem["sCA"].n),
                                                (sem["sCB"], sem["sCB"].n)])
            tst = []
            for dc in range(NDC):
                t = dma("gpsimd", y_out[ts_(dc), h * HALF:(h + 1) * HALF], xh_(dc),
                        dP, waits=[(eV, txn[dc])])
                tst.append(t)
            return tst

        halfprog(0)
        halfprog(1)
        b.emit("gpsimd", lambda e: e.nop(), waits=[(dP, dP.n)])

        b.replay(E(nc.Block()))

    return nc


def _blocked_lhsT(W):
    o, i = W.shape
    return np.ascontiguousarray(
        W.reshape(o // 128, 128, i // 128, 128).transpose(0, 3, 2, 1)
        .reshape(o, i))


def _bias_cols(v):
    return np.ascontiguousarray(v.reshape(NDC, 128).T)


def _mask2():
    m = np.zeros((128, 128), np.float32)
    tri = np.triu(np.ones((CH, CH), np.float32))
    m[0:CH, 0:CH] = tri
    m[CH:128, CH:128] = tri
    return m


def host_inputs_forward(core, x_full, w12ag, w3ag, attn):
    xs = x_full.reshape(8192, DM)[core * TOK:(core + 1) * TOK, :]
    m = {
        "x_in": np.ascontiguousarray(xs.T),
        "wq": _blocked_lhsT(attn["q_w"]), "wk": _blocked_lhsT(attn["k_w"]),
        "wv": _blocked_lhsT(attn["v_w"]), "wog": _blocked_lhsT(attn["og_w"]),
        "wo": _blocked_lhsT(attn["o_w"]),
        "wbt": np.ascontiguousarray(attn["b_w"].T),
        "qb": _bias_cols(attn["q_b"]), "kb": _bias_cols(attn["k_b"]),
        "vb": _bias_cols(attn["v_b"]), "ob": _bias_cols(attn["o_b"]),
        "bb": attn["b_b"].reshape(1, NH),
        "cmask": _mask2(),
        "cones_col": np.ones((128, 1), bfdt),
        "cones_row": np.ones((1, 128), np.float32),
        "cident": np.eye(128, dtype=bfdt),
    }
    if w12ag is not None:
        m["wag12"] = w12ag
        m["wag3"] = w3ag
    return m


def ag_layout_w12(w1deq, w2deq):
    out = np.empty((8 * W12_SH, DM), bfdt)
    for r in range(4):
        for src, base in ((w1deq, 0), (w2deq, 4)):
            sh = src[W12_SH * r:W12_SH * (r + 1), :]
            t = sh.T
            for fc in range(W12_FC):
                blk = np.empty((128, DM), bfdt)
                for dc in range(NDC):
                    blk[:, dc * 128:(dc + 1) * 128] = \
                        t[dc * 128:(dc + 1) * 128, fc * 128:(fc + 1) * 128]
                out[(fc * 8 + base + r) * 128:(fc * 8 + base + r) * 128 + 128,
                    :] = blk
    return out


def _ag_layout_w12_old(w1deq, w2deq):
    out = np.empty((8 * W12_SH, DM), bfdt)
    for r in range(4):
        for src, base in ((w1deq, 0), (w2deq, 4)):
            sh = src[W12_SH * r:W12_SH * (r + 1), :]
            t = sh.T
            blk = t.reshape(NDC, 128, W12_FC, 128)
            d = np.empty((W12_SH, DM), bfdt)
            for fc in range(W12_FC):
                for dc in range(NDC):
                    d[fc * 128:(fc + 1) * 128, dc * 128:(dc + 1) * 128] = \
                        t[dc * 128:(dc + 1) * 128, fc * 128:(fc + 1) * 128]
            out[(base + r) * W12_SH:(base + r + 1) * W12_SH, :] = d
    return out


def ag_layout_w3(w3deq):
    out = np.empty((8 * W3_SH, DFF), bfdt)
    t = w3deq.T
    for r in range(8):
        for dcl in range(2):
            dglob = r * 256 + dcl * 128
            blk = np.empty((128, DFF), bfdt)
            for fc in range(NFC):
                blk[:, fc * 128:(fc + 1) * 128] = \
                    t[fc * 128:(fc + 1) * 128, dglob:dglob + 128]
            out[(dcl * 8 + r) * 128:(dcl * 8 + r) * 128 + 128, :] = blk
    return out


def _ag_layout_w3_old(w3deq):
    out = np.empty((8 * W3_SH, DFF), bfdt)
    t = w3deq.T
    for r in range(8):
        blk = np.empty((W3_SH, DFF), bfdt)
        for dcl in range(2):
            dglob = r * 256 + dcl * 128
            for fc in range(NFC):
                blk[dcl * 128:(dcl + 1) * 128, fc * 128:(fc + 1) * 128] = \
                    t[fc * 128:(fc + 1) * 128, dglob:dglob + 128]
        out[r * W3_SH:(r + 1) * W3_SH, :] = blk
    return out




GAP = [NF4[j + 1] - NF4[j] for j in range(15)]


def build_quant(fused=False):
    nc = bass.Bass("TRN2", target_bir_lowering=False)
    wfull = nc.dram_tensor("wfull", [DFF, DM], BF16, kind="ExternalInput").ap()
    w3full = nc.dram_tensor("w3full", [DM, DFF], BF16, kind="ExternalInput").ap()
    rw12 = nc.dram_tensor("rw12", [W12_SH, DM], BF16, kind="ExternalInput").ap()
    rw3 = nc.dram_tensor("rw3", [W3_SH, DFF], BF16, kind="ExternalInput").ap()
    qident = nc.dram_tensor("qident", [128, 128], BF16, kind="ExternalInput").ap()
    crow88 = nc.dram_tensor("crow88", [1, 88], F32, kind="ExternalInput").ap()
    cone88 = nc.dram_tensor("cone88", [88, 1], F32, kind="ExternalInput").ap()
    ag1s = nc.dram_tensor("ag1s", [W12_SH, DM], BF16, kind="ExternalOutput").ap()
    ag2s = nc.dram_tensor("ag2s", [W3_SH, DFF], BF16, kind="ExternalOutput").ap()
    dq12 = nc.dram_tensor("dq12", [W12_SH, 32], BF16).ap()
    rc12 = nc.dram_tensor("rc12", [W12_SH, 32], F32).ap()
    dq3 = nc.dram_tensor("dq3", [W3_SH, 88], BF16).ap()
    rc3 = nc.dram_tensor("rc3", [W3_SH, 88], F32).ap()

    with ExitStack() as ctx:
        E = ctx.enter_context
        qt = E(nc.sbuf_tensor("qt", [88, 2 * 16384], BF16))
        scalf = E(nc.sbuf_tensor("scalf", [88, 2048], F32))
        scalf3 = E(nc.sbuf_tensor("scalf3", [88, 2048], F32))
        sq = E(nc.sbuf_tensor("sq", [88, 8 * 256], F32))
        dqs = E(nc.sbuf_tensor("dqs", [88, 256], BF16))
        cents = E(nc.sbuf_tensor("cents", [88, 256], BF16))
        smalls = E(nc.sbuf_tensor("smalls", [88, 16], F32))
        s11 = E(nc.sbuf_tensor("s11", [1, 8], F32))
        s11b = E(nc.sbuf_tensor("s11b", [1, 2], BF16))
        ident = E(nc.sbuf_tensor("ident", [128, 128], BF16))
        r88 = E(nc.sbuf_tensor("r88", [1, 88], F32))
        c88 = E(nc.sbuf_tensor("c88", [88, 1], F32))
        wrow = E(nc.sbuf_tensor("wrow", [128, 2 * DFF], BF16))
        rbf = E(nc.sbuf_tensor("rbf", [128, DFF], BF16))
        accf = E(nc.sbuf_tensor("accf", [128, DFF], F32))
        tmpf = E(nc.sbuf_tensor("tmpf", [128, DFF], F32))
        wdq = E(nc.sbuf_tensor("wdq", [128, DFF], BF16))
        slab = E(nc.sbuf_tensor("slab", [128, DFF], BF16))
        dqrow = E(nc.sbuf_tensor("dqrow", [128, 2 * 88], BF16))
        rcrow = E(nc.sbuf_tensor("rcrow", [128, 2 * 88], F32))
        PM = E(nc.psum_tensor("PM", [128, 512], F32))
        PT = E(nc.psum_tensor("PT", [128, 2048], BF16))

        b = B(nc)
        semnames = ("dC", "sQ0", "sQ1", "sR0", "sR1", "dQW", "dO",
                    "eT", "eV", "eA", "eP")
        sem = {n: Sem(E(nc.semaphore(n)), n) for n in semnames}
        dC, dQW, dO = sem["dC"], sem["dQW"], sem["dO"]
        sQ = (sem["sQ0"], sem["sQ1"])
        sR = (sem["sR0"], sem["sR1"])
        eT, eV, eA, eP = sem["eT"], sem["eV"], sem["eA"], sem["eP"]

        def dma(queue, out, in_, s, waits=()):
            return b.emit(queue, lambda e, o=out, i=in_: e.dma_start(out=o, in_=i),
                          waits=waits, inc=s, k=16)

        def mm(out, lhsT, rhs, waits=(), inc=eT):
            return b.emit("tensor",
                          lambda e, o=out, l=lhsT, r=rhs:
                          e.matmul(o, l, r, start=True, stop=True),
                          waits=waits, inc=inc)

        def vop(fn, waits=()):
            return b.emit("vector", fn, waits=waits, inc=eV)

        def pop(fn, waits=()):
            return b.emit("gpsimd", fn, waits=waits, inc=eP)

        dma("sync", ident[:], qident, dC)
        dma("sync", r88[:], crow88, dC)
        dma("sync", c88[:], cone88, dC)
        cbar = [(dC, dC.n)]
        for _e in ENGINES:
            b.emit(_e, lambda e: e.nop(), waits=cbar)

        def fullpass(src, scaldst, t0=None):
            toks = []
            tprev = [t0, t0]
            for t in range(8):
                par = t % 2
                w = [(eV, tprev[par])] if tprev[par] is not None else []
                tl = dma("sync", qt[:, par * 16384:(par + 1) * 16384],
                         src.rearrange("a bx -> (a bx)")
                         .rearrange("(t p f) -> t p f", t=8, p=88)[t, :, :],
                         sQ[par], waits=w)
                tv = vop(lambda e, par=par, t=t: e.tensor_reduce(
                    scaldst[:, t * 256:(t + 1) * 256],
                    qt[:, par * 16384:(par + 1) * 16384]
                    .rearrange("p (c z) -> p c z", z=64),
                    axis=mybir.AxisListType.X, op=ALU.max,
                    apply_absolute_value=True), waits=[(sQ[par], tl)])
                tprev[par] = tv
                toks.append(tv)
            return toks[-1], toks

        tf12, _ = fullpass(wfull, scalf)
        tf3, _ = fullpass(w3full, scalf3, t0=tf12)

        def mean_of(scaldst, tfull, mcol_off, s11_off):
            rs = smalls[:, mcol_off + 1:mcol_off + 2]
            t1 = vop(lambda e: e.tensor_reduce(rs, scaldst[:],
                                               axis=mybir.AxisListType.X,
                                               op=ALU.add), waits=[(eV, tfull)])
            t2 = mm(PM[0:1, 0:1], c88[:], rs, waits=[(eV, t1)])
            t3 = vop(lambda e: e.tensor_scalar(s11[0:1, s11_off:s11_off + 1],
                                               PM[0:1, 0:1], 1.0 / 180224.0, None,
                                               op0=ALU.mult), waits=[(eT, t2)])
            t4 = vop(lambda e: e.tensor_copy(s11b[0:1, s11_off:s11_off + 1],
                                             s11[0:1, s11_off:s11_off + 1]),
                     waits=[(eV, t3)])
            t5 = vop(lambda e: e.tensor_copy(s11[0:1, 4 + s11_off:5 + s11_off],
                                             s11b[0:1, s11_off:s11_off + 1]),
                     waits=[(eV, t4)])
            t6 = mm(PM[0:88, 1:2], r88[:], s11[0:1, 4 + s11_off:5 + s11_off],
                    waits=[(eV, t5)])
            t7 = vop(lambda e: e.tensor_copy(smalls[:, mcol_off:mcol_off + 1],
                                             PM[0:88, 1:2]), waits=[(eT, t6)])
            return t7

        tm12 = mean_of(scalf, tf12, 0, 0)
        tm3 = mean_of(scalf3, tf3, 2, 1)

        def scaler_stage(rsrc, ntiles, mcol_off, tmean, dq_d, rc_d, tprevload):
            mcol = smalls[:, mcol_off:mcol_off + 1]
            S = lambda a: sq[:, a * 256:(a + 1) * 256]
            scal, yv, tv, Fv, bb2, rcs = S(0), S(1), S(2), S(3), S(4), S(5)
            m1 = smalls[:, 4:5]
            qf = smalls[:, 5:6]
            rqf = smalls[:, 6:7]
            tlast = tprevload
            tEnd = None
            for st in range(ntiles):
                par = st % 2
                w = [(eV, tlast)] if tlast is not None else []
                tl = dma("sync", qt[:, par * 16384:(par + 1) * 16384],
                         rsrc.rearrange("a bx -> (a bx)")
                         .rearrange("(t p f) -> t p f", t=ntiles, p=88)[st, :, :],
                         sQ[par], waits=w)
                t1 = vop(lambda e, par=par: e.tensor_reduce(
                    scal, qt[:, par * 16384:(par + 1) * 16384]
                    .rearrange("p (c z) -> p c z", z=64),
                    axis=mybir.AxisListType.X, op=ALU.max,
                    apply_absolute_value=True),
                    waits=[(sQ[par], tl), (dQW, dQW.n)])
                t2 = vop(lambda e: e.tensor_scalar(cents[:], scal, mcol, None,
                                                   op0=ALU.subtract),
                         waits=[(eV, t1), (eV, tmean)])
                t3 = vop(lambda e: e.tensor_reduce(m1, cents[:],
                                                   axis=mybir.AxisListType.X,
                                                   op=ALU.max,
                                                   apply_absolute_value=True),
                         waits=[(eV, t2)])
                t4 = vop(lambda e: e.reciprocal(qf, m1), waits=[(eV, t3)])
                t5 = vop(lambda e: e.tensor_scalar(qf, qf, 128.0, None,
                                                   op0=ALU.mult), waits=[(eV, t4)])
                t6 = vop(lambda e: e.tensor_scalar(yv, cents[:], qf, None,
                                                   op0=ALU.mult), waits=[(eV, t5)])
                t7 = vop(lambda e: e.tensor_scalar(tv, yv, 128.5, None,
                                                   op0=ALU.add), waits=[(eV, t6)])
                t8 = vop(lambda e: e.memset(Fv, 0.0), waits=[(eV, t7)])
                tk = t8
                for k in range(8, -1, -1):
                    ka = vop(lambda e, k=k: e.tensor_scalar(
                        bb2, tv, float(2 ** k), None, op0=ALU.is_ge),
                        waits=[(eV, tk)])
                    kb = vop(lambda e, k=k: e.scalar_tensor_tensor(
                        tv, bb2, float(-(2 ** k)), tv, op0=ALU.mult, op1=ALU.add),
                        waits=[(eV, ka)])
                    tk = vop(lambda e, k=k: e.scalar_tensor_tensor(
                        Fv, bb2, float(2 ** k), Fv, op0=ALU.mult, op1=ALU.add),
                        waits=[(eV, kb)])
                t9 = vop(lambda e: e.tensor_scalar(Fv, Fv, -128.0, 127.0,
                                                   op0=ALU.add, op1=ALU.min),
                         waits=[(eV, tk)])
                t10 = vop(lambda e: e.tensor_scalar(Fv, Fv, -128.0, None,
                                                    op0=ALU.max), waits=[(eV, t9)])
                t11 = vop(lambda e: e.reciprocal(rqf, qf), waits=[(eV, t10)])
                t12 = vop(lambda e: e.tensor_scalar(dqs[:], Fv, rqf, None,
                                                    op0=ALU.mult), waits=[(eV, t11)])
                t13 = vop(lambda e: e.tensor_scalar(dqs[:], dqs[:], mcol, None,
                                                    op0=ALU.add), waits=[(eV, t12)])
                t14 = vop(lambda e: e.reciprocal(rcs, scal), waits=[(eV, t13)])
                nrows = 16384 // (dq_d.shape[1] * 64)
                td1 = dma("gpsimd", dq_d.rearrange("a bx -> (a bx)")
                          [st * 22528:(st + 1) * 22528]
                          .rearrange("(p c) -> p c", p=88), dqs[:], dQW,
                          waits=[(eV, t13)])
                td2 = dma("gpsimd", rc_d.rearrange("a bx -> (a bx)")
                          [st * 22528:(st + 1) * 22528]
                          .rearrange("(p c) -> p c", p=88), rcs, dQW,
                          waits=[(eV, t14)])
                tlast = t14
                tEnd = t14
            return tEnd, dQW.n

        tq12, _ = scaler_stage(rw12, 2, 0, tm12, dq12, rc12, tf3)
        tq3, ndqw = scaler_stage(rw3, 1, 2, tm3, dq3, rc3, tq12)

        def row_stage(rsrc, nrt, width, dq_d, rc_d, agdst, t_prev, s_prev=None):
            nbl = width // 64
            ndc = width // 128
            tev_bank = [t_prev, t_prev]
            tstores = [s_prev] if s_prev else []
            tlw = [t_prev, t_prev]
            for t in range(nrt):
                par = t % 2
                w = [(dQW, ndqw)]
                if tlw[par] is not None:
                    w.append((eV, tlw[par]))
                tl1 = dma("sync", wrow[:, par * width:(par + 1) * width],
                          rsrc[128 * t:128 * (t + 1), :], sR[par], waits=w)
                tl2 = dma("sync", dqrow[:, par * nbl:(par + 1) * nbl],
                          dq_d[128 * t:128 * (t + 1), :], sR[par])
                tl3 = dma("sync", rcrow[:, par * nbl:(par + 1) * nbl],
                          rc_d[128 * t:128 * (t + 1), :], sR[par])
                tok = sR[par].n
                t1 = vop(lambda e, par=par: e.tensor_tensor(
                    rbf[:, 0:width].rearrange("p (c z) -> p c z", z=64),
                    wrow[:, par * width:(par + 1) * width]
                    .rearrange("p (c z) -> p c z", z=64),
                    rcrow[:, par * nbl:(par + 1) * nbl].unsqueeze(2)
                    .broadcast_to([128, nbl, 64]), op=ALU.mult),
                    waits=[(sR[par], tok)])
                t2 = vop(lambda e: e.memset(accf[:, 0:width], float(NF4[0])),
                         waits=[(eV, t1)])
                tk = t2
                for j in range(15):
                    ja = vop(lambda e, j=j: e.tensor_scalar(
                        tmpf[:, 0:width], rbf[:, 0:width], MID[j], GAP[j],
                        op0=ALU.is_gt, op1=ALU.mult), waits=[(eV, tk)])
                    tk = vop(lambda e, j=j: e.tensor_tensor(
                        accf[:, 0:width], accf[:, 0:width], tmpf[:, 0:width],
                        op=ALU.add), waits=[(eV, ja)])
                t3 = vop(lambda e, par=par: e.tensor_tensor(
                    wdq[:, 0:width].rearrange("p (c z) -> p c z", z=64),
                    accf[:, 0:width].rearrange("p (c z) -> p c z", z=64),
                    dqrow[:, par * nbl:(par + 1) * nbl].unsqueeze(2)
                    .broadcast_to([128, nbl, 64]), op=ALU.mult),
                    waits=[(eV, tk)])
                tlw[par] = t3
                tev = None
                first_evac = True
                for dc in range(ndc):
                    bank = dc % 2
                    slot = bank * 1024
                    w2 = [(eV, t3)]
                    if tev_bank[bank] is not None:
                        w2.append((eV, tev_bank[bank]))
                    tt_ = b.emit("tensor", lambda e, dc=dc, slot=slot: e.transpose(
                        PT[:, slot:slot + 128],
                        wdq[:, dc * 128:(dc + 1) * 128], ident[:]),
                        waits=w2, inc=eT)
                    w3_ = [(eT, tt_)]
                    if first_evac and tstores:
                        w3_.append((dO, tstores[-1]))
                        first_evac = False
                    tev = vop(lambda e, dc=dc, slot=slot: e.tensor_copy(
                        slab[:, dc * 128:(dc + 1) * 128], PT[:, slot:slot + 128]),
                        waits=w3_)
                    tev_bank[bank] = tev
                ts_ = dma("gpsimd", agdst[128 * t:128 * (t + 1), :],
                          slab[:, 0:width], dO, waits=[(eV, tev)])
                tstores.append(ts_)
            return tstores

        st1 = row_stage(rw12, W12_FC, DM, dq12, rc12, ag1s, tq3)
        st2 = row_stage(rw3, 2, DFF, dq3, rc3, ag2s, eV.n, s_prev=st1[-1])
        b.emit("gpsimd", lambda e: e.nop(), waits=[(dO, dO.n)])
        b.replay(E(nc.Block()))
    return nc


def host_inputs_quant(core, w1, w2, w3):
    wfull = w1 if core < 4 else w2
    r = core % 4
    return {
        "wfull": np.ascontiguousarray(wfull),
        "w3full": np.ascontiguousarray(w3),
        "rw12": np.ascontiguousarray(wfull[W12_SH * r:W12_SH * (r + 1), :]),
        "rw3": np.ascontiguousarray(w3[W3_SH * core:W3_SH * (core + 1), :]),
        "qident": np.eye(128, dtype=bfdt),
        "crow88": np.ones((1, 88), np.float32),
        "cone88": np.ones((88, 1), np.float32),
    }


FUSED = True


def kernel(**inputs):
    x = np.asarray(inputs["L0_attn_hidden_states"])
    w1 = np.ascontiguousarray(np.asarray(inputs["L0_ffn_weight1"]))
    w2 = np.ascontiguousarray(np.asarray(inputs["L0_ffn_weight2"]))
    w3 = np.ascontiguousarray(np.asarray(inputs["L0_ffn_weight3"]))
    attn = {k: np.ascontiguousarray(np.asarray(inputs["L0_attn_" + k]))
            for k in ("q_w", "q_b", "k_w", "k_b", "v_w", "v_b", "og_w",
                      "o_w", "o_b", "b_w", "b_b")}

    if FUSED:
        ncf = build_forward(stage="full", fused=True)
        fmaps = []
        for c in range(NCORES):
            m = host_inputs_forward(c, x, None, None, attn)
            q = host_inputs_quant(c, w1, w2, w3)
            for k in ("qident", "wfull", "w3full"):
                del q[k]
            m.update(q)
            m12 = np.zeros((1, 4), np.float32)
            m12[0, 0 if c < 4 else 1] = 1.0
            m3 = np.zeros((1, 4), np.float32)
            m3[0, 2] = 1.0
            m["cm12"] = m12
            m["cm3"] = m3
            m["cnmid"] = np.tile(-np.array(MID, np.float32), (128, 1))
            fmaps.append(m)
        resf = run_bass_kernel_spmd(ncf, fmaps, core_ids=list(range(NCORES)))
    else:
        ncq = build_quant()
        qmaps = [host_inputs_quant(c, w1, w2, w3) for c in range(NCORES)]
        resq = run_bass_kernel_spmd(ncq, qmaps, core_ids=list(range(NCORES)))
        a1 = np.stack([np.asarray(resq.results[c]["ag1s"])
                       for c in range(NCORES)])
        w12ag = np.ascontiguousarray(
            a1.reshape(8, W12_FC, 128, DM).transpose(1, 0, 2, 3)
            .reshape(8 * W12_SH, DM))
        a2 = np.stack([np.asarray(resq.results[c]["ag2s"])
                       for c in range(NCORES)])
        w3ag = np.ascontiguousarray(
            a2.reshape(8, 2, 128, DFF).transpose(1, 0, 2, 3)
            .reshape(8 * W3_SH, DFF))
        ncf = build_forward(stage="full")
        fmaps = [host_inputs_forward(c, x, w12ag, w3ag, attn)
                 for c in range(NCORES)]
        resf = run_bass_kernel_spmd(ncf, fmaps, core_ids=list(range(NCORES)))
    out = np.concatenate([np.asarray(resf.results[c]["y_out"]).T
                          for c in range(NCORES)], axis=0)
    return np.ascontiguousarray(out.reshape(2, 4096, DM))


# version 12
# speedup vs baseline: 1.0067x; 1.0067x over previous
import math
from contextlib import ExitStack

import numpy as np
import ml_dtypes

import concourse.bass as bass
import concourse.mybir as mybir
from concourse.bass_utils import run_bass_kernel_spmd

F32 = mybir.dt.float32
BF16 = mybir.dt.bfloat16
U8 = mybir.dt.uint8
AF = mybir.ActivationFunctionType
ALU = mybir.AluOpType
bfdt = ml_dtypes.bfloat16

NCORES = 8
DM = 2048
DFF = 5632
NDC = DM // 128
NFC = DFF // 128
TOK = 1024
HALF = 512
NSUB = HALF // 128
NH, DH, CH = 16, 128, 64
NCH = HALF // CH
EPS_LN = 1e-5
SCALE = float(DH) ** 0.5

NF4 = [float(bfdt(v)) for v in (-1.0, -0.6962, -0.5251, -0.3949, -0.2844, -0.1848,
                                -0.0911, 0.0, 0.0796, 0.1609, 0.2461, 0.3379,
                                0.4407, 0.5626, 0.723, 1.0)]
MID = [(NF4[j] + NF4[j + 1]) * 0.5 for j in range(15)]

W12_SH = DFF // 4
W12_FC = W12_SH // 128
W3_SH = DM // 8

ENGINES = ("sync", "tensor", "vector", "scalar", "gpsimd")


class Sem:
    __slots__ = ("h", "n", "name")

    def __init__(self, h, name):
        self.h = h
        self.n = 0
        self.name = name


class B:
    def __init__(self, nc):
        self.nc = nc
        self.q = {e: [] for e in ENGINES}
        self.last_wait = {e: {} for e in ENGINES}

    def emit(self, eng, fn, waits=(), inc=None, k=1):
        ws = []
        lw = self.last_wait[eng]
        for s, v in waits:
            if s is None or v is None or v <= 0:
                continue
            if lw.get(s.name, -1) >= v:
                continue
            lw[s.name] = v
            ws.append((s.h, v))
        tok = None
        if inc is not None:
            inc.n += k
            tok = inc.n
        self.q[eng].append((ws, fn, None if inc is None else inc.h, k))
        return tok

    def replay(self, block):
        for e in ENGINES:
            ops = self.q[e]

            def run(eng, ops=ops):
                for ws, fn, inch, k in ops:
                    for h, v in ws:
                        eng.wait_ge(h, v)
                    ins = fn(eng)
                    if inch is not None:
                        ins.then_inc(inch, k)

            getattr(block, e)(run)


def ts_(i, n=1):
    return slice(i * 128, (i + n) * 128)


NOACT = False
NOMM = False
GV_PLAIN_COPY = False
NO_TRANSPOSE = False
TR_MODE = "normal"
QK_TRANSPOSES = False
BETA_SUBS = 4
BETA_IN_Q2 = False
LN_SKIP = set()


def build_forward(stage="full", fused=False):
    nc = bass.Bass("TRN2", target_bir_lowering=False)

    x_in = nc.dram_tensor("x_in", [DM, TOK], F32, kind="ExternalInput").ap()
    if fused:
        rw12 = nc.dram_tensor("rw12", [W12_SH, DM], BF16, kind="ExternalInput").ap()
        cm12 = nc.dram_tensor("cm12", [1, 4], F32, kind="ExternalInput").ap()
        cnmid = nc.dram_tensor("cnmid", [128, 15], F32, kind="ExternalInput").ap()
        cm3 = nc.dram_tensor("cm3", [1, 4], F32, kind="ExternalInput").ap()
        ccin = nc.dram_tensor("ccin", [1, 4], F32).ap()
        ccout = nc.dram_tensor("ccout", [1, 4], F32).ap()
        rw3 = nc.dram_tensor("rw3", [W3_SH, DFF], BF16, kind="ExternalInput").ap()
        crow88 = nc.dram_tensor("crow88", [1, 88], F32, kind="ExternalInput").ap()
        cone88 = nc.dram_tensor("cone88", [88, 1], F32, kind="ExternalInput").ap()
        ag1s = nc.dram_tensor("ag1s", [W12_SH, DM], BF16).ap()
        ag2s = nc.dram_tensor("ag2s", [W3_SH, DFF], BF16).ap()
        wag12 = nc.dram_tensor("wag12", [8 * W12_SH, DM], BF16).ap()
        wag3 = nc.dram_tensor("wag3", [8 * W3_SH, DFF], BF16).ap()
        dq12 = nc.dram_tensor("dq12", [W12_SH, 32], BF16).ap()
        rc12 = nc.dram_tensor("rc12", [W12_SH, 32], F32).ap()
        dq3 = nc.dram_tensor("dq3", [W3_SH, 88], BF16).ap()
        rc3 = nc.dram_tensor("rc3", [W3_SH, 88], F32).ap()
    else:
        wag12 = nc.dram_tensor("wag12", [8 * W12_SH, DM],
                               BF16, kind="ExternalInput").ap()
        wag3 = nc.dram_tensor("wag3", [8 * W3_SH, DFF],
                              BF16, kind="ExternalInput").ap()
    wq = nc.dram_tensor("wq", [DM, DM], F32, kind="ExternalInput").ap()
    wk = nc.dram_tensor("wk", [DM, DM], F32, kind="ExternalInput").ap()
    wv = nc.dram_tensor("wv", [DM, DM], F32, kind="ExternalInput").ap()
    wog = nc.dram_tensor("wog", [DM, DM], F32, kind="ExternalInput").ap()
    wo = nc.dram_tensor("wo", [DM, DM], F32, kind="ExternalInput").ap()
    wbt = nc.dram_tensor("wbt", [DM, NH], F32, kind="ExternalInput").ap()
    qb = nc.dram_tensor("qb", [128, NDC], F32, kind="ExternalInput").ap()
    kb = nc.dram_tensor("kb", [128, NDC], F32, kind="ExternalInput").ap()
    vb = nc.dram_tensor("vb", [128, NDC], F32, kind="ExternalInput").ap()
    ob = nc.dram_tensor("ob", [128, NDC], F32, kind="ExternalInput").ap()
    bb = nc.dram_tensor("bb", [1, NH], F32, kind="ExternalInput").ap()
    cmask = nc.dram_tensor("cmask", [128, 128], F32, kind="ExternalInput").ap()
    cones_col = nc.dram_tensor("cones_col", [128, 1], BF16, kind="ExternalInput").ap()
    cones_row = nc.dram_tensor("cones_row", [1, 128], F32, kind="ExternalInput").ap()
    cident = nc.dram_tensor("cident", [128, 128], BF16, kind="ExternalInput").ap()

    y_out = nc.dram_tensor("y_out", [DM, TOK], F32, kind="ExternalOutput").ap()
    wcache = nc.dram_tensor("wcache", [5 * DM, DM], BF16).ap()

    with ExitStack() as ctx:
        E = ctx.enter_context

        bbs = E(nc.sbuf_tensor("bbs", [128, NH], F32))
        wbres = E(nc.sbuf_tensor("wbres", [128, NDC * NH], BF16))
        bscr = E(nc.sbuf_tensor("bscr", [128, NH], F32))
        bscr2 = E(nc.sbuf_tensor("bscr2", [128, NH], F32))
        beta = E(nc.sbuf_tensor("beta", [128, NSUB * NH], F32))
        xh = E(nc.sbuf_tensor("xh", [128, NDC * HALF], F32))
        xbf = E(nc.sbuf_tensor("xbf", [128, NDC * HALF], BF16))
        xln = E(nc.sbuf_tensor("xln", [128, NDC * HALF], BF16))
        arena = E(nc.sbuf_tensor("arena", [128, NFC * HALF], BF16))
        wbA = E(nc.sbuf_tensor("wbA", [128, 2 * DM], BF16))
        wbB = E(nc.sbuf_tensor("wbB", [128, 2 * DM], BF16))
        wb3 = E(nc.sbuf_tensor("wb3", [128, 2 * DFF], BF16))
        wf32 = E(nc.sbuf_tensor("wf32", [128, 2 * DM], F32))
        sc1 = E(nc.sbuf_tensor("sc1", [1, 4 * HALF], F32))
        bcB = E(nc.sbuf_tensor("bcB", [128, HALF], BF16))
        bcC = E(nc.sbuf_tensor("bcC", [128, HALF], BF16))
        lntmp = E(nc.sbuf_tensor("lntmp", [128, HALF], BF16))
        qbs = E(nc.sbuf_tensor("qbs", [128, NDC], F32))
        kbs = E(nc.sbuf_tensor("kbs", [128, NDC], F32))
        vbs = E(nc.sbuf_tensor("vbs", [128, NDC], F32))
        obs = E(nc.sbuf_tensor("obs", [128, NDC], F32))
        msk = E(nc.sbuf_tensor("msk", [128, 128], F32))
        onc = E(nc.sbuf_tensor("onc", [128, 1], BF16))
        onr = E(nc.sbuf_tensor("onr", [1, 128], F32))
        idn = E(nc.sbuf_tensor("idn", [128, 128], BF16))
        q2 = E(nc.sbuf_tensor("q2", [128, HALF], BF16))
        vt2 = E(nc.sbuf_tensor("vt2", [128, 2 * HALF], BF16))
        mskball = E(nc.sbuf_tensor("mskball", [128, NSUB * NH * 128], BF16))

        if fused:
            sqq = E(nc.sbuf_tensor("sqq", [88, 6 * 256], F32))
            dqs = E(nc.sbuf_tensor("dqs", [88, 256], BF16))
            cents = E(nc.sbuf_tensor("cents", [88, 256], BF16))
            smalls = E(nc.sbuf_tensor("smalls", [88, 16], F32))
            s11 = E(nc.sbuf_tensor("s11", [1, 8], F32))
            s11b = E(nc.sbuf_tensor("s11b", [1, 2], BF16))
            r88 = E(nc.sbuf_tensor("r88", [1, 88], F32))
            c88 = E(nc.sbuf_tensor("c88", [88, 1], F32))
            m12row = E(nc.sbuf_tensor("m12row", [1, 4], F32))
            m3row = E(nc.sbuf_tensor("m3row", [1, 4], F32))
            nmid = E(nc.sbuf_tensor("nmid", [128, 15], F32))
            rcr2 = E(nc.sbuf_tensor("rcr2", [128, 176], F32))
        PS1 = E(nc.psum_tensor("PS1", [128, 1024], F32))
        PS2 = E(nc.psum_tensor("PS2", [128, 1024], F32))
        PS3 = E(nc.psum_tensor("PS3", [128, 1024], F32))
        PT = E(nc.psum_tensor("PT", [128, 2048], BF16))

        b = B(nc)
        semnames = ("dX", "dC", "dP", "sBR", "sWA0", "sWA1", "sWB0", "sWB1",
                    "sW30", "sW31", "sWF0", "sWF1", "sCA", "sCB", "eT", "eV", "eA",
                    "sQ0", "sQ1", "sR0", "sR1", "dQW", "dO", "ccS", "eP",
                    "sWA2", "sWB2")
        sem = {n: Sem(E(nc.semaphore(n)), n) for n in semnames}
        dX, dC, dP, sBR = sem["dX"], sem["dC"], sem["dP"], sem["sBR"]
        sWA = (sem["sWA0"], sem["sWA1"])
        sWB = (sem["sWB0"], sem["sWB1"])
        sW3 = (sem["sW30"], sem["sW31"])
        sWF = (sem["sWF0"], sem["sWF1"])
        sCC = (sem["sCA"], sem["sCB"])
        eT, eV, eA = sem["eT"], sem["eV"], sem["eA"]
        sQ = (sem["sQ0"], sem["sQ1"])
        sR = (sem["sR0"], sem["sR1"])
        dQW, dO, ccS = sem["dQW"], sem["dO"], sem["ccS"]

        def dma(queue, out, in_, s, waits=()):
            return b.emit(queue, lambda e, o=out, i=in_: e.dma_start(out=o, in_=i),
                          waits=waits, inc=s, k=16)

        def mm(out, lhsT, rhs, start, stop, waits=(), inc=None):
            return b.emit("tensor",
                          lambda e, o=out, l=lhsT, r=rhs, st=start, sp=stop:
                          e.matmul(o, l, r, start=st, stop=sp),
                          waits=waits, inc=inc)

        def vop(fn, waits=()):
            return b.emit("vector", fn, waits=waits, inc=eV)

        def aop(fn, waits=()):
            return b.emit("scalar", fn, waits=waits, inc=eA)

        def xh_(dc):
            return xh[:, dc * HALF:(dc + 1) * HALF]

        def xbf_(dc):
            return xbf[:, dc * HALF:(dc + 1) * HALF]

        def xln_(dc):
            return xln[:, dc * HALF:(dc + 1) * HALF]

        def ar_(fc):
            return arena[:, fc * HALF:(fc + 1) * HALF]

        def slab_tile(buf, off, idx, width):
            return buf[:, off:off + NDC * width].rearrange(
                "p (c f) -> p c f", f=width)[:, idx, :]

        dma("sync", msk[:], cmask, dC)
        dma("sync", onc[:], cones_col, dC)
        dma("sync", onr[:], cones_row, dC)
        dma("sync", qbs[:], qb, dC)
        dma("sync", kbs[:], kb, dC)
        dma("sync", vbs[:], vb, dC)
        dma("sync", obs[:], ob, dC)
        dma("sync", idn[:], cident, dC)
        t_wbres = dma("gpsimd", wbres[:].rearrange("p (c n) -> p c n", n=NH),
                      wbt.rearrange("(c p) n -> p c n", p=128), sBR)
        t_bbrow = dma("sync", sc1[0:1, 0:NH], bb, dC)
        t_consts = t_bbrow
        for pi_, wsrc_ in enumerate((wq, wk, wog, wv, wo)):
            dma("gpsimd", wcache[DM * pi_:DM * (pi_ + 1), :], wsrc_,
                sem["sWF0"])
        cbar = [(dC, dC.n), (sBR, sBR.n)]
        for _e in ENGINES:
            b.emit(_e, lambda e: e.nop(), waits=cbar)
        t = mm(PS3[:, 0:NH], onr[:], sc1[0:1, 0:NH], True, True, inc=eT)
        t_bbs = vop(lambda e: e.tensor_copy(bbs[:], PS3[:, 0:NH]), waits=[(eT, t)])

        if fused:
            dma("sync", r88[:], crow88, dC)
            dma("sync", c88[:], cone88, dC)
            dma("sync", m12row[:], cm12, dC)
            dma("sync", m3row[:], cm3, dC)
            dma("sync", nmid[:], cnmid, dC)
            qt_ = (lambda: arena[0:88, 0:16384],
                   lambda: arena[0:88, 0:16384])
            scalfs = (wf32[0:88, 0:2048], wf32[0:88, 2048:4096])

            def q_fullpass(src, scaldst, t0=None, t0p=None):
                tprev = [t0, t0p]
                tv = tvp = None
                for t in range(8):
                    par = t % 2
                    w = ([(eV, tprev[par])] if tprev[par] is not None
                         else [(dC, dC.n)])
                    tl = dma("sync" if par == 0 else "scalar", qt_[par](),
                             src.rearrange("a bx -> (a bx)")
                             .rearrange("(t p f) -> t p f", t=8, p=88)[t, :, :],
                             sQ[par], waits=w)
                    fn = (lambda e, par=par, t=t: e.tensor_reduce(
                        scaldst[:, t * 256:(t + 1) * 256],
                        qt_[par]().rearrange("p (c z) -> p c z", z=64),
                        axis=mybir.AxisListType.X, op=ALU.max,
                        apply_absolute_value=True))
                    tv = vop(fn, waits=[(sQ[par], tl)])
                    tprev[par] = tv
                return tv, tvp

            def q_shard_scalers(rsrc, ntiles, dst, t0=None):
                tprev = t0
                tv = None
                for t in range(ntiles):
                    w = [(eV, tprev)] if tprev is not None else [(dC, dC.n)]
                    tl = dma("sync", qt_[0](),
                             rsrc.rearrange("a bx -> (a bx)")
                             .rearrange("(t p f) -> t p f", t=ntiles, p=88)[t, :, :],
                             sQ[0], waits=w)
                    tv = vop(lambda e, t=t: e.tensor_reduce(
                        dst[:, t * 256:(t + 1) * 256],
                        qt_[0]().rearrange("p (c z) -> p c z", z=64),
                        axis=mybir.AxisListType.X, op=ALU.max,
                        apply_absolute_value=True), waits=[(sQ[0], tl)])
                    tprev = tv
                return tv

            scal12k = wf32[0:88, 0:512]
            scal3k = wf32[0:88, 512:768]
            ts12 = q_shard_scalers(rw12, 2, scal12k)
            ts3 = q_shard_scalers(rw3, 1, scal3k, t0=ts12)

            rs12 = smalls[:, 12:13]
            rs3 = smalls[:, 13:14]
            t1 = vop(lambda e: e.tensor_reduce(rs12, scal12k,
                                               axis=mybir.AxisListType.X,
                                               op=ALU.add), waits=[(eV, ts12)])
            t1b = vop(lambda e: e.tensor_reduce(rs3, scal3k,
                                                axis=mybir.AxisListType.X,
                                                op=ALU.add), waits=[(eV, ts3)])
            t2 = mm(PS3[0:1, 0:1], c88[:], rs12, True, True,
                    waits=[(eV, t1)], inc=eT)
            t2b = mm(PS3[0:1, 1:2], c88[:], rs3, True, True,
                     waits=[(eV, t1b)], inc=eT)
            t3 = vop(lambda e: e.tensor_copy(s11[0:1, 0:1], PS3[0:1, 0:1]),
                     waits=[(eT, t2)])
            t3b = vop(lambda e: e.tensor_copy(s11[0:1, 1:2], PS3[0:1, 1:2]),
                      waits=[(eT, t2b)])
            t4 = vop(lambda e: e.tensor_scalar(s11[0:1, 4:8], m12row[:],
                                               s11[0:1, 0:1], None, op0=ALU.mult),
                     waits=[(eV, t3)])
            t5 = vop(lambda e: e.scalar_tensor_tensor(s11[0:1, 4:8], m3row[:],
                                                      s11[0:1, 1:2],
                                                      s11[0:1, 4:8],
                                                      op0=ALU.mult, op1=ALU.add),
                     waits=[(eV, t4), (eV, t3b)])
            tcw = dma("gpsimd", ccin, s11[0:1, 4:8], dQW, waits=[(eV, t5)])
            tcc = b.emit("gpsimd",
                         lambda e: e.collective_compute(
                             "AllReduce", ALU.add,
                             replica_groups=[list(range(8))],
                             ins=[ccin.opt()], outs=[ccout.opt()]),
                         waits=[(dQW, tcw)], inc=ccS, k=1)
            tcr = dma("gpsimd", s11[0:1, 4:8], ccout, dQW, waits=[(ccS, tcc)])

            def q_mean(mrow, mcol_off, s11_off):
                t6 = vop(lambda e: e.tensor_tensor(s11b[0:1, 0:4].bitcast(F32)
                                                   [0:1, 0:2][0:1, 0:2]
                                                   if False else
                                                   smalls[0:1, 8:12],
                                                   s11[0:1, 4:8], mrow[:],
                                                   op=ALU.mult),
                         waits=[(dQW, tcr)])
                t7 = vop(lambda e: e.tensor_reduce(s11[0:1, s11_off:s11_off + 1],
                                                   smalls[0:1, 8:12],
                                                   axis=mybir.AxisListType.X,
                                                   op=ALU.add), waits=[(eV, t6)])
                t8 = vop(lambda e: e.tensor_scalar(s11[0:1, s11_off:s11_off + 1],
                                                   s11[0:1, s11_off:s11_off + 1],
                                                   1.0 / 180224.0, None,
                                                   op0=ALU.mult), waits=[(eV, t7)])
                t9 = vop(lambda e: e.tensor_copy(s11b[0:1, s11_off:s11_off + 1],
                                                 s11[0:1, s11_off:s11_off + 1]),
                         waits=[(eV, t8)])
                t10 = vop(lambda e: e.tensor_copy(s11[0:1, 2 + s11_off:3 + s11_off],
                                                  s11b[0:1, s11_off:s11_off + 1]),
                          waits=[(eV, t9)])
                t11 = mm(PS3[0:88, 1:2], r88[:], s11[0:1, 2 + s11_off:3 + s11_off],
                         True, True, waits=[(eV, t10)], inc=eT)
                t12 = vop(lambda e: e.tensor_copy(smalls[:, mcol_off:mcol_off + 1],
                                                  PS3[0:88, 1:2]), waits=[(eT, t11)])
                return t12

            tm12 = q_mean(m12row, 0, 0)
            tm3 = q_mean(m3row, 2, 1)
            tf3 = tm3

            def q_scaler_stage(scal_res, ntiles, mcol_off, tmean, dq_d, rc_d,
                               tprevload):
                mcol = smalls[:, mcol_off:mcol_off + 1]
                S = lambda a: sqq[:, a * 256:(a + 1) * 256]
                yv, tv, Fv, bb2, rcs = S(1), S(2), S(3), S(4), S(5)
                m1 = smalls[:, 8:9]
                qf = smalls[:, 9:10]
                rqf = smalls[:, 10:11]
                tlast = tprevload
                for st in range(ntiles):
                    scal = scal_res[:, st * 256:(st + 1) * 256]
                    t1 = tlast
                    t2 = vop(lambda e, scal=scal: e.tensor_scalar(
                        cents[:], scal, mcol, None, op0=ALU.subtract),
                        waits=[(eV, tmean), (dQW, dQW.n)] +
                              ([(eV, tlast)] if tlast is not None else []))
                    t3 = vop(lambda e: e.tensor_reduce(m1, cents[:],
                                                       axis=mybir.AxisListType.X,
                                                       op=ALU.max,
                                                       apply_absolute_value=True),
                             waits=[(eV, t2)])
                    t4 = vop(lambda e: e.reciprocal(qf, m1), waits=[(eV, t3)])
                    t5 = vop(lambda e: e.tensor_scalar(qf, qf, 128.0, None,
                                                       op0=ALU.mult),
                             waits=[(eV, t4)])
                    t6 = vop(lambda e: e.tensor_scalar(yv, cents[:], qf, None,
                                                       op0=ALU.mult),
                             waits=[(eV, t5)])
                    t7 = vop(lambda e: e.tensor_scalar(tv, yv, 128.5, None,
                                                       op0=ALU.add),
                             waits=[(eV, t6)])
                    t8 = vop(lambda e: e.memset(Fv, 0.0), waits=[(eV, t7)])
                    tk = t8
                    for k in range(8, -1, -1):
                        ka = vop(lambda e, k=k: e.tensor_scalar(
                            bb2, tv, float(2 ** k), None, op0=ALU.is_ge),
                            waits=[(eV, tk)])
                        kb = vop(lambda e, k=k: e.scalar_tensor_tensor(
                            tv, bb2, float(-(2 ** k)), tv, op0=ALU.mult,
                            op1=ALU.add), waits=[(eV, ka)])
                        tk = vop(lambda e, k=k: e.scalar_tensor_tensor(
                            Fv, bb2, float(2 ** k), Fv, op0=ALU.mult,
                            op1=ALU.add), waits=[(eV, kb)])
                    t9 = vop(lambda e: e.tensor_scalar(Fv, Fv, -128.0, 127.0,
                                                       op0=ALU.add, op1=ALU.min),
                             waits=[(eV, tk)])
                    t10 = vop(lambda e: e.tensor_scalar(Fv, Fv, -128.0, None,
                                                        op0=ALU.max),
                              waits=[(eV, t9)])
                    t11 = vop(lambda e: e.reciprocal(rqf, qf), waits=[(eV, t10)])
                    t12 = vop(lambda e: e.tensor_scalar(dqs[:], Fv, rqf, None,
                                                        op0=ALU.mult),
                              waits=[(eV, t11)])
                    t13 = vop(lambda e: e.tensor_scalar(dqs[:], dqs[:], mcol, None,
                                                        op0=ALU.add),
                              waits=[(eV, t12)])
                    t14 = vop(lambda e, scal=scal: e.reciprocal(rcs, scal),
                              waits=[(eV, t13)])
                    dma("gpsimd", dq_d.rearrange("a bx -> (a bx)")
                        [st * 22528:(st + 1) * 22528]
                        .rearrange("(p c) -> p c", p=88), dqs[:], dQW,
                        waits=[(eV, t13)])
                    dma("gpsimd", rc_d.rearrange("a bx -> (a bx)")
                        [st * 22528:(st + 1) * 22528]
                        .rearrange("(p c) -> p c", p=88), rcs, dQW,
                        waits=[(eV, t14)])
                    tlast = t14
                return tlast

            tq12 = q_scaler_stage(scal12k, 2, 0, tm12, dq12, rc12, tm3)
            tq3 = q_scaler_stage(scal3k, 1, 2, tm3, dq3, rc3, tq12)
            ndqw = dQW.n

            eP = sem["eP"]

            def q_row_stage(rsrc, nrt, width, dq_d, rc_d, agdst, t_prev,
                            s_prev=None, tp_prev=None, agcb=None):
                nbl = width // 64
                ndc = width // 128
                hw = width // 2
                hb = nbl // 2
                wrow_ = lambda par: wb3[:, par * width:(par + 1) * width]
                rbf_ = xbf[:, 0:width]
                accf_ = arena[:, 0:width]
                tmpf_ = xln[:, 0:width]
                wdq_ = mskball[:, 0:width]
                slab_ = arena[:, width:2 * width]
                dqr_ = lambda par: q2[:, par * nbl:(par + 1) * nbl]
                rcr_ = lambda par: rcr2[:, par * nbl:(par + 1) * nbl]
                tev_bank = [t_prev, t_prev]
                tstores = [s_prev] if s_prev else []
                tlw = [t_prev, t_prev]
                tlwp = [tp_prev, tp_prev]

                C0 = float(NF4[0]) + sum(GAP) * 0.5
                sgn_ok = False

                def halfops(eng, inc_s, c0, c1, b0, b1, par, tok):
                    emitq = (lambda fn, waits: b.emit(eng, fn, waits=waits,
                                                      inc=inc_s))
                    nb = b1 - b0
                    t1 = emitq(lambda e, par=par: e.tensor_tensor(
                        rbf_[:, c0:c1].rearrange("p (c z) -> p c z", z=64),
                        wrow_(par)[:, c0:c1].rearrange("p (c z) -> p c z", z=64),
                        rcr_(par)[:, b0:b1].unsqueeze(2)
                        .broadcast_to([128, nb, 64]), op=ALU.mult),
                        [(sR[par], tok)])
                    if sgn_ok:
                        tmpb = (xln[:, width:2 * width],
                                xln[:, 2 * width:3 * width])
                        t2 = emitq(lambda e: e.memset(accf_[:, c0:c1], C0),
                                   [(inc_s, t1)])
                        tk = t2
                        tstt = {}
                        tsg = {}
                        for j in range(15):
                            wsg = [(eV, t1)]
                            if j >= 2:
                                wsg.append((eV, tstt[j - 2]))
                            tsg[j] = aop(lambda e, j=j: e.activation(
                                tmpb[j % 2][:, c0:c1],
                                rbf_[:, c0:c1], AF.Sign,
                                bias=nmid[:, j:j + 1]),
                                waits=wsg)
                            tk = emitq(lambda e, j=j: e.scalar_tensor_tensor(
                                accf_[:, c0:c1],
                                tmpb[j % 2][:, c0:c1], GAP[j] * 0.5,
                                accf_[:, c0:c1], op0=ALU.mult, op1=ALU.add),
                                [(eA, tsg[j]), (inc_s, tk)])
                            tstt[j] = tk
                    else:
                        t2 = emitq(lambda e: e.memset(accf_[:, c0:c1],
                                                      float(NF4[0])),
                                   [(inc_s, t1)])
                        tk = t2
                        for j in range(15):
                            ja = emitq(lambda e, j=j: e.tensor_scalar(
                                tmpf_[:, c0:c1], rbf_[:, c0:c1], MID[j], None,
                                op0=ALU.is_gt), [(inc_s, tk)])
                            tk = emitq(lambda e, j=j: e.scalar_tensor_tensor(
                                accf_[:, c0:c1], tmpf_[:, c0:c1], GAP[j],
                                accf_[:, c0:c1], op0=ALU.mult, op1=ALU.add),
                                [(inc_s, ja)])
                    t3 = emitq(lambda e, par=par: e.tensor_tensor(
                        wdq_[:, c0:c1].rearrange("p (c z) -> p c z", z=64),
                        accf_[:, c0:c1].rearrange("p (c z) -> p c z", z=64),
                        dqr_(par)[:, b0:b1].unsqueeze(2)
                        .broadcast_to([128, nb, 64]), op=ALU.mult), [(inc_s, tk)])
                    return t3

                for t in range(nrt):
                    par = t % 2
                    w = [(dQW, ndqw)]
                    if tlw[par] is not None:
                        w.append((eV, tlw[par]))
                    if tlwp[par] is not None:
                        w.append((eP, tlwp[par]))
                    dma("sync", wrow_(par), rsrc[128 * t:128 * (t + 1), :],
                        sR[par], waits=w)
                    dma("sync", dqr_(par), dq_d[128 * t:128 * (t + 1), :], sR[par])
                    dma("sync", rcr_(par), rc_d[128 * t:128 * (t + 1), :], sR[par])
                    tok = sR[par].n
                    t3v = halfops("vector", eV, 0, width, 0, nbl, par, tok)
                    t3p = None
                    tlw[par] = t3v
                    tlwp[par] = None
                    tev = None
                    first_evac = True
                    for dc in range(ndc):
                        bank = dc % 2
                        slot = bank * 1024
                        w2 = [(eV, t3v)]
                        if tev_bank[bank] is not None:
                            w2.append((eV, tev_bank[bank]))
                        tt_ = b.emit("tensor",
                                     lambda e, dc=dc, slot=slot: e.transpose(
                                         PT[:, slot:slot + 128],
                                         wdq_[:, dc * 128:(dc + 1) * 128], idn[:]),
                                     waits=w2, inc=eT)
                        w3_ = [(eT, tt_)]
                        if first_evac and tstores:
                            w3_.append((dO, tstores[-1]))
                            first_evac = False
                        tev = vop(lambda e, dc=dc, slot=slot: e.tensor_copy(
                            slab_[:, dc * 128:(dc + 1) * 128],
                            PT[:, slot:slot + 128]), waits=w3_)
                        tev_bank[bank] = tev
                    ts_ = dma("gpsimd", agdst[128 * t:128 * (t + 1), :],
                              slab_[:, 0:width], dO, waits=[(eV, tev)])
                    tstores.append(ts_)
                    if agcb is not None:
                        agcb(t, ts_)
                return tstores, tlw, tlwp

            def ag1cb(t, ts_):
                b.emit("gpsimd",
                       lambda e, t=t: e.collective_compute(
                           "AllGather", ALU.bypass,
                           replica_groups=[list(range(8))],
                           ins=[ag1s[128 * t:128 * (t + 1), :].opt()],
                           outs=[wag12[1024 * t:1024 * (t + 1), :].opt()]),
                       waits=[(dO, ts_)], inc=ccS, k=1)

            def ag2cb(t, ts_):
                b.emit("gpsimd",
                       lambda e, t=t: e.collective_compute(
                           "AllGather", ALU.bypass,
                           replica_groups=[list(range(8))],
                           ins=[ag2s[128 * t:128 * (t + 1), :].opt()],
                           outs=[wag3[1024 * t:1024 * (t + 1), :].opt()]),
                       waits=[(dO, ts_)], inc=ccS, k=1)

            st1, tl1, tp1 = q_row_stage(rw12, W12_FC, DM, dq12, rc12, ag1s, tq3,
                                        agcb=ag1cb)
            q_row_stage(rw3, 2, DFF, dq3, rc3, ag2s, eV.n, s_prev=st1[-1],
                        tp_prev=max(t for t in tp1 if t is not None)
                        if any(tp1) else None, agcb=ag2cb)
            qbar = [(ss_, ss_.n) for ss_ in sem.values()]
            for _e in ENGINES:
                b.emit(_e, lambda e: e.nop(), waits=qbar)



        def halfprog(h):
            if h > 0:
                bar = [(ss, ss.n) for ss in sem.values()]
                for eng in ("sync", "tensor", "vector", "scalar", "gpsimd"):
                    b.emit(eng, lambda e: e.nop(), waits=bar)
            for dc in range(NDC):
                dma("sync", xh_(dc), x_in[ts_(dc), h * HALF:(h + 1) * HALF], dX)
            xtok = [(dX, dX.n)] * NDC

            def layernorm(xtok, extra_xbf=()):
                tb = []
                for dc in range(NDC):
                    w = [xtok[dc]] + list(extra_xbf)
                    tb.append(vop(lambda e, dc=dc: e.tensor_copy(xbf_(dc), xh_(dc)),
                                  waits=w))
                if "stats" in LN_SKIP:
                    tstat = b.emit("tensor", lambda e: e.nop(),
                                   waits=[(eV, tb[-1])], inc=eT)
                else:
                    tsq_consumed = None
                    for dc in range(NDC):
                        w = [(eV, tb[dc])]
                        if tsq_consumed is not None:
                            w.append((eT, tsq_consumed))
                        tq = vop(lambda e, dc=dc: e.tensor_tensor(
                            lntmp[:], xbf_(dc), xbf_(dc), op=ALU.mult), waits=w)
                        mm(PS1[0:1, 0:512], onc[:], xbf_(dc), dc == 0, dc == NDC - 1,
                           waits=[(eV, tb[dc])])
                        tsq_consumed = mm(PS2[0:1, 0:512], onc[:], lntmp[:],
                                          dc == 0, dc == NDC - 1,
                                          waits=[(eV, tq)], inc=eT)
                    tstat = tsq_consumed
                S = lambda a: sc1[0:1, a * 512:(a + 1) * 512]
                mean, ex2, m2, ve = S(0), S(1), S(2), S(3)
                r, t_, mrs = S(2), S(1), S(1)
                if "scal" in LN_SKIP:
                    tz0 = vop(lambda e: e.memset(bcB[:], 1.0), waits=[(eT, tstat)])
                    tBc = tCc = vop(lambda e: e.memset(bcC[:], 0.0),
                                    waits=[(eV, tz0)])
                    txl = []
                    if "apply" in LN_SKIP:
                        return tb
                    prev = tCc
                    for dc in range(NDC):
                        ta = vop(lambda e, dc=dc: e.tensor_tensor(
                            lntmp[:], xbf_(dc), bcB[:], op=ALU.mult),
                            waits=[(eV, tBc), (eV, prev)])
                        prev = vop(lambda e, dc=dc: e.tensor_tensor(
                            xln_(dc), lntmp[:], bcC[:], op=ALU.subtract),
                            waits=[(eV, ta)])
                        txl.append(prev)
                    return txl
                t1 = vop(lambda e: e.tensor_copy(mean, PS1[0:1, 0:512]),
                         waits=[(eT, tstat)])
                t2 = vop(lambda e: e.tensor_copy(ex2, PS2[0:1, 0:512]),
                         waits=[(eT, tstat)])
                t3 = vop(lambda e: e.tensor_scalar(mean, mean, 1.0 / DM, None,
                                                   op0=ALU.mult), waits=[(eV, t1)])
                t4 = vop(lambda e: e.tensor_scalar(ex2, ex2, 1.0 / DM, None,
                                                   op0=ALU.mult), waits=[(eV, t2)])
                t5 = vop(lambda e: e.tensor_tensor(m2, mean, mean, op=ALU.mult),
                         waits=[(eV, t3)])
                t6 = vop(lambda e: e.scalar_tensor_tensor(ve, ex2, EPS_LN, m2,
                                                          op0=ALU.add,
                                                          op1=ALU.subtract),
                         waits=[(eV, t4), (eV, t5)])
                if "act" in LN_SKIP:
                    t8v = vop(lambda e: e.memset(r, 1.0), waits=[(eV, t6)])
                    last = (eV, t8v)
                else:
                    t7 = aop(lambda e: e.activation(t_, ve, AF.Ln), waits=[(eV, t6)])
                    t8 = aop(lambda e: e.activation(r, t_, AF.Exp, scale=-0.5),
                             waits=[(eA, t7)])
                    last = (eA, t8)
                for _ in range(0 if "newton" in LN_SKIP else 2):
                    ta = vop(lambda e: e.tensor_tensor(t_, r, r, op=ALU.mult),
                             waits=[last])
                    tb2 = vop(lambda e: e.tensor_tensor(t_, t_, ve, op=ALU.mult),
                              waits=[(eV, ta)])
                    tc = vop(lambda e: e.tensor_scalar(t_, t_, -0.5, 1.5,
                                                       op0=ALU.mult, op1=ALU.add),
                             waits=[(eV, tb2)])
                    td = vop(lambda e: e.tensor_tensor(r, r, t_, op=ALU.mult),
                             waits=[(eV, tc)])
                    last = (eV, td)
                tm = vop(lambda e: e.tensor_tensor(mrs, mean, r, op=ALU.mult),
                         waits=[last])
                tB = mm(PS3[:, 0:512], onr[:], r, True, True,
                        waits=[(eV, tm)], inc=eT)
                tC = mm(PS3[:, 512:1024], onr[:], mrs, True, True,
                        waits=[(eV, tm)], inc=eT)
                tBc = vop(lambda e: e.tensor_copy(bcB[:], PS3[:, 0:512]),
                          waits=[(eT, tB)])
                tCc = vop(lambda e: e.tensor_copy(bcC[:], PS3[:, 512:1024]),
                          waits=[(eT, tC)])
                txl = []
                prev = tCc
                for dc in range(NDC):
                    ta = vop(lambda e, dc=dc: e.tensor_tensor(lntmp[:], xbf_(dc),
                                                              bcB[:], op=ALU.mult),
                             waits=[(eV, tBc), (eV, prev)])
                    prev = vop(lambda e, dc=dc: e.tensor_tensor(
                        xln_(dc), lntmp[:], bcC[:], op=ALU.subtract),
                        waits=[(eV, ta)])
                    txl.append(prev)
                return txl

            def ffn(txl, extra_arena=(), extra_wb3=(), extra_wAB=()):
                tslabA = [None, None]
                tsilu = [None] * NFC
                tmmA = [None] * NFC
                for fc in range(NFC):
                    r_, lfc = fc // W12_FC, fc % W12_FC
                    par = fc % 2
                    w = ([(eT, tmmA[fc - 2])] if fc >= 2 else
                         list(extra_wAB) + [(sCC[0], sCC[0].n)])
                    tslabA[par] = dma("sync", wbA[:, par * DM:(par + 1) * DM],
                                      wag12[(lfc * 8 + r_) * 128:
                                            (lfc * 8 + r_) * 128 + 128, :],
                                      sWA[par], waits=w)
                    bank = PS1[:, par * 512:(par + 1) * 512]
                    for dc in range(NDC):
                        wts = [(sWA[par], tslabA[par]), (eV, txl[dc])]
                        if dc == 0 and fc >= 2 and tsilu[fc - 2] is not None:
                            wts.append((eA, tsilu[fc - 2]))
                        tk = mm(bank, slab_tile(wbA, par * DM, dc, 128),
                                xln_(dc), dc == 0, dc == NDC - 1, waits=wts,
                                inc=eT if dc == NDC - 1 else None)
                    tmmA[fc] = tk
                    wts = [(eT, tmmA[fc])] + list(extra_arena)
                    tsilu[fc] = aop(lambda e, fc=fc, bank=bank: e.activation(
                        ar_(fc), bank, AF.Silu), waits=wts)
                tmmB = [None] * NFC
                th = [None] * NFC
                for fc in range(NFC):
                    r_, lfc = fc // W12_FC, fc % W12_FC
                    par = fc % 2
                    w = ([(eT, tmmB[fc - 2])] if fc >= 2 else
                         list(extra_wAB) + [(sCC[1], sCC[1].n)])
                    tsb = dma("sync", wbB[:, par * DM:(par + 1) * DM],
                              wag12[(lfc * 8 + r_ + 4) * 128:
                                    (lfc * 8 + r_ + 4) * 128 + 128, :],
                              sWB[par], waits=w)
                    bank = PS2[:, par * 512:(par + 1) * 512]
                    for dc in range(NDC):
                        wts = [(sWB[par], tsb), (eV, txl[dc])]
                        if dc == 0 and fc >= 2 and th[fc - 2] is not None:
                            wts.append((eV, th[fc - 2]))
                        tk = mm(bank, slab_tile(wbB, par * DM, dc, 128),
                                xln_(dc), dc == 0, dc == NDC - 1, waits=wts,
                                inc=eT if dc == NDC - 1 else None)
                    tmmB[fc] = tk
                    th[fc] = vop(lambda e, fc=fc, bank=bank: e.tensor_tensor(
                        ar_(fc), ar_(fc), bank, op=ALU.mult),
                        waits=[(eT, tmmB[fc]), (eA, tsilu[fc])])
                tmmC = [None] * NDC
                txn = [None] * NDC
                for dc in range(NDC):
                    r_, dcl = dc // 2, dc % 2
                    par = dc % 2
                    w = ([(eT, tmmC[dc - 2])] if dc >= 2 else []) + list(extra_wb3)
                    ts3 = dma("sync", wb3[:, par * DFF:(par + 1) * DFF],
                              wag3[(dcl * 8 + r_) * 128:
                                   (dcl * 8 + r_) * 128 + 128, :],
                              sW3[par], waits=w)
                    bank = PS3[:, par * 512:(par + 1) * 512]
                    for fc in range(NFC):
                        wts = [(sW3[par], ts3), (eV, th[fc])]
                        if fc == 0 and dc >= 2 and txn[dc - 2] is not None:
                            wts.append((eV, txn[dc - 2]))
                        tk = mm(bank,
                                wb3[:, par * DFF:(par + 1) * DFF].rearrange(
                                    "p (c f) -> p c f", f=128)[:, fc, :],
                                ar_(fc), fc == 0, fc == NFC - 1, waits=wts,
                                inc=eT if fc == NFC - 1 else None)
                    tmmC[dc] = tk
                    txn[dc] = vop(lambda e, dc=dc, bank=bank: e.scalar_tensor_tensor(
                        xh_(dc), bank, 0.5, xh_(dc), op0=ALU.mult, op1=ALU.add),
                        waits=[(eT, tmmC[dc])])
                return txn, tmmC[NDC - 1]

            def attention(txl, extra_arena, t_wAB_free, sub="all"):
                if sub == "ln2":
                    tdump = []
                    for dc in range(NDC):
                        tdump.append(vop(lambda e, dc=dc: e.tensor_copy(
                            xh_(dc), xln_(dc)),
                            waits=[(eV, eV.n), (eA, eA.n), (eT, eT.n)]))
                    return tdump, eT.n, eT.n
                qn_ = lambda oc: arena[:, 0 * 8192 + oc * HALF: 0 * 8192 + (oc + 1) * HALF]
                gate_ = lambda oc: arena[:, 8192 + oc * HALF: 8192 + (oc + 1) * HALF]
                kn_ = lambda oc: xbf_(oc)
                gv_ = lambda sub: wb3[:, sub * DM:(sub + 1) * DM]

                tbeta = []
                tprev = None
                for bsub in range(min(NSUB, BETA_SUBS)):
                    if not NOMM:
                        for dc in range(NDC):
                            w = [(eV, txl[dc])]
                            if dc == 0 and tprev is not None:
                                w.append((eV, tprev))
                            tz = mm(PS3[:, 0:NH],
                                    xln[:, dc * HALF + bsub * 128:dc * HALF + (bsub + 1) * 128],
                                    wbres[:].rearrange("p (c n) -> p c n", n=NH)[:, dc, :],
                                    dc == 0, dc == NDC - 1,
                                    waits=w, inc=eT if dc == NDC - 1 else None)
                    bsl = beta[:, bsub * NH:(bsub + 1) * NH]
                    if NOMM:
                        t0 = vop(lambda e, bsl=bsl: e.memset(bsl, 0.25),
                                 waits=([(eV, tprev)] if tprev else []))
                    else:
                        t0 = vop(lambda e, bsl=bsl: e.tensor_tensor(
                            bsl, PS3[:, 0:NH], bbs[:], op=ALU.add),
                            waits=[(eT, tz), (eV, t_bbs)])
                    if NOACT:
                        tprev = t0
                        tbeta.append(tprev)
                        continue
                    ta1 = aop(lambda e, bsl=bsl: e.activation(bscr[:], bsl, AF.Abs),
                              waits=[(eV, t0)])
                    ta2 = aop(lambda e: e.activation(bscr2[:], bscr[:], AF.Exp,
                                                     scale=-1.0), waits=[(eA, ta1)])
                    ta3 = aop(lambda e: e.activation(bscr[:], bscr2[:], AF.Ln,
                                                     bias=1.0), waits=[(eA, ta2)])
                    tprev = vop(lambda e, bsl=bsl: e.scalar_tensor_tensor(
                        bsl, bsl, 0.0, bscr[:], op0=ALU.max, op1=ALU.add),
                        waits=[(eA, ta3)])
                    tbeta.append(tprev)
                t_beta_all = tprev

                if sub == "beta":
                    tdump = []
                    for dc in range(NDC):
                        tdump.append(vop(lambda e, dc=dc: e.tensor_copy(
                            xh_(dc), xln_(dc)),
                            waits=[(eV, eV.n), (eA, eA.n), (eT, eT.n)]))
                    return tdump, eT.n, eT.n

                passes = [("q", wq, qbs, qn_), ("k", wk, kbs, kn_),
                          ("og", wog, None, gate_), ("v", wv, vbs, None)]
                if sub == "qk":
                    passes = passes[:2]
                SLAB4 = (lambda: wbA[:, 0:DM], lambda: wbA[:, DM:2 * DM],
                         lambda: wbB[:, 0:DM], lambda: wbB[:, DM:2 * DM])
                SSEM4 = (sWA[0], sWA[1], sWB[0], sWB[1])
                BANK4 = (lambda: PS2[:, 0:512], lambda: PS2[:, 512:1024],
                         lambda: PS1[:, 0:512], lambda: PS1[:, 512:1024])
                tslab = [None] * 4
                tconv = [None] * 4
                tgrp = [None] * 4
                tevac = [None] * 4
                normq = {}
                last_tss = [None]
                last_tn = [None]
                tgv = [None] * (NSUB * NH)
                tnorm_done = {}
                for pi, (pname, wsrc, bias, dst) in enumerate(passes):
                    for oc in range(NDC):
                        par = oc % 4
                        wpar = oc % 2
                        gi = pi * NDC + oc
                        w = [(eT, tgrp[par]) if tgrp[par] is not None
                             else (eT, t_wAB_free),
                             (sem["sWF0"], 80)]
                        tl = dma("scalar" if par < 2 else "sync", SLAB4[par](),
                                 wcache[DM * pi + 128 * oc:
                                        DM * pi + 128 * (oc + 1), :],
                                 SSEM4[par], waits=w)
                        slab = SLAB4[par]()
                        tslab_tok = (SSEM4[par], tl)
                        bank = BANK4[par]()
                        for kc in range(NDC):
                            wts = [tslab_tok, (eV, txl[kc])]
                            if kc == 0 and tevac[par] is not None:
                                wts.append(tevac[par])
                            tg = mm(bank, slab.rearrange("p (c f) -> p c f",
                                                         f=128)[:, kc, :],
                                    xln_(kc), kc == 0, kc == NDC - 1, waits=wts,
                                    inc=eT if kc == NDC - 1 else None)
                        tgrp[par] = tg
                        if pname in ("q", "k"):
                            dtile = dst(oc)
                            wts = [(eT, tg)]
                            if pname == "q":
                                wts += list(extra_arena)
                            te = aop(lambda e, dtile=dtile, bank=bank, bias=bias,
                                     oc=oc: e.activation(
                                dtile, bank, AF.Identity, bias=bias[:, oc:oc + 1]),
                                waits=wts)

                            def norm_stage1(st):
                                tss = mm(PS3[0:1, 512:1024], onc[:], q2[:],
                                         True, True,
                                         waits=[(eV, st["tq2"])], inc=eT)
                                st["tss"] = tss
                                S = lambda a: sc1[0:1, a * 512:(a + 1) * 512]
                                ss, tt = S(0), S(1)
                                rr = S(2 + st["oc"] % 2)
                                t1 = vop(lambda e, ss=ss: e.tensor_copy(
                                    ss, PS3[0:1, 512:1024]), waits=[(eT, tss)])
                                t7 = aop(lambda e, ss=ss, tt=tt: e.activation(
                                    tt, ss, AF.Ln), waits=[(eV, t1)])
                                t8 = aop(lambda e, rr=rr, tt=tt: e.activation(
                                    rr, tt, AF.Exp, scale=-0.5), waits=[(eA, t7)])
                                last = (eA, t8)
                                for _ in range(2):
                                    ta = vop(lambda e, rr=rr, tt=tt:
                                             e.tensor_tensor(tt, rr, rr,
                                                             op=ALU.mult),
                                             waits=[last])
                                    tb2 = vop(lambda e, ss=ss, tt=tt:
                                              e.tensor_tensor(tt, tt, ss,
                                                              op=ALU.mult),
                                              waits=[(eV, ta)])
                                    tc2 = vop(lambda e, tt=tt: e.tensor_scalar(
                                        tt, tt, -0.5, 1.5, op0=ALU.mult,
                                        op1=ALU.add), waits=[(eV, tb2)])
                                    td = vop(lambda e, rr=rr, tt=tt:
                                             e.tensor_tensor(rr, rr, tt,
                                                             op=ALU.mult),
                                             waits=[(eV, tc2)])
                                    last = (eV, td)
                                st["rr"] = rr
                                st["tnewt"] = last

                            def norm_stage2(st):
                                w_ = [st["tnewt"]]
                                if last_tn[0] is not None:
                                    w_.append((eV, last_tn[0]))
                                tfb = mm(PS3[:, 0:512], onr[:], st["rr"],
                                         True, True, waits=w_, inc=eT)
                                tn = vop(lambda e, dtile=st["dtile"]:
                                         e.tensor_tensor(
                                             dtile, dtile, PS3[:, 0:512],
                                             op=ALU.mult), waits=[(eT, tfb)])
                                tnorm_done[(st["pname"], st["oc"])] = tn
                                last_tn[0] = tn
                                return tn

                            if oc >= 1:
                                norm_stage1(normq[oc - 1])
                                last_tss[0] = normq[oc - 1]["tss"]
                            tq2 = vop(lambda e, dtile=dtile: e.tensor_tensor(
                                q2[:], dtile, dtile, op=ALU.mult),
                                waits=[(eA, te)] +
                                      ([(eT, last_tss[0])]
                                       if last_tss[0] is not None else []))
                            normq[oc] = {"dtile": dtile, "tq2": tq2, "oc": oc,
                                         "pname": pname}
                            if oc >= 2:
                                tevac[(oc - 2) % 4] = (eV,
                                                       norm_stage2(normq[oc - 2]))
                            if oc == NDC - 1:
                                norm_stage1(normq[oc])
                                last_tss[0] = normq[oc]["tss"]
                                tevac[(oc - 1) % 4] = (eV,
                                                       norm_stage2(normq[oc - 1]))
                                tevac[oc % 4] = (eV, norm_stage2(normq[oc]))
                                normq.clear()
                        elif pname == "og":
                            te = aop(lambda e, oc=oc, bank=bank: e.activation(
                                gate_(oc), bank, AF.Silu),
                                waits=[(eT, tg)] + list(extra_arena))
                            tevac[par] = (eA, te)
                        else:
                            vp = oc % 2
                            te = aop(lambda e, bank=bank, oc=oc, vp=vp: e.activation(
                                vt2[:, vp * HALF:(vp + 1) * HALF], bank, AF.Identity,
                                bias=vbs[:, oc:oc + 1]), waits=[(eT, tg)])
                            for vsub in range(NSUB):
                                g = oc * NSUB + vsub
                                slot = (g % 2) * 1024
                                w2 = [(eA, te)]
                                if g >= 2:
                                    w2.append((eV, tgv[g - 2]))
                                if NO_TRANSPOSE:
                                    tt_ = b.emit("tensor", lambda e: e.nop(),
                                                 waits=w2, inc=eT)
                                else:
                                    if TR_MODE == "fixed_in":
                                        tt_ = b.emit("tensor",
                                                     lambda e, slot=slot:
                                                     e.transpose(PT[:, slot:slot + 128],
                                                                 idn[:], idn[:]),
                                                     waits=w2, inc=eT)
                                    elif TR_MODE == "slot0":
                                        tt_ = b.emit("tensor",
                                                     lambda e, vsub=vsub, vp=vp:
                                                     e.transpose(
                                                         PT[:, 0:128],
                                                         vt2[:, vp * HALF + vsub * 128:
                                                             vp * HALF + (vsub + 1) * 128],
                                                         idn[:]),
                                                     waits=w2, inc=eT)
                                    else:
                                        tt_ = b.emit("tensor",
                                                     lambda e, vsub=vsub, vp=vp, slot=slot:
                                                     e.transpose(
                                                         PT[:, slot:slot + 128],
                                                         vt2[:, vp * HALF + vsub * 128:
                                                             vp * HALF + (vsub + 1) * 128],
                                                         idn[:]),
                                                     waits=w2, inc=eT)
                                if GV_PLAIN_COPY:
                                    tgv[oc * NSUB + vsub] = vop(
                                        lambda e, vsub=vsub, oc=oc, slot=slot:
                                        e.tensor_copy(
                                            gv_(vsub)[:, oc * 128:(oc + 1) * 128],
                                            PT[:, slot:slot + 128]),
                                        waits=[(eT, tt_), (eV, t_beta_all)])
                                else:
                                    tgv[oc * NSUB + vsub] = vop(
                                        lambda e, vsub=vsub, oc=oc, slot=slot:
                                        e.tensor_scalar(
                                            gv_(vsub)[:, oc * 128:(oc + 1) * 128],
                                            PT[:, slot:slot + 128],
                                            beta[:, vsub * NH + oc: vsub * NH + oc + 1],
                                            None, op0=ALU.mult),
                                        waits=[(eT, tt_), (eV, t_beta_all)])
                            tevac[par] = (eA, te)
                t_gv_all = eV.n
                t_kq_all = eV.n

                if sub == "qk" and QK_TRANSPOSES:
                    tlast = None
                    for i_ in range(8):
                        tlast = b.emit("tensor",
                                       lambda e, i_=i_: e.transpose(
                                           PT[:, i_ * 128:(i_ + 1) * 128],
                                           idn[:], idn[:]),
                                       waits=[(eT, eT.n), (eV, eV.n), (eA, eA.n)],
                                       inc=eT)
                if sub in ("proj", "qk"):
                    tdump = []
                    for dc in range(NDC):
                        tdump.append(vop(lambda e, dc=dc: e.tensor_copy(
                            xh_(dc), qn_(dc)),
                            waits=[(eV, eV.n), (eA, eA.n), (eT, eT.n)]))
                    return tdump, eT.n, eT.n

                t_projV, t_projA = eV.n, eA.n
                tmsk = [[None, None] for _ in range(NSUB)]
                for sp in range(NSUB):
                    for ic in range(2):
                        ps = PS1 if ic == 0 else PS2
                        for hh in range(NH):
                            w = [(eV, tnorm_done[("q", hh)]),
                                 (eV, tnorm_done[("k", hh)])]
                            if hh == 0 and sp == 0:
                                w += [(eV, t_projV), (eA, t_projA)]
                            if hh == 0 and sp >= 1:
                                w.append((eV, tmsk[sp - 1][ic]))
                            tsc = mm(ps[:, hh * CH:(hh + 1) * CH],
                                     kn_(hh)[:, sp * 128:(sp + 1) * 128],
                                     qn_(hh)[:, sp * 128 + ic * CH:
                                             sp * 128 + (ic + 1) * CH],
                                     True, True, waits=w,
                                     inc=eT if hh == NH - 1 else None)
                        mb = msk[:, ic * CH:(ic + 1) * CH].unsqueeze(1) \
                            .broadcast_to([128, NH, CH])
                        tmsk[sp][ic] = vop(
                            lambda e, sp=sp, ic=ic, ps=ps, mb=mb: e.tensor_tensor(
                                mskball[:, sp * NH * 128:(sp + 1) * NH * 128]
                                .rearrange("p (n i) -> p n i", i=128)
                                [:, :, ic * CH:(ic + 1) * CH],
                                ps[:, 0:NH * CH].rearrange("p (n i) -> p n i", i=CH),
                                mb, op=ALU.mult), waits=[(eT, tsc)])
                t_sc_last = eT.n

                tgat = [None] * NH
                for hh in range(NH):
                    par = hh % 2
                    bank = PS3[:, par * 512:(par + 1) * 512]
                    for sp in range(NSUB):
                        w = [(eV, tmsk[sp][0]), (eV, tmsk[sp][1]), (eV, t_gv_all)]
                        if sp == 0 and hh < 2:
                            w.append((eV, t_projV))
                        if sp == 0 and hh >= 2 and tgat[hh - 2] is not None:
                            w.append((eV, tgat[hh - 2]))
                        ta = mm(bank[:, sp * 128:(sp + 1) * 128],
                                gv_(sp)[:, hh * 128:(hh + 1) * 128],
                                mskball[:, sp * NH * 128 + hh * 128:
                                        sp * NH * 128 + (hh + 1) * 128],
                                True, True, waits=w,
                                inc=eT if sp == NSUB - 1 else None)
                    tgat[hh] = vop(lambda e, hh=hh, bank=bank: e.tensor_tensor(
                        gate_(hh), bank, gate_(hh), op=ALU.mult),
                        waits=[(eT, ta)])

                if sub == "score":
                    tdump = []
                    for dc in range(NDC):
                        tdump.append(vop(lambda e, dc=dc: e.tensor_copy(
                            xh_(dc), gate_(dc)),
                            waits=[(eV, eV.n), (eA, eA.n), (eT, eT.n)]))
                    return tdump, eT.n, eT.n

                tslab2 = [None, None]
                tgrp2 = [None] * 4
                txn = [None] * NDC
                for oc in range(NDC):
                    par = oc % 4
                    wpar = oc % 2
                    w = [(eT, tgrp2[par]) if tgrp2[par] is not None
                         else (eT, tgrp[par])]
                    tl = dma("scalar" if par < 2 else "sync", SLAB4[par](),
                             wcache[DM * 4 + 128 * oc:DM * 4 + 128 * (oc + 1), :],
                             SSEM4[par], waits=w)
                    slab = SLAB4[par]()
                    ttok = (SSEM4[par], tl)
                    bank = BANK4[par]()
                    for kc in range(NDC):
                        w = [ttok, (eV, tgat[kc])]
                        if kc == 0 and oc >= 4 and txn[oc - 4] is not None:
                            w.append((eV, txn[oc - 4]))
                        tg = mm(bank, slab.rearrange("p (c f) -> p c f",
                                                     f=128)[:, kc, :],
                                gate_(kc), kc == 0, kc == NDC - 1, waits=w,
                                inc=eT if kc == NDC - 1 else None)
                    tgrp2[par] = tg
                    txn[oc] = vop(lambda e, oc=oc, bank=bank: e.scalar_tensor_tensor(
                        xh_(oc), bank, obs[:, oc:oc + 1], xh_(oc),
                        op0=ALU.add, op1=ALU.add),
                        waits=[(eT, tg)])
                t_arena_free = eT.n
                return txn, t_sc_last, t_arena_free

            txl = layernorm(xtok)
            if stage == "ln1":
                tst = []
                for dc in range(NDC):
                    t = vop(lambda e, dc=dc: e.tensor_copy(xh_(dc), xln_(dc)),
                            waits=[(eV, txl[dc])])
                    tst.append(t)
                txn = tst
            elif stage == "betaonly":
                txn, t_sc_last, t_ar_free = attention(
                    txl, extra_arena=[], t_wAB_free=0, sub="beta")
            else:
                txn, t_ffn_last = ffn(txl)
                if stage not in ("ffn1",):
                    txl2 = layernorm([(eV, t) for t in txn],
                                     extra_xbf=[(eT, t_ffn_last)])
                    asub = {"attnp": "proj", "attns": "score", "attnb": "beta",
                            "attnq": "qk", "attnl": "ln2"}.get(stage, "all")
                    txn, t_sc_last, t_ar_free = attention(
                        txl2, extra_arena=[(eT, t_ffn_last)],
                        t_wAB_free=t_ffn_last, sub=asub)
                    if stage == "full" and asub == "all":
                        txl3 = layernorm([(eV, t) for t in txn],
                                         extra_xbf=[(eT, t_sc_last)])
                        txn, _ = ffn(txl3,
                                     extra_arena=[(eT, t_ar_free)],
                                     extra_wb3=[(eT, t_ar_free)],
                                     extra_wAB=[(eT, t_ar_free),
                                                (sem["sCA"], sem["sCA"].n),
                                                (sem["sCB"], sem["sCB"].n)])
            tst = []
            for dc in range(NDC):
                t = dma("gpsimd", y_out[ts_(dc), h * HALF:(h + 1) * HALF], xh_(dc),
                        dP, waits=[(eV, txn[dc])])
                tst.append(t)
            return tst

        halfprog(0)
        halfprog(1)
        b.emit("gpsimd", lambda e: e.nop(), waits=[(dP, dP.n)])

        b.replay(E(nc.Block()))

    return nc


def _blocked_lhsT(W):
    o, i = W.shape
    return np.ascontiguousarray(
        W.reshape(o // 128, 128, i // 128, 128).transpose(0, 3, 2, 1)
        .reshape(o, i))


def _bias_cols(v):
    return np.ascontiguousarray(v.reshape(NDC, 128).T)


def _mask2():
    m = np.zeros((128, 128), np.float32)
    tri = np.triu(np.ones((CH, CH), np.float32))
    m[0:CH, 0:CH] = tri
    m[CH:128, CH:128] = tri
    return m


def host_inputs_forward(core, x_full, w12ag, w3ag, attn):
    xs = x_full.reshape(8192, DM)[core * TOK:(core + 1) * TOK, :]
    m = {
        "x_in": np.ascontiguousarray(xs.T),
        "wq": _blocked_lhsT(attn["q_w"]), "wk": _blocked_lhsT(attn["k_w"]),
        "wv": _blocked_lhsT(attn["v_w"]), "wog": _blocked_lhsT(attn["og_w"]),
        "wo": _blocked_lhsT(attn["o_w"]),
        "wbt": np.ascontiguousarray(attn["b_w"].T),
        "qb": _bias_cols(attn["q_b"]), "kb": _bias_cols(attn["k_b"]),
        "vb": _bias_cols(attn["v_b"]), "ob": _bias_cols(attn["o_b"]),
        "bb": attn["b_b"].reshape(1, NH),
        "cmask": _mask2(),
        "cones_col": np.ones((128, 1), bfdt),
        "cones_row": np.ones((1, 128), np.float32),
        "cident": np.eye(128, dtype=bfdt),
    }
    if w12ag is not None:
        m["wag12"] = w12ag
        m["wag3"] = w3ag
    return m


def ag_layout_w12(w1deq, w2deq):
    out = np.empty((8 * W12_SH, DM), bfdt)
    for r in range(4):
        for src, base in ((w1deq, 0), (w2deq, 4)):
            sh = src[W12_SH * r:W12_SH * (r + 1), :]
            t = sh.T
            for fc in range(W12_FC):
                blk = np.empty((128, DM), bfdt)
                for dc in range(NDC):
                    blk[:, dc * 128:(dc + 1) * 128] = \
                        t[dc * 128:(dc + 1) * 128, fc * 128:(fc + 1) * 128]
                out[(fc * 8 + base + r) * 128:(fc * 8 + base + r) * 128 + 128,
                    :] = blk
    return out


def _ag_layout_w12_old(w1deq, w2deq):
    out = np.empty((8 * W12_SH, DM), bfdt)
    for r in range(4):
        for src, base in ((w1deq, 0), (w2deq, 4)):
            sh = src[W12_SH * r:W12_SH * (r + 1), :]
            t = sh.T
            blk = t.reshape(NDC, 128, W12_FC, 128)
            d = np.empty((W12_SH, DM), bfdt)
            for fc in range(W12_FC):
                for dc in range(NDC):
                    d[fc * 128:(fc + 1) * 128, dc * 128:(dc + 1) * 128] = \
                        t[dc * 128:(dc + 1) * 128, fc * 128:(fc + 1) * 128]
            out[(base + r) * W12_SH:(base + r + 1) * W12_SH, :] = d
    return out


def ag_layout_w3(w3deq):
    out = np.empty((8 * W3_SH, DFF), bfdt)
    t = w3deq.T
    for r in range(8):
        for dcl in range(2):
            dglob = r * 256 + dcl * 128
            blk = np.empty((128, DFF), bfdt)
            for fc in range(NFC):
                blk[:, fc * 128:(fc + 1) * 128] = \
                    t[fc * 128:(fc + 1) * 128, dglob:dglob + 128]
            out[(dcl * 8 + r) * 128:(dcl * 8 + r) * 128 + 128, :] = blk
    return out


def _ag_layout_w3_old(w3deq):
    out = np.empty((8 * W3_SH, DFF), bfdt)
    t = w3deq.T
    for r in range(8):
        blk = np.empty((W3_SH, DFF), bfdt)
        for dcl in range(2):
            dglob = r * 256 + dcl * 128
            for fc in range(NFC):
                blk[dcl * 128:(dcl + 1) * 128, fc * 128:(fc + 1) * 128] = \
                    t[fc * 128:(fc + 1) * 128, dglob:dglob + 128]
        out[r * W3_SH:(r + 1) * W3_SH, :] = blk
    return out




GAP = [NF4[j + 1] - NF4[j] for j in range(15)]


def build_quant(fused=False):
    nc = bass.Bass("TRN2", target_bir_lowering=False)
    wfull = nc.dram_tensor("wfull", [DFF, DM], BF16, kind="ExternalInput").ap()
    w3full = nc.dram_tensor("w3full", [DM, DFF], BF16, kind="ExternalInput").ap()
    rw12 = nc.dram_tensor("rw12", [W12_SH, DM], BF16, kind="ExternalInput").ap()
    rw3 = nc.dram_tensor("rw3", [W3_SH, DFF], BF16, kind="ExternalInput").ap()
    qident = nc.dram_tensor("qident", [128, 128], BF16, kind="ExternalInput").ap()
    crow88 = nc.dram_tensor("crow88", [1, 88], F32, kind="ExternalInput").ap()
    cone88 = nc.dram_tensor("cone88", [88, 1], F32, kind="ExternalInput").ap()
    ag1s = nc.dram_tensor("ag1s", [W12_SH, DM], BF16, kind="ExternalOutput").ap()
    ag2s = nc.dram_tensor("ag2s", [W3_SH, DFF], BF16, kind="ExternalOutput").ap()
    dq12 = nc.dram_tensor("dq12", [W12_SH, 32], BF16).ap()
    rc12 = nc.dram_tensor("rc12", [W12_SH, 32], F32).ap()
    dq3 = nc.dram_tensor("dq3", [W3_SH, 88], BF16).ap()
    rc3 = nc.dram_tensor("rc3", [W3_SH, 88], F32).ap()

    with ExitStack() as ctx:
        E = ctx.enter_context
        qt = E(nc.sbuf_tensor("qt", [88, 2 * 16384], BF16))
        scalf = E(nc.sbuf_tensor("scalf", [88, 2048], F32))
        scalf3 = E(nc.sbuf_tensor("scalf3", [88, 2048], F32))
        sq = E(nc.sbuf_tensor("sq", [88, 8 * 256], F32))
        dqs = E(nc.sbuf_tensor("dqs", [88, 256], BF16))
        cents = E(nc.sbuf_tensor("cents", [88, 256], BF16))
        smalls = E(nc.sbuf_tensor("smalls", [88, 16], F32))
        s11 = E(nc.sbuf_tensor("s11", [1, 8], F32))
        s11b = E(nc.sbuf_tensor("s11b", [1, 2], BF16))
        ident = E(nc.sbuf_tensor("ident", [128, 128], BF16))
        r88 = E(nc.sbuf_tensor("r88", [1, 88], F32))
        c88 = E(nc.sbuf_tensor("c88", [88, 1], F32))
        wrow = E(nc.sbuf_tensor("wrow", [128, 2 * DFF], BF16))
        rbf = E(nc.sbuf_tensor("rbf", [128, DFF], BF16))
        accf = E(nc.sbuf_tensor("accf", [128, DFF], F32))
        tmpf = E(nc.sbuf_tensor("tmpf", [128, DFF], F32))
        wdq = E(nc.sbuf_tensor("wdq", [128, DFF], BF16))
        slab = E(nc.sbuf_tensor("slab", [128, DFF], BF16))
        dqrow = E(nc.sbuf_tensor("dqrow", [128, 2 * 88], BF16))
        rcrow = E(nc.sbuf_tensor("rcrow", [128, 2 * 88], F32))
        PM = E(nc.psum_tensor("PM", [128, 512], F32))
        PT = E(nc.psum_tensor("PT", [128, 2048], BF16))

        b = B(nc)
        semnames = ("dC", "sQ0", "sQ1", "sR0", "sR1", "dQW", "dO",
                    "eT", "eV", "eA", "eP")
        sem = {n: Sem(E(nc.semaphore(n)), n) for n in semnames}
        dC, dQW, dO = sem["dC"], sem["dQW"], sem["dO"]
        sQ = (sem["sQ0"], sem["sQ1"])
        sR = (sem["sR0"], sem["sR1"])
        eT, eV, eA, eP = sem["eT"], sem["eV"], sem["eA"], sem["eP"]

        def dma(queue, out, in_, s, waits=()):
            return b.emit(queue, lambda e, o=out, i=in_: e.dma_start(out=o, in_=i),
                          waits=waits, inc=s, k=16)

        def mm(out, lhsT, rhs, waits=(), inc=eT):
            return b.emit("tensor",
                          lambda e, o=out, l=lhsT, r=rhs:
                          e.matmul(o, l, r, start=True, stop=True),
                          waits=waits, inc=inc)

        def vop(fn, waits=()):
            return b.emit("vector", fn, waits=waits, inc=eV)

        def pop(fn, waits=()):
            return b.emit("gpsimd", fn, waits=waits, inc=eP)

        dma("sync", ident[:], qident, dC)
        dma("sync", r88[:], crow88, dC)
        dma("sync", c88[:], cone88, dC)
        cbar = [(dC, dC.n)]
        for _e in ENGINES:
            b.emit(_e, lambda e: e.nop(), waits=cbar)

        def fullpass(src, scaldst, t0=None):
            toks = []
            tprev = [t0, t0]
            for t in range(8):
                par = t % 2
                w = [(eV, tprev[par])] if tprev[par] is not None else []
                tl = dma("sync", qt[:, par * 16384:(par + 1) * 16384],
                         src.rearrange("a bx -> (a bx)")
                         .rearrange("(t p f) -> t p f", t=8, p=88)[t, :, :],
                         sQ[par], waits=w)
                tv = vop(lambda e, par=par, t=t: e.tensor_reduce(
                    scaldst[:, t * 256:(t + 1) * 256],
                    qt[:, par * 16384:(par + 1) * 16384]
                    .rearrange("p (c z) -> p c z", z=64),
                    axis=mybir.AxisListType.X, op=ALU.max,
                    apply_absolute_value=True), waits=[(sQ[par], tl)])
                tprev[par] = tv
                toks.append(tv)
            return toks[-1], toks

        tf12, _ = fullpass(wfull, scalf)
        tf3, _ = fullpass(w3full, scalf3, t0=tf12)

        def mean_of(scaldst, tfull, mcol_off, s11_off):
            rs = smalls[:, mcol_off + 1:mcol_off + 2]
            t1 = vop(lambda e: e.tensor_reduce(rs, scaldst[:],
                                               axis=mybir.AxisListType.X,
                                               op=ALU.add), waits=[(eV, tfull)])
            t2 = mm(PM[0:1, 0:1], c88[:], rs, waits=[(eV, t1)])
            t3 = vop(lambda e: e.tensor_scalar(s11[0:1, s11_off:s11_off + 1],
                                               PM[0:1, 0:1], 1.0 / 180224.0, None,
                                               op0=ALU.mult), waits=[(eT, t2)])
            t4 = vop(lambda e: e.tensor_copy(s11b[0:1, s11_off:s11_off + 1],
                                             s11[0:1, s11_off:s11_off + 1]),
                     waits=[(eV, t3)])
            t5 = vop(lambda e: e.tensor_copy(s11[0:1, 4 + s11_off:5 + s11_off],
                                             s11b[0:1, s11_off:s11_off + 1]),
                     waits=[(eV, t4)])
            t6 = mm(PM[0:88, 1:2], r88[:], s11[0:1, 4 + s11_off:5 + s11_off],
                    waits=[(eV, t5)])
            t7 = vop(lambda e: e.tensor_copy(smalls[:, mcol_off:mcol_off + 1],
                                             PM[0:88, 1:2]), waits=[(eT, t6)])
            return t7

        tm12 = mean_of(scalf, tf12, 0, 0)
        tm3 = mean_of(scalf3, tf3, 2, 1)

        def scaler_stage(rsrc, ntiles, mcol_off, tmean, dq_d, rc_d, tprevload):
            mcol = smalls[:, mcol_off:mcol_off + 1]
            S = lambda a: sq[:, a * 256:(a + 1) * 256]
            scal, yv, tv, Fv, bb2, rcs = S(0), S(1), S(2), S(3), S(4), S(5)
            m1 = smalls[:, 4:5]
            qf = smalls[:, 5:6]
            rqf = smalls[:, 6:7]
            tlast = tprevload
            tEnd = None
            for st in range(ntiles):
                par = st % 2
                w = [(eV, tlast)] if tlast is not None else []
                tl = dma("sync", qt[:, par * 16384:(par + 1) * 16384],
                         rsrc.rearrange("a bx -> (a bx)")
                         .rearrange("(t p f) -> t p f", t=ntiles, p=88)[st, :, :],
                         sQ[par], waits=w)
                t1 = vop(lambda e, par=par: e.tensor_reduce(
                    scal, qt[:, par * 16384:(par + 1) * 16384]
                    .rearrange("p (c z) -> p c z", z=64),
                    axis=mybir.AxisListType.X, op=ALU.max,
                    apply_absolute_value=True),
                    waits=[(sQ[par], tl), (dQW, dQW.n)])
                t2 = vop(lambda e: e.tensor_scalar(cents[:], scal, mcol, None,
                                                   op0=ALU.subtract),
                         waits=[(eV, t1), (eV, tmean)])
                t3 = vop(lambda e: e.tensor_reduce(m1, cents[:],
                                                   axis=mybir.AxisListType.X,
                                                   op=ALU.max,
                                                   apply_absolute_value=True),
                         waits=[(eV, t2)])
                t4 = vop(lambda e: e.reciprocal(qf, m1), waits=[(eV, t3)])
                t5 = vop(lambda e: e.tensor_scalar(qf, qf, 128.0, None,
                                                   op0=ALU.mult), waits=[(eV, t4)])
                t6 = vop(lambda e: e.tensor_scalar(yv, cents[:], qf, None,
                                                   op0=ALU.mult), waits=[(eV, t5)])
                t7 = vop(lambda e: e.tensor_scalar(tv, yv, 128.5, None,
                                                   op0=ALU.add), waits=[(eV, t6)])
                t8 = vop(lambda e: e.memset(Fv, 0.0), waits=[(eV, t7)])
                tk = t8
                for k in range(8, -1, -1):
                    ka = vop(lambda e, k=k: e.tensor_scalar(
                        bb2, tv, float(2 ** k), None, op0=ALU.is_ge),
                        waits=[(eV, tk)])
                    kb = vop(lambda e, k=k: e.scalar_tensor_tensor(
                        tv, bb2, float(-(2 ** k)), tv, op0=ALU.mult, op1=ALU.add),
                        waits=[(eV, ka)])
                    tk = vop(lambda e, k=k: e.scalar_tensor_tensor(
                        Fv, bb2, float(2 ** k), Fv, op0=ALU.mult, op1=ALU.add),
                        waits=[(eV, kb)])
                t9 = vop(lambda e: e.tensor_scalar(Fv, Fv, -128.0, 127.0,
                                                   op0=ALU.add, op1=ALU.min),
                         waits=[(eV, tk)])
                t10 = vop(lambda e: e.tensor_scalar(Fv, Fv, -128.0, None,
                                                    op0=ALU.max), waits=[(eV, t9)])
                t11 = vop(lambda e: e.reciprocal(rqf, qf), waits=[(eV, t10)])
                t12 = vop(lambda e: e.tensor_scalar(dqs[:], Fv, rqf, None,
                                                    op0=ALU.mult), waits=[(eV, t11)])
                t13 = vop(lambda e: e.tensor_scalar(dqs[:], dqs[:], mcol, None,
                                                    op0=ALU.add), waits=[(eV, t12)])
                t14 = vop(lambda e: e.reciprocal(rcs, scal), waits=[(eV, t13)])
                nrows = 16384 // (dq_d.shape[1] * 64)
                td1 = dma("gpsimd", dq_d.rearrange("a bx -> (a bx)")
                          [st * 22528:(st + 1) * 22528]
                          .rearrange("(p c) -> p c", p=88), dqs[:], dQW,
                          waits=[(eV, t13)])
                td2 = dma("gpsimd", rc_d.rearrange("a bx -> (a bx)")
                          [st * 22528:(st + 1) * 22528]
                          .rearrange("(p c) -> p c", p=88), rcs, dQW,
                          waits=[(eV, t14)])
                tlast = t14
                tEnd = t14
            return tEnd, dQW.n

        tq12, _ = scaler_stage(rw12, 2, 0, tm12, dq12, rc12, tf3)
        tq3, ndqw = scaler_stage(rw3, 1, 2, tm3, dq3, rc3, tq12)

        def row_stage(rsrc, nrt, width, dq_d, rc_d, agdst, t_prev, s_prev=None):
            nbl = width // 64
            ndc = width // 128
            tev_bank = [t_prev, t_prev]
            tstores = [s_prev] if s_prev else []
            tlw = [t_prev, t_prev]
            for t in range(nrt):
                par = t % 2
                w = [(dQW, ndqw)]
                if tlw[par] is not None:
                    w.append((eV, tlw[par]))
                tl1 = dma("sync", wrow[:, par * width:(par + 1) * width],
                          rsrc[128 * t:128 * (t + 1), :], sR[par], waits=w)
                tl2 = dma("sync", dqrow[:, par * nbl:(par + 1) * nbl],
                          dq_d[128 * t:128 * (t + 1), :], sR[par])
                tl3 = dma("sync", rcrow[:, par * nbl:(par + 1) * nbl],
                          rc_d[128 * t:128 * (t + 1), :], sR[par])
                tok = sR[par].n
                t1 = vop(lambda e, par=par: e.tensor_tensor(
                    rbf[:, 0:width].rearrange("p (c z) -> p c z", z=64),
                    wrow[:, par * width:(par + 1) * width]
                    .rearrange("p (c z) -> p c z", z=64),
                    rcrow[:, par * nbl:(par + 1) * nbl].unsqueeze(2)
                    .broadcast_to([128, nbl, 64]), op=ALU.mult),
                    waits=[(sR[par], tok)])
                t2 = vop(lambda e: e.memset(accf[:, 0:width], float(NF4[0])),
                         waits=[(eV, t1)])
                tk = t2
                for j in range(15):
                    ja = vop(lambda e, j=j: e.tensor_scalar(
                        tmpf[:, 0:width], rbf[:, 0:width], MID[j], GAP[j],
                        op0=ALU.is_gt, op1=ALU.mult), waits=[(eV, tk)])
                    tk = vop(lambda e, j=j: e.tensor_tensor(
                        accf[:, 0:width], accf[:, 0:width], tmpf[:, 0:width],
                        op=ALU.add), waits=[(eV, ja)])
                t3 = vop(lambda e, par=par: e.tensor_tensor(
                    wdq[:, 0:width].rearrange("p (c z) -> p c z", z=64),
                    accf[:, 0:width].rearrange("p (c z) -> p c z", z=64),
                    dqrow[:, par * nbl:(par + 1) * nbl].unsqueeze(2)
                    .broadcast_to([128, nbl, 64]), op=ALU.mult),
                    waits=[(eV, tk)])
                tlw[par] = t3
                tev = None
                first_evac = True
                for dc in range(ndc):
                    bank = dc % 2
                    slot = bank * 1024
                    w2 = [(eV, t3)]
                    if tev_bank[bank] is not None:
                        w2.append((eV, tev_bank[bank]))
                    tt_ = b.emit("tensor", lambda e, dc=dc, slot=slot: e.transpose(
                        PT[:, slot:slot + 128],
                        wdq[:, dc * 128:(dc + 1) * 128], ident[:]),
                        waits=w2, inc=eT)
                    w3_ = [(eT, tt_)]
                    if first_evac and tstores:
                        w3_.append((dO, tstores[-1]))
                        first_evac = False
                    tev = vop(lambda e, dc=dc, slot=slot: e.tensor_copy(
                        slab[:, dc * 128:(dc + 1) * 128], PT[:, slot:slot + 128]),
                        waits=w3_)
                    tev_bank[bank] = tev
                ts_ = dma("gpsimd", agdst[128 * t:128 * (t + 1), :],
                          slab[:, 0:width], dO, waits=[(eV, tev)])
                tstores.append(ts_)
            return tstores

        st1 = row_stage(rw12, W12_FC, DM, dq12, rc12, ag1s, tq3)
        st2 = row_stage(rw3, 2, DFF, dq3, rc3, ag2s, eV.n, s_prev=st1[-1])
        b.emit("gpsimd", lambda e: e.nop(), waits=[(dO, dO.n)])
        b.replay(E(nc.Block()))
    return nc


def host_inputs_quant(core, w1, w2, w3):
    wfull = w1 if core < 4 else w2
    r = core % 4
    return {
        "wfull": np.ascontiguousarray(wfull),
        "w3full": np.ascontiguousarray(w3),
        "rw12": np.ascontiguousarray(wfull[W12_SH * r:W12_SH * (r + 1), :]),
        "rw3": np.ascontiguousarray(w3[W3_SH * core:W3_SH * (core + 1), :]),
        "qident": np.eye(128, dtype=bfdt),
        "crow88": np.ones((1, 88), np.float32),
        "cone88": np.ones((88, 1), np.float32),
    }


FUSED = True


def kernel(**inputs):
    x = np.asarray(inputs["L0_attn_hidden_states"])
    w1 = np.ascontiguousarray(np.asarray(inputs["L0_ffn_weight1"]))
    w2 = np.ascontiguousarray(np.asarray(inputs["L0_ffn_weight2"]))
    w3 = np.ascontiguousarray(np.asarray(inputs["L0_ffn_weight3"]))
    attn = {k: np.ascontiguousarray(np.asarray(inputs["L0_attn_" + k]))
            for k in ("q_w", "q_b", "k_w", "k_b", "v_w", "v_b", "og_w",
                      "o_w", "o_b", "b_w", "b_b")}

    if FUSED:
        ncf = build_forward(stage="full", fused=True)
        fmaps = []
        for c in range(NCORES):
            m = host_inputs_forward(c, x, None, None, attn)
            q = host_inputs_quant(c, w1, w2, w3)
            for k in ("qident", "wfull", "w3full"):
                del q[k]
            m.update(q)
            m12 = np.zeros((1, 4), np.float32)
            m12[0, 0 if c < 4 else 1] = 1.0
            m3 = np.zeros((1, 4), np.float32)
            m3[0, 2] = 1.0
            m["cm12"] = m12
            m["cm3"] = m3
            m["cnmid"] = np.tile(-np.array(MID, np.float32), (128, 1))
            fmaps.append(m)
        resf = run_bass_kernel_spmd(ncf, fmaps, core_ids=list(range(NCORES)))
    else:
        ncq = build_quant()
        qmaps = [host_inputs_quant(c, w1, w2, w3) for c in range(NCORES)]
        resq = run_bass_kernel_spmd(ncq, qmaps, core_ids=list(range(NCORES)))
        a1 = np.stack([np.asarray(resq.results[c]["ag1s"])
                       for c in range(NCORES)])
        w12ag = np.ascontiguousarray(
            a1.reshape(8, W12_FC, 128, DM).transpose(1, 0, 2, 3)
            .reshape(8 * W12_SH, DM))
        a2 = np.stack([np.asarray(resq.results[c]["ag2s"])
                       for c in range(NCORES)])
        w3ag = np.ascontiguousarray(
            a2.reshape(8, 2, 128, DFF).transpose(1, 0, 2, 3)
            .reshape(8 * W3_SH, DFF))
        ncf = build_forward(stage="full")
        fmaps = [host_inputs_forward(c, x, w12ag, w3ag, attn)
                 for c in range(NCORES)]
        resf = run_bass_kernel_spmd(ncf, fmaps, core_ids=list(range(NCORES)))
    out = np.concatenate([np.asarray(resf.results[c]["y_out"]).T
                          for c in range(NCORES)], axis=0)
    return np.ascontiguousarray(out.reshape(2, 4096, DM))


# version 13
# speedup vs baseline: 1.0100x; 1.0032x over previous
import math
from contextlib import ExitStack

import numpy as np
import ml_dtypes

import concourse.bass as bass
import concourse.mybir as mybir
from concourse.bass_utils import run_bass_kernel_spmd

F32 = mybir.dt.float32
BF16 = mybir.dt.bfloat16
U8 = mybir.dt.uint8
AF = mybir.ActivationFunctionType
ALU = mybir.AluOpType
bfdt = ml_dtypes.bfloat16

NCORES = 8
DM = 2048
DFF = 5632
NDC = DM // 128
NFC = DFF // 128
TOK = 1024
HALF = 512
NSUB = HALF // 128
NH, DH, CH = 16, 128, 64
NCH = HALF // CH
EPS_LN = 1e-5
SCALE = float(DH) ** 0.5

NF4 = [float(bfdt(v)) for v in (-1.0, -0.6962, -0.5251, -0.3949, -0.2844, -0.1848,
                                -0.0911, 0.0, 0.0796, 0.1609, 0.2461, 0.3379,
                                0.4407, 0.5626, 0.723, 1.0)]
MID = [(NF4[j] + NF4[j + 1]) * 0.5 for j in range(15)]

W12_SH = DFF // 4
W12_FC = W12_SH // 128
W3_SH = DM // 8

ENGINES = ("sync", "tensor", "vector", "scalar", "gpsimd")


class Sem:
    __slots__ = ("h", "n", "name")

    def __init__(self, h, name):
        self.h = h
        self.n = 0
        self.name = name


class B:
    def __init__(self, nc):
        self.nc = nc
        self.q = {e: [] for e in ENGINES}
        self.last_wait = {e: {} for e in ENGINES}

    def emit(self, eng, fn, waits=(), inc=None, k=1):
        ws = []
        lw = self.last_wait[eng]
        for s, v in waits:
            if s is None or v is None or v <= 0:
                continue
            if lw.get(s.name, -1) >= v:
                continue
            lw[s.name] = v
            ws.append((s.h, v))
        tok = None
        if inc is not None:
            inc.n += k
            tok = inc.n
        self.q[eng].append((ws, fn, None if inc is None else inc.h, k))
        return tok

    def replay(self, block):
        for e in ENGINES:
            ops = self.q[e]

            def run(eng, ops=ops):
                for ws, fn, inch, k in ops:
                    for h, v in ws:
                        eng.wait_ge(h, v)
                    ins = fn(eng)
                    if inch is not None:
                        ins.then_inc(inch, k)

            getattr(block, e)(run)


def ts_(i, n=1):
    return slice(i * 128, (i + n) * 128)


NOACT = False
NOMM = False
GV_PLAIN_COPY = False
NO_TRANSPOSE = False
TR_MODE = "normal"
QK_TRANSPOSES = False
BETA_SUBS = 4
BETA_IN_Q2 = False
LN_SKIP = set()


def build_forward(stage="full", fused=False):
    nc = bass.Bass("TRN2", target_bir_lowering=False)

    x_in = nc.dram_tensor("x_in", [DM, TOK], F32, kind="ExternalInput").ap()
    if fused:
        rw12 = nc.dram_tensor("rw12", [W12_SH, DM], BF16, kind="ExternalInput").ap()
        cm12 = nc.dram_tensor("cm12", [1, 4], F32, kind="ExternalInput").ap()
        cnmid = nc.dram_tensor("cnmid", [128, 15], F32, kind="ExternalInput").ap()
        cm3 = nc.dram_tensor("cm3", [1, 4], F32, kind="ExternalInput").ap()
        ccin = nc.dram_tensor("ccin", [1, 4], F32).ap()
        ccout = nc.dram_tensor("ccout", [1, 4], F32).ap()
        rw3 = nc.dram_tensor("rw3", [W3_SH, DFF], BF16, kind="ExternalInput").ap()
        crow88 = nc.dram_tensor("crow88", [1, 88], F32, kind="ExternalInput").ap()
        cone88 = nc.dram_tensor("cone88", [88, 1], F32, kind="ExternalInput").ap()
        ag1s = nc.dram_tensor("ag1s", [W12_SH, DM], BF16).ap()
        ag2s = nc.dram_tensor("ag2s", [W3_SH, DFF], BF16).ap()
        wag12 = nc.dram_tensor("wag12", [8 * W12_SH, DM], BF16).ap()
        wag3 = nc.dram_tensor("wag3", [8 * W3_SH, DFF], BF16).ap()
        dq12 = nc.dram_tensor("dq12", [W12_SH, 32], BF16).ap()
        rc12 = nc.dram_tensor("rc12", [W12_SH, 32], F32).ap()
        dq3 = nc.dram_tensor("dq3", [W3_SH, 88], BF16).ap()
        rc3 = nc.dram_tensor("rc3", [W3_SH, 88], F32).ap()
    else:
        wag12 = nc.dram_tensor("wag12", [8 * W12_SH, DM],
                               BF16, kind="ExternalInput").ap()
        wag3 = nc.dram_tensor("wag3", [8 * W3_SH, DFF],
                              BF16, kind="ExternalInput").ap()
    wq = nc.dram_tensor("wq", [DM, DM], F32, kind="ExternalInput").ap()
    wk = nc.dram_tensor("wk", [DM, DM], F32, kind="ExternalInput").ap()
    wv = nc.dram_tensor("wv", [DM, DM], F32, kind="ExternalInput").ap()
    wog = nc.dram_tensor("wog", [DM, DM], F32, kind="ExternalInput").ap()
    wo = nc.dram_tensor("wo", [DM, DM], F32, kind="ExternalInput").ap()
    wbt = nc.dram_tensor("wbt", [DM, NH], F32, kind="ExternalInput").ap()
    qb = nc.dram_tensor("qb", [128, NDC], F32, kind="ExternalInput").ap()
    kb = nc.dram_tensor("kb", [128, NDC], F32, kind="ExternalInput").ap()
    vb = nc.dram_tensor("vb", [128, NDC], F32, kind="ExternalInput").ap()
    ob = nc.dram_tensor("ob", [128, NDC], F32, kind="ExternalInput").ap()
    bb = nc.dram_tensor("bb", [1, NH], F32, kind="ExternalInput").ap()
    cmask = nc.dram_tensor("cmask", [128, 128], F32, kind="ExternalInput").ap()
    cones_col = nc.dram_tensor("cones_col", [128, 1], BF16, kind="ExternalInput").ap()
    cones_row = nc.dram_tensor("cones_row", [1, 128], F32, kind="ExternalInput").ap()
    cident = nc.dram_tensor("cident", [128, 128], BF16, kind="ExternalInput").ap()

    y_out = nc.dram_tensor("y_out", [DM, TOK], F32, kind="ExternalOutput").ap()
    wcache = nc.dram_tensor("wcache", [5 * DM, DM], BF16).ap()

    with ExitStack() as ctx:
        E = ctx.enter_context

        bbs = E(nc.sbuf_tensor("bbs", [128, NH], F32))
        wbres = E(nc.sbuf_tensor("wbres", [128, NDC * NH], BF16))
        bscr = E(nc.sbuf_tensor("bscr", [128, NH], F32))
        bscr2 = E(nc.sbuf_tensor("bscr2", [128, NH], F32))
        beta = E(nc.sbuf_tensor("beta", [128, NSUB * NH], F32))
        xh = E(nc.sbuf_tensor("xh", [128, NDC * HALF], F32))
        xbf = E(nc.sbuf_tensor("xbf", [128, NDC * HALF], BF16))
        xln = E(nc.sbuf_tensor("xln", [128, NDC * HALF], BF16))
        arena = E(nc.sbuf_tensor("arena", [128, NFC * HALF], BF16))
        wbA = E(nc.sbuf_tensor("wbA", [128, 2 * DM], BF16))
        wbB = E(nc.sbuf_tensor("wbB", [128, 2 * DM], BF16))
        wb3 = E(nc.sbuf_tensor("wb3", [128, 2 * DFF], BF16))
        wf32 = E(nc.sbuf_tensor("wf32", [128, 2 * DM], F32))
        sc1 = E(nc.sbuf_tensor("sc1", [1, 4 * HALF], F32))
        bcB = E(nc.sbuf_tensor("bcB", [128, HALF], BF16))
        bcC = E(nc.sbuf_tensor("bcC", [128, HALF], BF16))
        lntmp = E(nc.sbuf_tensor("lntmp", [128, HALF], BF16))
        qbs = E(nc.sbuf_tensor("qbs", [128, NDC], F32))
        kbs = E(nc.sbuf_tensor("kbs", [128, NDC], F32))
        vbs = E(nc.sbuf_tensor("vbs", [128, NDC], F32))
        obs = E(nc.sbuf_tensor("obs", [128, NDC], F32))
        msk = E(nc.sbuf_tensor("msk", [128, 128], F32))
        onc = E(nc.sbuf_tensor("onc", [128, 1], BF16))
        onr = E(nc.sbuf_tensor("onr", [1, 128], F32))
        idn = E(nc.sbuf_tensor("idn", [128, 128], BF16))
        q2 = E(nc.sbuf_tensor("q2", [128, HALF], BF16))
        vt2 = E(nc.sbuf_tensor("vt2", [128, 2 * HALF], BF16))
        mskball = E(nc.sbuf_tensor("mskball", [128, NSUB * NH * 128], BF16))

        if fused:
            sqq = E(nc.sbuf_tensor("sqq", [88, 6 * 256], F32))
            dqs = E(nc.sbuf_tensor("dqs", [88, 256], BF16))
            cents = E(nc.sbuf_tensor("cents", [88, 256], BF16))
            smalls = E(nc.sbuf_tensor("smalls", [88, 16], F32))
            s11 = E(nc.sbuf_tensor("s11", [1, 8], F32))
            s11b = E(nc.sbuf_tensor("s11b", [1, 2], BF16))
            r88 = E(nc.sbuf_tensor("r88", [1, 88], F32))
            c88 = E(nc.sbuf_tensor("c88", [88, 1], F32))
            m12row = E(nc.sbuf_tensor("m12row", [1, 4], F32))
            m3row = E(nc.sbuf_tensor("m3row", [1, 4], F32))
            nmid = E(nc.sbuf_tensor("nmid", [128, 15], F32))
            rcr2 = E(nc.sbuf_tensor("rcr2", [128, 176], F32))
        PS1 = E(nc.psum_tensor("PS1", [128, 1024], F32))
        PS2 = E(nc.psum_tensor("PS2", [128, 1024], F32))
        PS3 = E(nc.psum_tensor("PS3", [128, 1024], F32))
        PT = E(nc.psum_tensor("PT", [128, 2048], BF16))

        b = B(nc)
        semnames = ("dX", "dC", "dP", "sBR", "sWA0", "sWA1", "sWB0", "sWB1",
                    "sW30", "sW31", "sWF0", "sWF1", "sCA", "sCB", "eT", "eV", "eA",
                    "sQ0", "sQ1", "sR0", "sR1", "dQW", "dO", "ccS", "eP",
                    "sWA2", "sWB2")
        sem = {n: Sem(E(nc.semaphore(n)), n) for n in semnames}
        dX, dC, dP, sBR = sem["dX"], sem["dC"], sem["dP"], sem["sBR"]
        sWA = (sem["sWA0"], sem["sWA1"])
        sWB = (sem["sWB0"], sem["sWB1"])
        sW3 = (sem["sW30"], sem["sW31"])
        sWF = (sem["sWF0"], sem["sWF1"])
        sCC = (sem["sCA"], sem["sCB"])
        eT, eV, eA = sem["eT"], sem["eV"], sem["eA"]
        sQ = (sem["sQ0"], sem["sQ1"])
        sR = (sem["sR0"], sem["sR1"])
        dQW, dO, ccS = sem["dQW"], sem["dO"], sem["ccS"]

        def dma(queue, out, in_, s, waits=()):
            return b.emit(queue, lambda e, o=out, i=in_: e.dma_start(out=o, in_=i),
                          waits=waits, inc=s, k=16)

        def mm(out, lhsT, rhs, start, stop, waits=(), inc=None):
            return b.emit("tensor",
                          lambda e, o=out, l=lhsT, r=rhs, st=start, sp=stop:
                          e.matmul(o, l, r, start=st, stop=sp),
                          waits=waits, inc=inc)

        def vop(fn, waits=()):
            return b.emit("vector", fn, waits=waits, inc=eV)

        def aop(fn, waits=()):
            return b.emit("scalar", fn, waits=waits, inc=eA)

        def xh_(dc):
            return xh[:, dc * HALF:(dc + 1) * HALF]

        def xbf_(dc):
            return xbf[:, dc * HALF:(dc + 1) * HALF]

        def xln_(dc):
            return xln[:, dc * HALF:(dc + 1) * HALF]

        def ar_(fc):
            return arena[:, fc * HALF:(fc + 1) * HALF]

        def slab_tile(buf, off, idx, width):
            return buf[:, off:off + NDC * width].rearrange(
                "p (c f) -> p c f", f=width)[:, idx, :]

        dma("sync", msk[:], cmask, dC)
        dma("sync", onc[:], cones_col, dC)
        dma("sync", onr[:], cones_row, dC)
        dma("sync", qbs[:], qb, dC)
        dma("sync", kbs[:], kb, dC)
        dma("sync", vbs[:], vb, dC)
        dma("sync", obs[:], ob, dC)
        dma("sync", idn[:], cident, dC)
        t_wbres = dma("gpsimd", wbres[:].rearrange("p (c n) -> p c n", n=NH),
                      wbt.rearrange("(c p) n -> p c n", p=128), sBR)
        t_bbrow = dma("sync", sc1[0:1, 0:NH], bb, dC)
        t_consts = t_bbrow
        for pi_, wsrc_ in enumerate((wq, wk, wog, wv, wo)):
            dma("gpsimd", wcache[DM * pi_:DM * (pi_ + 1), :], wsrc_,
                sem["sWF0"])
        cbar = [(dC, dC.n), (sBR, sBR.n)]
        for _e in ENGINES:
            b.emit(_e, lambda e: e.nop(), waits=cbar)
        t = mm(PS3[:, 0:NH], onr[:], sc1[0:1, 0:NH], True, True, inc=eT)
        t_bbs = vop(lambda e: e.tensor_copy(bbs[:], PS3[:, 0:NH]), waits=[(eT, t)])

        if fused:
            dma("sync", r88[:], crow88, dC)
            dma("sync", c88[:], cone88, dC)
            dma("sync", m12row[:], cm12, dC)
            dma("sync", m3row[:], cm3, dC)
            dma("sync", nmid[:], cnmid, dC)
            qt_ = (lambda: arena[0:88, 0:16384],
                   lambda: arena[0:88, 0:16384])
            scalfs = (wf32[0:88, 0:2048], wf32[0:88, 2048:4096])

            def q_fullpass(src, scaldst, t0=None, t0p=None):
                tprev = [t0, t0p]
                tv = tvp = None
                for t in range(8):
                    par = t % 2
                    w = ([(eV, tprev[par])] if tprev[par] is not None
                         else [(dC, dC.n)])
                    tl = dma("sync" if par == 0 else "scalar", qt_[par](),
                             src.rearrange("a bx -> (a bx)")
                             .rearrange("(t p f) -> t p f", t=8, p=88)[t, :, :],
                             sQ[par], waits=w)
                    fn = (lambda e, par=par, t=t: e.tensor_reduce(
                        scaldst[:, t * 256:(t + 1) * 256],
                        qt_[par]().rearrange("p (c z) -> p c z", z=64),
                        axis=mybir.AxisListType.X, op=ALU.max,
                        apply_absolute_value=True))
                    tv = vop(fn, waits=[(sQ[par], tl)])
                    tprev[par] = tv
                return tv, tvp

            def q_shard_scalers(rsrc, ntiles, dst, t0=None):
                tprev = t0
                tv = None
                for t in range(ntiles):
                    w = [(eV, tprev)] if tprev is not None else [(dC, dC.n)]
                    tl = dma("sync", qt_[0](),
                             rsrc.rearrange("a bx -> (a bx)")
                             .rearrange("(t p f) -> t p f", t=ntiles, p=88)[t, :, :],
                             sQ[0], waits=w)
                    tv = vop(lambda e, t=t: e.tensor_reduce(
                        dst[:, t * 256:(t + 1) * 256],
                        qt_[0]().rearrange("p (c z) -> p c z", z=64),
                        axis=mybir.AxisListType.X, op=ALU.max,
                        apply_absolute_value=True), waits=[(sQ[0], tl)])
                    tprev = tv
                return tv

            scal12k = wf32[0:88, 0:512]
            scal3k = wf32[0:88, 512:768]
            ts12 = q_shard_scalers(rw12, 2, scal12k)
            ts3 = q_shard_scalers(rw3, 1, scal3k, t0=ts12)

            rs12 = smalls[:, 12:13]
            rs3 = smalls[:, 13:14]
            t1 = vop(lambda e: e.tensor_reduce(rs12, scal12k,
                                               axis=mybir.AxisListType.X,
                                               op=ALU.add), waits=[(eV, ts12)])
            t1b = vop(lambda e: e.tensor_reduce(rs3, scal3k,
                                                axis=mybir.AxisListType.X,
                                                op=ALU.add), waits=[(eV, ts3)])
            t2 = mm(PS3[0:1, 0:1], c88[:], rs12, True, True,
                    waits=[(eV, t1)], inc=eT)
            t2b = mm(PS3[0:1, 1:2], c88[:], rs3, True, True,
                     waits=[(eV, t1b)], inc=eT)
            t3 = vop(lambda e: e.tensor_copy(s11[0:1, 0:1], PS3[0:1, 0:1]),
                     waits=[(eT, t2)])
            t3b = vop(lambda e: e.tensor_copy(s11[0:1, 1:2], PS3[0:1, 1:2]),
                      waits=[(eT, t2b)])
            t4 = vop(lambda e: e.tensor_scalar(s11[0:1, 4:8], m12row[:],
                                               s11[0:1, 0:1], None, op0=ALU.mult),
                     waits=[(eV, t3)])
            t5 = vop(lambda e: e.scalar_tensor_tensor(s11[0:1, 4:8], m3row[:],
                                                      s11[0:1, 1:2],
                                                      s11[0:1, 4:8],
                                                      op0=ALU.mult, op1=ALU.add),
                     waits=[(eV, t4), (eV, t3b)])
            tcw = dma("gpsimd", ccin, s11[0:1, 4:8], dQW, waits=[(eV, t5)])
            tcc = b.emit("gpsimd",
                         lambda e: e.collective_compute(
                             "AllReduce", ALU.add,
                             replica_groups=[list(range(8))],
                             ins=[ccin.opt()], outs=[ccout.opt()]),
                         waits=[(dQW, tcw)], inc=ccS, k=1)
            tcr = dma("gpsimd", s11[0:1, 4:8], ccout, dQW, waits=[(ccS, tcc)])

            def q_mean(mrow, mcol_off, s11_off):
                t6 = vop(lambda e: e.tensor_tensor(s11b[0:1, 0:4].bitcast(F32)
                                                   [0:1, 0:2][0:1, 0:2]
                                                   if False else
                                                   smalls[0:1, 8:12],
                                                   s11[0:1, 4:8], mrow[:],
                                                   op=ALU.mult),
                         waits=[(dQW, tcr)])
                t7 = vop(lambda e: e.tensor_reduce(s11[0:1, s11_off:s11_off + 1],
                                                   smalls[0:1, 8:12],
                                                   axis=mybir.AxisListType.X,
                                                   op=ALU.add), waits=[(eV, t6)])
                t8 = vop(lambda e: e.tensor_scalar(s11[0:1, s11_off:s11_off + 1],
                                                   s11[0:1, s11_off:s11_off + 1],
                                                   1.0 / 180224.0, None,
                                                   op0=ALU.mult), waits=[(eV, t7)])
                t9 = vop(lambda e: e.tensor_copy(s11b[0:1, s11_off:s11_off + 1],
                                                 s11[0:1, s11_off:s11_off + 1]),
                         waits=[(eV, t8)])
                t10 = vop(lambda e: e.tensor_copy(s11[0:1, 2 + s11_off:3 + s11_off],
                                                  s11b[0:1, s11_off:s11_off + 1]),
                          waits=[(eV, t9)])
                t11 = mm(PS3[0:88, 1:2], r88[:], s11[0:1, 2 + s11_off:3 + s11_off],
                         True, True, waits=[(eV, t10)], inc=eT)
                t12 = vop(lambda e: e.tensor_copy(smalls[:, mcol_off:mcol_off + 1],
                                                  PS3[0:88, 1:2]), waits=[(eT, t11)])
                return t12

            tm12 = q_mean(m12row, 0, 0)
            tm3 = q_mean(m3row, 2, 1)
            tf3 = tm3

            def q_scaler_stage(scal_res, ntiles, mcol_off, tmean, dq_d, rc_d,
                               tprevload):
                mcol = smalls[:, mcol_off:mcol_off + 1]
                S = lambda a: sqq[:, a * 256:(a + 1) * 256]
                yv, tv, Fv, bb2, rcs = S(1), S(2), S(3), S(4), S(5)
                m1 = smalls[:, 8:9]
                qf = smalls[:, 9:10]
                rqf = smalls[:, 10:11]
                tlast = tprevload
                for st in range(ntiles):
                    scal = scal_res[:, st * 256:(st + 1) * 256]
                    t1 = tlast
                    t2 = vop(lambda e, scal=scal: e.tensor_scalar(
                        cents[:], scal, mcol, None, op0=ALU.subtract),
                        waits=[(eV, tmean), (dQW, dQW.n)] +
                              ([(eV, tlast)] if tlast is not None else []))
                    t3 = vop(lambda e: e.tensor_reduce(m1, cents[:],
                                                       axis=mybir.AxisListType.X,
                                                       op=ALU.max,
                                                       apply_absolute_value=True),
                             waits=[(eV, t2)])
                    t4 = vop(lambda e: e.reciprocal(qf, m1), waits=[(eV, t3)])
                    t5 = vop(lambda e: e.tensor_scalar(qf, qf, 128.0, None,
                                                       op0=ALU.mult),
                             waits=[(eV, t4)])
                    t6 = vop(lambda e: e.tensor_scalar(yv, cents[:], qf, None,
                                                       op0=ALU.mult),
                             waits=[(eV, t5)])
                    t7 = vop(lambda e: e.tensor_scalar(tv, yv, 128.5, None,
                                                       op0=ALU.add),
                             waits=[(eV, t6)])
                    t8 = vop(lambda e: e.memset(Fv, 0.0), waits=[(eV, t7)])
                    tk = t8
                    for k in range(8, -1, -1):
                        ka = vop(lambda e, k=k: e.tensor_scalar(
                            bb2, tv, float(2 ** k), None, op0=ALU.is_ge),
                            waits=[(eV, tk)])
                        kb = vop(lambda e, k=k: e.scalar_tensor_tensor(
                            tv, bb2, float(-(2 ** k)), tv, op0=ALU.mult,
                            op1=ALU.add), waits=[(eV, ka)])
                        tk = vop(lambda e, k=k: e.scalar_tensor_tensor(
                            Fv, bb2, float(2 ** k), Fv, op0=ALU.mult,
                            op1=ALU.add), waits=[(eV, kb)])
                    t9 = vop(lambda e: e.tensor_scalar(Fv, Fv, -128.0, 127.0,
                                                       op0=ALU.add, op1=ALU.min),
                             waits=[(eV, tk)])
                    t10 = vop(lambda e: e.tensor_scalar(Fv, Fv, -128.0, None,
                                                        op0=ALU.max),
                              waits=[(eV, t9)])
                    t11 = vop(lambda e: e.reciprocal(rqf, qf), waits=[(eV, t10)])
                    t12 = vop(lambda e: e.tensor_scalar(dqs[:], Fv, rqf, None,
                                                        op0=ALU.mult),
                              waits=[(eV, t11)])
                    t13 = vop(lambda e: e.tensor_scalar(dqs[:], dqs[:], mcol, None,
                                                        op0=ALU.add),
                              waits=[(eV, t12)])
                    t14 = vop(lambda e, scal=scal: e.reciprocal(rcs, scal),
                              waits=[(eV, t13)])
                    dma("gpsimd", dq_d.rearrange("a bx -> (a bx)")
                        [st * 22528:(st + 1) * 22528]
                        .rearrange("(p c) -> p c", p=88), dqs[:], dQW,
                        waits=[(eV, t13)])
                    dma("gpsimd", rc_d.rearrange("a bx -> (a bx)")
                        [st * 22528:(st + 1) * 22528]
                        .rearrange("(p c) -> p c", p=88), rcs, dQW,
                        waits=[(eV, t14)])
                    tlast = t14
                return tlast

            tq12 = q_scaler_stage(scal12k, 2, 0, tm12, dq12, rc12, tm3)
            tq3 = q_scaler_stage(scal3k, 1, 2, tm3, dq3, rc3, tq12)
            ndqw = dQW.n

            eP = sem["eP"]

            def q_row_stage(rsrc, nrt, width, dq_d, rc_d, agdst, t_prev,
                            s_prev=None, tp_prev=None, agcb=None):
                nbl = width // 64
                ndc = width // 128
                hw = width // 2
                hb = nbl // 2
                wrow_ = lambda par: wb3[:, par * width:(par + 1) * width]
                rbf_ = xbf[:, 0:width]
                accf_ = arena[:, 0:width]
                tmpf_ = xln[:, 0:width]
                wdq_ = mskball[:, 0:width]
                slab_ = arena[:, width:2 * width]
                dqr_ = lambda par: q2[:, par * nbl:(par + 1) * nbl]
                rcr_ = lambda par: rcr2[:, par * nbl:(par + 1) * nbl]
                tev_bank = [t_prev, t_prev]
                tstores = [s_prev] if s_prev else []
                tlw = [t_prev, t_prev]
                tlwp = [tp_prev, tp_prev]

                C0 = float(NF4[0]) + sum(GAP) * 0.5
                sgn_ok = False

                def halfops(eng, inc_s, c0, c1, b0, b1, par, tok, w0=()):
                    emitq = (lambda fn, waits: b.emit(eng, fn, waits=waits,
                                                      inc=inc_s))
                    nb = b1 - b0
                    t1 = emitq(lambda e, par=par: e.tensor_tensor(
                        rbf_[:, c0:c1].rearrange("p (c z) -> p c z", z=64),
                        wrow_(par)[:, c0:c1].rearrange("p (c z) -> p c z", z=64),
                        rcr_(par)[:, b0:b1].unsqueeze(2)
                        .broadcast_to([128, nb, 64]), op=ALU.mult),
                        [(sR[par], tok)] + list(w0))
                    if sgn_ok:
                        tmpb = (xln[:, width:2 * width],
                                xln[:, 2 * width:3 * width])
                        t2 = emitq(lambda e: e.memset(accf_[:, c0:c1], C0),
                                   [(inc_s, t1)])
                        tk = t2
                        tstt = {}
                        tsg = {}
                        for j in range(15):
                            wsg = [(eV, t1)]
                            if j >= 2:
                                wsg.append((eV, tstt[j - 2]))
                            tsg[j] = aop(lambda e, j=j: e.activation(
                                tmpb[j % 2][:, c0:c1],
                                rbf_[:, c0:c1], AF.Sign,
                                bias=nmid[:, j:j + 1]),
                                waits=wsg)
                            tk = emitq(lambda e, j=j: e.scalar_tensor_tensor(
                                accf_[:, c0:c1],
                                tmpb[j % 2][:, c0:c1], GAP[j] * 0.5,
                                accf_[:, c0:c1], op0=ALU.mult, op1=ALU.add),
                                [(eA, tsg[j]), (inc_s, tk)])
                            tstt[j] = tk
                    else:
                        t2 = emitq(lambda e: e.memset(accf_[:, c0:c1],
                                                      float(NF4[0])),
                                   [(inc_s, t1)])
                        tk = t2
                        for j in range(15):
                            ja = emitq(lambda e, j=j: e.tensor_scalar(
                                tmpf_[:, c0:c1], rbf_[:, c0:c1], MID[j], None,
                                op0=ALU.is_gt), [(inc_s, tk)])
                            tk = emitq(lambda e, j=j: e.scalar_tensor_tensor(
                                accf_[:, c0:c1], tmpf_[:, c0:c1], GAP[j],
                                accf_[:, c0:c1], op0=ALU.mult, op1=ALU.add),
                                [(inc_s, ja)])
                    t3 = emitq(lambda e, par=par: e.tensor_tensor(
                        wdq_[:, c0:c1].rearrange("p (c z) -> p c z", z=64),
                        accf_[:, c0:c1].rearrange("p (c z) -> p c z", z=64),
                        dqr_(par)[:, b0:b1].unsqueeze(2)
                        .broadcast_to([128, nb, 64]), op=ALU.mult), [(inc_s, tk)])
                    return t3

                for t in range(nrt):
                    par = t % 2
                    w = [(dQW, ndqw)]
                    if t == 0:
                        w.append((dO, dO.n))
                    if tlw[par] is not None:
                        w.append((eV, tlw[par]))
                    if tlwp[par] is not None:
                        w.append((eP, tlwp[par]))
                    dma("sync", wrow_(par), rsrc[128 * t:128 * (t + 1), :],
                        sR[par], waits=w)
                    dma("sync", dqr_(par), dq_d[128 * t:128 * (t + 1), :], sR[par])
                    dma("sync", rcr_(par), rc_d[128 * t:128 * (t + 1), :], sR[par])
                    tok = sR[par].n
                    t3v = halfops("vector", eV, 0, width, 0, nbl, par, tok,
                                  w0=([(dO, dO.n)] if t == 0 else ()))
                    t3p = None
                    tlw[par] = t3v
                    tlwp[par] = None
                    tev = None
                    first_evac = True
                    for dc in range(ndc):
                        bank = dc % 2
                        slot = bank * 1024
                        w2 = [(eV, t3v)]
                        if tev_bank[bank] is not None:
                            w2.append((eV, tev_bank[bank]))
                        tt_ = b.emit("tensor",
                                     lambda e, dc=dc, slot=slot: e.transpose(
                                         PT[:, slot:slot + 128],
                                         wdq_[:, dc * 128:(dc + 1) * 128], idn[:]),
                                     waits=w2, inc=eT)
                        w3_ = [(eT, tt_)]
                        if first_evac and tstores:
                            w3_.append((dO, tstores[-1]))
                            first_evac = False
                        tev = vop(lambda e, dc=dc, slot=slot: e.tensor_copy(
                            slab_[:, dc * 128:(dc + 1) * 128],
                            PT[:, slot:slot + 128]), waits=w3_)
                        tev_bank[bank] = tev
                    ts_ = dma("gpsimd", agdst[128 * t:128 * (t + 1), :],
                              slab_[:, 0:width], dO, waits=[(eV, tev)])
                    tstores.append(ts_)
                    if agcb is not None:
                        agcb(t, ts_)
                return tstores, tlw, tlwp

            def ag1cb(t, ts_):
                b.emit("gpsimd",
                       lambda e, t=t: e.collective_compute(
                           "AllGather", ALU.bypass,
                           replica_groups=[list(range(8))],
                           ins=[ag1s[128 * t:128 * (t + 1), :].opt()],
                           outs=[wag12[1024 * t:1024 * (t + 1), :].opt()]),
                       waits=[(dO, ts_)], inc=ccS, k=1)

            def ag2cb(t, ts_):
                b.emit("gpsimd",
                       lambda e, t=t: e.collective_compute(
                           "AllGather", ALU.bypass,
                           replica_groups=[list(range(8))],
                           ins=[ag2s[128 * t:128 * (t + 1), :].opt()],
                           outs=[wag3[1024 * t:1024 * (t + 1), :].opt()]),
                       waits=[(dO, ts_)], inc=ccS, k=1)

            st1, tl1, tp1 = q_row_stage(rw12, W12_FC, DM, dq12, rc12, ag1s, tq3,
                                        agcb=ag1cb)
            q_row_stage(rw3, 2, DFF, dq3, rc3, ag2s, eV.n, s_prev=st1[-1],
                        tp_prev=max(t for t in tp1 if t is not None)
                        if any(tp1) else None, agcb=ag2cb)
            qbar = [(ss_, ss_.n) for ss_ in sem.values()]
            for _e in ENGINES:
                b.emit(_e, lambda e: e.nop(), waits=qbar)



        def halfprog(h):
            if h > 0:
                bar = [(ss, ss.n) for ss in sem.values()]
                for eng in ("sync", "tensor", "vector", "scalar", "gpsimd"):
                    b.emit(eng, lambda e: e.nop(), waits=bar)
            for dc in range(NDC):
                dma("sync", xh_(dc), x_in[ts_(dc), h * HALF:(h + 1) * HALF], dX)
            xtok = [(dX, dX.n)] * NDC

            def layernorm(xtok, extra_xbf=()):
                tb = []
                for dc in range(NDC):
                    w = [xtok[dc]] + list(extra_xbf)
                    tb.append(vop(lambda e, dc=dc: e.tensor_copy(xbf_(dc), xh_(dc)),
                                  waits=w))
                if "stats" in LN_SKIP:
                    tstat = b.emit("tensor", lambda e: e.nop(),
                                   waits=[(eV, tb[-1])], inc=eT)
                else:
                    tsq_consumed = None
                    for dc in range(NDC):
                        w = [(eV, tb[dc])]
                        if tsq_consumed is not None:
                            w.append((eT, tsq_consumed))
                        tq = vop(lambda e, dc=dc: e.tensor_tensor(
                            lntmp[:], xbf_(dc), xbf_(dc), op=ALU.mult), waits=w)
                        mm(PS1[0:1, 0:512], onc[:], xbf_(dc), dc == 0, dc == NDC - 1,
                           waits=[(eV, tb[dc])])
                        tsq_consumed = mm(PS2[0:1, 0:512], onc[:], lntmp[:],
                                          dc == 0, dc == NDC - 1,
                                          waits=[(eV, tq)], inc=eT)
                    tstat = tsq_consumed
                S = lambda a: sc1[0:1, a * 512:(a + 1) * 512]
                mean, ex2, m2, ve = S(0), S(1), S(2), S(3)
                r, t_, mrs = S(2), S(1), S(1)
                if "scal" in LN_SKIP:
                    tz0 = vop(lambda e: e.memset(bcB[:], 1.0), waits=[(eT, tstat)])
                    tBc = tCc = vop(lambda e: e.memset(bcC[:], 0.0),
                                    waits=[(eV, tz0)])
                    txl = []
                    if "apply" in LN_SKIP:
                        return tb
                    prev = tCc
                    for dc in range(NDC):
                        ta = vop(lambda e, dc=dc: e.tensor_tensor(
                            lntmp[:], xbf_(dc), bcB[:], op=ALU.mult),
                            waits=[(eV, tBc), (eV, prev)])
                        prev = vop(lambda e, dc=dc: e.tensor_tensor(
                            xln_(dc), lntmp[:], bcC[:], op=ALU.subtract),
                            waits=[(eV, ta)])
                        txl.append(prev)
                    return txl
                t1 = vop(lambda e: e.tensor_copy(mean, PS1[0:1, 0:512]),
                         waits=[(eT, tstat)])
                t2 = vop(lambda e: e.tensor_copy(ex2, PS2[0:1, 0:512]),
                         waits=[(eT, tstat)])
                t3 = vop(lambda e: e.tensor_scalar(mean, mean, 1.0 / DM, None,
                                                   op0=ALU.mult), waits=[(eV, t1)])
                t4 = vop(lambda e: e.tensor_scalar(ex2, ex2, 1.0 / DM, None,
                                                   op0=ALU.mult), waits=[(eV, t2)])
                t5 = vop(lambda e: e.tensor_tensor(m2, mean, mean, op=ALU.mult),
                         waits=[(eV, t3)])
                t6 = vop(lambda e: e.scalar_tensor_tensor(ve, ex2, EPS_LN, m2,
                                                          op0=ALU.add,
                                                          op1=ALU.subtract),
                         waits=[(eV, t4), (eV, t5)])
                if "act" in LN_SKIP:
                    t8v = vop(lambda e: e.memset(r, 1.0), waits=[(eV, t6)])
                    last = (eV, t8v)
                else:
                    t7 = aop(lambda e: e.activation(t_, ve, AF.Ln), waits=[(eV, t6)])
                    t8 = aop(lambda e: e.activation(r, t_, AF.Exp, scale=-0.5),
                             waits=[(eA, t7)])
                    last = (eA, t8)
                for _ in range(0 if "newton" in LN_SKIP else 2):
                    ta = vop(lambda e: e.tensor_tensor(t_, r, r, op=ALU.mult),
                             waits=[last])
                    tb2 = vop(lambda e: e.tensor_tensor(t_, t_, ve, op=ALU.mult),
                              waits=[(eV, ta)])
                    tc = vop(lambda e: e.tensor_scalar(t_, t_, -0.5, 1.5,
                                                       op0=ALU.mult, op1=ALU.add),
                             waits=[(eV, tb2)])
                    td = vop(lambda e: e.tensor_tensor(r, r, t_, op=ALU.mult),
                             waits=[(eV, tc)])
                    last = (eV, td)
                tm = vop(lambda e: e.tensor_tensor(mrs, mean, r, op=ALU.mult),
                         waits=[last])
                tB = mm(PS3[:, 0:512], onr[:], r, True, True,
                        waits=[(eV, tm)], inc=eT)
                tC = mm(PS3[:, 512:1024], onr[:], mrs, True, True,
                        waits=[(eV, tm)], inc=eT)
                tBc = vop(lambda e: e.tensor_copy(bcB[:], PS3[:, 0:512]),
                          waits=[(eT, tB)])
                tCc = vop(lambda e: e.tensor_copy(bcC[:], PS3[:, 512:1024]),
                          waits=[(eT, tC)])
                txl = []
                prev = tCc
                for dc in range(NDC):
                    ta = vop(lambda e, dc=dc: e.tensor_tensor(lntmp[:], xbf_(dc),
                                                              bcB[:], op=ALU.mult),
                             waits=[(eV, tBc), (eV, prev)])
                    prev = vop(lambda e, dc=dc: e.tensor_tensor(
                        xln_(dc), lntmp[:], bcC[:], op=ALU.subtract),
                        waits=[(eV, ta)])
                    txl.append(prev)
                return txl

            def ffn(txl, extra_arena=(), extra_wb3=(), extra_wAB=()):
                tslabA = [None, None]
                tsilu = [None] * NFC
                tmmA = [None] * NFC
                for fc in range(NFC):
                    r_, lfc = fc // W12_FC, fc % W12_FC
                    par = fc % 2
                    w = ([(eT, tmmA[fc - 2])] if fc >= 2 else
                         list(extra_wAB) + [(sCC[0], sCC[0].n)])
                    tslabA[par] = dma("sync", wbA[:, par * DM:(par + 1) * DM],
                                      wag12[(lfc * 8 + r_) * 128:
                                            (lfc * 8 + r_) * 128 + 128, :],
                                      sWA[par], waits=w)
                    bank = PS1[:, par * 512:(par + 1) * 512]
                    for dc in range(NDC):
                        wts = [(sWA[par], tslabA[par]), (eV, txl[dc])]
                        if dc == 0 and fc >= 2 and tsilu[fc - 2] is not None:
                            wts.append((eA, tsilu[fc - 2]))
                        tk = mm(bank, slab_tile(wbA, par * DM, dc, 128),
                                xln_(dc), dc == 0, dc == NDC - 1, waits=wts,
                                inc=eT if dc == NDC - 1 else None)
                    tmmA[fc] = tk
                    wts = [(eT, tmmA[fc])] + list(extra_arena)
                    tsilu[fc] = aop(lambda e, fc=fc, bank=bank: e.activation(
                        ar_(fc), bank, AF.Silu), waits=wts)
                tmmB = [None] * NFC
                th = [None] * NFC
                for fc in range(NFC):
                    r_, lfc = fc // W12_FC, fc % W12_FC
                    par = fc % 2
                    w = ([(eT, tmmB[fc - 2])] if fc >= 2 else
                         list(extra_wAB) + [(sCC[1], sCC[1].n)])
                    tsb = dma("sync", wbB[:, par * DM:(par + 1) * DM],
                              wag12[(lfc * 8 + r_ + 4) * 128:
                                    (lfc * 8 + r_ + 4) * 128 + 128, :],
                              sWB[par], waits=w)
                    bank = PS2[:, par * 512:(par + 1) * 512]
                    for dc in range(NDC):
                        wts = [(sWB[par], tsb), (eV, txl[dc])]
                        if dc == 0 and fc >= 2 and th[fc - 2] is not None:
                            wts.append((eV, th[fc - 2]))
                        tk = mm(bank, slab_tile(wbB, par * DM, dc, 128),
                                xln_(dc), dc == 0, dc == NDC - 1, waits=wts,
                                inc=eT if dc == NDC - 1 else None)
                    tmmB[fc] = tk
                    th[fc] = vop(lambda e, fc=fc, bank=bank: e.tensor_tensor(
                        ar_(fc), ar_(fc), bank, op=ALU.mult),
                        waits=[(eT, tmmB[fc]), (eA, tsilu[fc])])
                tmmC = [None] * NDC
                txn = [None] * NDC
                for dc in range(NDC):
                    r_, dcl = dc // 2, dc % 2
                    par = dc % 2
                    w = ([(eT, tmmC[dc - 2])] if dc >= 2 else []) + list(extra_wb3)
                    ts3 = dma("sync", wb3[:, par * DFF:(par + 1) * DFF],
                              wag3[(dcl * 8 + r_) * 128:
                                   (dcl * 8 + r_) * 128 + 128, :],
                              sW3[par], waits=w)
                    bank = PS3[:, par * 512:(par + 1) * 512]
                    for fc in range(NFC):
                        wts = [(sW3[par], ts3), (eV, th[fc])]
                        if fc == 0 and dc >= 2 and txn[dc - 2] is not None:
                            wts.append((eV, txn[dc - 2]))
                        tk = mm(bank,
                                wb3[:, par * DFF:(par + 1) * DFF].rearrange(
                                    "p (c f) -> p c f", f=128)[:, fc, :],
                                ar_(fc), fc == 0, fc == NFC - 1, waits=wts,
                                inc=eT if fc == NFC - 1 else None)
                    tmmC[dc] = tk
                    txn[dc] = vop(lambda e, dc=dc, bank=bank: e.scalar_tensor_tensor(
                        xh_(dc), bank, 0.5, xh_(dc), op0=ALU.mult, op1=ALU.add),
                        waits=[(eT, tmmC[dc])])
                return txn, tmmC[NDC - 1]

            def attention(txl, extra_arena, t_wAB_free, sub="all"):
                if sub == "ln2":
                    tdump = []
                    for dc in range(NDC):
                        tdump.append(vop(lambda e, dc=dc: e.tensor_copy(
                            xh_(dc), xln_(dc)),
                            waits=[(eV, eV.n), (eA, eA.n), (eT, eT.n)]))
                    return tdump, eT.n, eT.n
                qn_ = lambda oc: arena[:, 0 * 8192 + oc * HALF: 0 * 8192 + (oc + 1) * HALF]
                gate_ = lambda oc: arena[:, 8192 + oc * HALF: 8192 + (oc + 1) * HALF]
                kn_ = lambda oc: xbf_(oc)
                gv_ = lambda sub: wb3[:, sub * DM:(sub + 1) * DM]

                tbeta = []
                tprev = None
                for bsub in range(min(NSUB, BETA_SUBS)):
                    if not NOMM:
                        for dc in range(NDC):
                            w = [(eV, txl[dc])]
                            if dc == 0 and tprev is not None:
                                w.append((eV, tprev))
                            tz = mm(PS3[:, 0:NH],
                                    xln[:, dc * HALF + bsub * 128:dc * HALF + (bsub + 1) * 128],
                                    wbres[:].rearrange("p (c n) -> p c n", n=NH)[:, dc, :],
                                    dc == 0, dc == NDC - 1,
                                    waits=w, inc=eT if dc == NDC - 1 else None)
                    bsl = beta[:, bsub * NH:(bsub + 1) * NH]
                    if NOMM:
                        t0 = vop(lambda e, bsl=bsl: e.memset(bsl, 0.25),
                                 waits=([(eV, tprev)] if tprev else []))
                    else:
                        t0 = vop(lambda e, bsl=bsl: e.tensor_tensor(
                            bsl, PS3[:, 0:NH], bbs[:], op=ALU.add),
                            waits=[(eT, tz), (eV, t_bbs)])
                    if NOACT:
                        tprev = t0
                        tbeta.append(tprev)
                        continue
                    ta1 = aop(lambda e, bsl=bsl: e.activation(bscr[:], bsl, AF.Abs),
                              waits=[(eV, t0)])
                    ta2 = aop(lambda e: e.activation(bscr2[:], bscr[:], AF.Exp,
                                                     scale=-1.0), waits=[(eA, ta1)])
                    ta3 = aop(lambda e: e.activation(bscr[:], bscr2[:], AF.Ln,
                                                     bias=1.0), waits=[(eA, ta2)])
                    tprev = vop(lambda e, bsl=bsl: e.scalar_tensor_tensor(
                        bsl, bsl, 0.0, bscr[:], op0=ALU.max, op1=ALU.add),
                        waits=[(eA, ta3)])
                    tbeta.append(tprev)
                t_beta_all = tprev

                if sub == "beta":
                    tdump = []
                    for dc in range(NDC):
                        tdump.append(vop(lambda e, dc=dc: e.tensor_copy(
                            xh_(dc), xln_(dc)),
                            waits=[(eV, eV.n), (eA, eA.n), (eT, eT.n)]))
                    return tdump, eT.n, eT.n

                passes = [("q", wq, qbs, qn_), ("k", wk, kbs, kn_),
                          ("og", wog, None, gate_), ("v", wv, vbs, None)]
                if sub == "qk":
                    passes = passes[:2]
                SLAB4 = (lambda: wbA[:, 0:DM], lambda: wbA[:, DM:2 * DM],
                         lambda: wbB[:, 0:DM], lambda: wbB[:, DM:2 * DM])
                SSEM4 = (sWA[0], sWA[1], sWB[0], sWB[1])
                BANK4 = (lambda: PS2[:, 0:512], lambda: PS2[:, 512:1024],
                         lambda: PS1[:, 0:512], lambda: PS1[:, 512:1024])
                tslab = [None] * 4
                tconv = [None] * 4
                tgrp = [None] * 4
                tevac = [None] * 4
                normq = {}
                last_tss = [None]
                last_tn = [None]
                tgv = [None] * (NSUB * NH)
                tnorm_done = {}
                for pi, (pname, wsrc, bias, dst) in enumerate(passes):
                    for oc in range(NDC):
                        par = oc % 4
                        wpar = oc % 2
                        gi = pi * NDC + oc
                        w = [(eT, tgrp[par]) if tgrp[par] is not None
                             else (eT, t_wAB_free),
                             (sem["sWF0"], 80)]
                        tl = dma("scalar" if par < 2 else "sync", SLAB4[par](),
                                 wcache[DM * pi + 128 * oc:
                                        DM * pi + 128 * (oc + 1), :],
                                 SSEM4[par], waits=w)
                        slab = SLAB4[par]()
                        tslab_tok = (SSEM4[par], tl)
                        bank = BANK4[par]()
                        for kc in range(NDC):
                            wts = [tslab_tok, (eV, txl[kc])]
                            if kc == 0 and tevac[par] is not None:
                                wts.append(tevac[par])
                            tg = mm(bank, slab.rearrange("p (c f) -> p c f",
                                                         f=128)[:, kc, :],
                                    xln_(kc), kc == 0, kc == NDC - 1, waits=wts,
                                    inc=eT if kc == NDC - 1 else None)
                        tgrp[par] = tg
                        if pname in ("q", "k"):
                            dtile = dst(oc)
                            wts = [(eT, tg)]
                            if pname == "q":
                                wts += list(extra_arena)
                            te = aop(lambda e, dtile=dtile, bank=bank, bias=bias,
                                     oc=oc: e.activation(
                                dtile, bank, AF.Identity, bias=bias[:, oc:oc + 1]),
                                waits=wts)

                            def norm_stage1(st):
                                tss = mm(PS3[0:1, 512:1024], onc[:], q2[:],
                                         True, True,
                                         waits=[(eV, st["tq2"])], inc=eT)
                                st["tss"] = tss
                                S = lambda a: sc1[0:1, a * 512:(a + 1) * 512]
                                ss, tt = S(0), S(1)
                                rr = S(2 + st["oc"] % 2)
                                t1 = vop(lambda e, ss=ss: e.tensor_copy(
                                    ss, PS3[0:1, 512:1024]), waits=[(eT, tss)])
                                t7 = aop(lambda e, ss=ss, tt=tt: e.activation(
                                    tt, ss, AF.Ln), waits=[(eV, t1)])
                                t8 = aop(lambda e, rr=rr, tt=tt: e.activation(
                                    rr, tt, AF.Exp, scale=-0.5), waits=[(eA, t7)])
                                last = (eA, t8)
                                for _ in range(2):
                                    ta = vop(lambda e, rr=rr, tt=tt:
                                             e.tensor_tensor(tt, rr, rr,
                                                             op=ALU.mult),
                                             waits=[last])
                                    tb2 = vop(lambda e, ss=ss, tt=tt:
                                              e.tensor_tensor(tt, tt, ss,
                                                              op=ALU.mult),
                                              waits=[(eV, ta)])
                                    tc2 = vop(lambda e, tt=tt: e.tensor_scalar(
                                        tt, tt, -0.5, 1.5, op0=ALU.mult,
                                        op1=ALU.add), waits=[(eV, tb2)])
                                    td = vop(lambda e, rr=rr, tt=tt:
                                             e.tensor_tensor(rr, rr, tt,
                                                             op=ALU.mult),
                                             waits=[(eV, tc2)])
                                    last = (eV, td)
                                st["rr"] = rr
                                st["tnewt"] = last

                            def norm_stage2(st):
                                w_ = [st["tnewt"]]
                                if last_tn[0] is not None:
                                    w_.append((eV, last_tn[0]))
                                tfb = mm(PS3[:, 0:512], onr[:], st["rr"],
                                         True, True, waits=w_, inc=eT)
                                tn = vop(lambda e, dtile=st["dtile"]:
                                         e.tensor_tensor(
                                             dtile, dtile, PS3[:, 0:512],
                                             op=ALU.mult), waits=[(eT, tfb)])
                                tnorm_done[(st["pname"], st["oc"])] = tn
                                last_tn[0] = tn
                                return tn

                            if oc >= 1:
                                norm_stage1(normq[oc - 1])
                                last_tss[0] = normq[oc - 1]["tss"]
                            tq2 = vop(lambda e, dtile=dtile: e.tensor_tensor(
                                q2[:], dtile, dtile, op=ALU.mult),
                                waits=[(eA, te)] +
                                      ([(eT, last_tss[0])]
                                       if last_tss[0] is not None else []))
                            normq[oc] = {"dtile": dtile, "tq2": tq2, "oc": oc,
                                         "pname": pname}
                            if oc >= 2:
                                tevac[(oc - 2) % 4] = (eV,
                                                       norm_stage2(normq[oc - 2]))
                            if oc == NDC - 1:
                                norm_stage1(normq[oc])
                                last_tss[0] = normq[oc]["tss"]
                                tevac[(oc - 1) % 4] = (eV,
                                                       norm_stage2(normq[oc - 1]))
                                tevac[oc % 4] = (eV, norm_stage2(normq[oc]))
                                normq.clear()
                        elif pname == "og":
                            te = aop(lambda e, oc=oc, bank=bank: e.activation(
                                gate_(oc), bank, AF.Silu),
                                waits=[(eT, tg)] + list(extra_arena))
                            tevac[par] = (eA, te)
                        else:
                            vp = oc % 2
                            te = aop(lambda e, bank=bank, oc=oc, vp=vp: e.activation(
                                vt2[:, vp * HALF:(vp + 1) * HALF], bank, AF.Identity,
                                bias=vbs[:, oc:oc + 1]), waits=[(eT, tg)])
                            for vsub in range(NSUB):
                                g = oc * NSUB + vsub
                                slot = (g % 2) * 1024
                                w2 = [(eA, te)]
                                if g >= 2:
                                    w2.append((eV, tgv[g - 2]))
                                if NO_TRANSPOSE:
                                    tt_ = b.emit("tensor", lambda e: e.nop(),
                                                 waits=w2, inc=eT)
                                else:
                                    if TR_MODE == "fixed_in":
                                        tt_ = b.emit("tensor",
                                                     lambda e, slot=slot:
                                                     e.transpose(PT[:, slot:slot + 128],
                                                                 idn[:], idn[:]),
                                                     waits=w2, inc=eT)
                                    elif TR_MODE == "slot0":
                                        tt_ = b.emit("tensor",
                                                     lambda e, vsub=vsub, vp=vp:
                                                     e.transpose(
                                                         PT[:, 0:128],
                                                         vt2[:, vp * HALF + vsub * 128:
                                                             vp * HALF + (vsub + 1) * 128],
                                                         idn[:]),
                                                     waits=w2, inc=eT)
                                    else:
                                        tt_ = b.emit("tensor",
                                                     lambda e, vsub=vsub, vp=vp, slot=slot:
                                                     e.transpose(
                                                         PT[:, slot:slot + 128],
                                                         vt2[:, vp * HALF + vsub * 128:
                                                             vp * HALF + (vsub + 1) * 128],
                                                         idn[:]),
                                                     waits=w2, inc=eT)
                                if GV_PLAIN_COPY:
                                    tgv[oc * NSUB + vsub] = vop(
                                        lambda e, vsub=vsub, oc=oc, slot=slot:
                                        e.tensor_copy(
                                            gv_(vsub)[:, oc * 128:(oc + 1) * 128],
                                            PT[:, slot:slot + 128]),
                                        waits=[(eT, tt_), (eV, t_beta_all)])
                                else:
                                    tgv[oc * NSUB + vsub] = vop(
                                        lambda e, vsub=vsub, oc=oc, slot=slot:
                                        e.tensor_scalar(
                                            gv_(vsub)[:, oc * 128:(oc + 1) * 128],
                                            PT[:, slot:slot + 128],
                                            beta[:, vsub * NH + oc: vsub * NH + oc + 1],
                                            None, op0=ALU.mult),
                                        waits=[(eT, tt_), (eV, t_beta_all)])
                            tevac[par] = (eA, te)
                t_gv_all = eV.n
                t_kq_all = eV.n

                if sub == "qk" and QK_TRANSPOSES:
                    tlast = None
                    for i_ in range(8):
                        tlast = b.emit("tensor",
                                       lambda e, i_=i_: e.transpose(
                                           PT[:, i_ * 128:(i_ + 1) * 128],
                                           idn[:], idn[:]),
                                       waits=[(eT, eT.n), (eV, eV.n), (eA, eA.n)],
                                       inc=eT)
                if sub in ("proj", "qk"):
                    tdump = []
                    for dc in range(NDC):
                        tdump.append(vop(lambda e, dc=dc: e.tensor_copy(
                            xh_(dc), qn_(dc)),
                            waits=[(eV, eV.n), (eA, eA.n), (eT, eT.n)]))
                    return tdump, eT.n, eT.n

                t_projV, t_projA = eV.n, eA.n
                tmsk = [[None, None] for _ in range(NSUB)]
                for sp in range(NSUB):
                    for ic in range(2):
                        ps = PS1 if ic == 0 else PS2
                        for hh in range(NH):
                            w = [(eV, tnorm_done[("q", hh)]),
                                 (eV, tnorm_done[("k", hh)])]
                            if hh == 0 and sp == 0:
                                w += [(eV, t_projV), (eA, t_projA)]
                            if hh == 0 and sp >= 1:
                                w.append((eV, tmsk[sp - 1][ic]))
                            tsc = mm(ps[:, hh * CH:(hh + 1) * CH],
                                     kn_(hh)[:, sp * 128:(sp + 1) * 128],
                                     qn_(hh)[:, sp * 128 + ic * CH:
                                             sp * 128 + (ic + 1) * CH],
                                     True, True, waits=w,
                                     inc=eT if hh == NH - 1 else None)
                        mb = msk[:, ic * CH:(ic + 1) * CH].unsqueeze(1) \
                            .broadcast_to([128, NH, CH])
                        tmsk[sp][ic] = vop(
                            lambda e, sp=sp, ic=ic, ps=ps, mb=mb: e.tensor_tensor(
                                mskball[:, sp * NH * 128:(sp + 1) * NH * 128]
                                .rearrange("p (n i) -> p n i", i=128)
                                [:, :, ic * CH:(ic + 1) * CH],
                                ps[:, 0:NH * CH].rearrange("p (n i) -> p n i", i=CH),
                                mb, op=ALU.mult), waits=[(eT, tsc)])
                t_sc_last = eT.n

                tgat = [None] * NH
                for hh in range(NH):
                    par = hh % 2
                    bank = PS3[:, par * 512:(par + 1) * 512]
                    for sp in range(NSUB):
                        w = [(eV, tmsk[sp][0]), (eV, tmsk[sp][1]), (eV, t_gv_all)]
                        if sp == 0 and hh < 2:
                            w.append((eV, t_projV))
                        if sp == 0 and hh >= 2 and tgat[hh - 2] is not None:
                            w.append((eV, tgat[hh - 2]))
                        ta = mm(bank[:, sp * 128:(sp + 1) * 128],
                                gv_(sp)[:, hh * 128:(hh + 1) * 128],
                                mskball[:, sp * NH * 128 + hh * 128:
                                        sp * NH * 128 + (hh + 1) * 128],
                                True, True, waits=w,
                                inc=eT if sp == NSUB - 1 else None)
                    tgat[hh] = vop(lambda e, hh=hh, bank=bank: e.tensor_tensor(
                        gate_(hh), bank, gate_(hh), op=ALU.mult),
                        waits=[(eT, ta)])

                if sub == "score":
                    tdump = []
                    for dc in range(NDC):
                        tdump.append(vop(lambda e, dc=dc: e.tensor_copy(
                            xh_(dc), gate_(dc)),
                            waits=[(eV, eV.n), (eA, eA.n), (eT, eT.n)]))
                    return tdump, eT.n, eT.n

                tslab2 = [None, None]
                tgrp2 = [None] * 4
                txn = [None] * NDC
                for oc in range(NDC):
                    par = oc % 4
                    wpar = oc % 2
                    w = [(eT, tgrp2[par]) if tgrp2[par] is not None
                         else (eT, tgrp[par])]
                    tl = dma("scalar" if par < 2 else "sync", SLAB4[par](),
                             wcache[DM * 4 + 128 * oc:DM * 4 + 128 * (oc + 1), :],
                             SSEM4[par], waits=w)
                    slab = SLAB4[par]()
                    ttok = (SSEM4[par], tl)
                    bank = BANK4[par]()
                    for kc in range(NDC):
                        w = [ttok, (eV, tgat[kc])]
                        if kc == 0 and oc >= 4 and txn[oc - 4] is not None:
                            w.append((eV, txn[oc - 4]))
                        tg = mm(bank, slab.rearrange("p (c f) -> p c f",
                                                     f=128)[:, kc, :],
                                gate_(kc), kc == 0, kc == NDC - 1, waits=w,
                                inc=eT if kc == NDC - 1 else None)
                    tgrp2[par] = tg
                    txn[oc] = vop(lambda e, oc=oc, bank=bank: e.scalar_tensor_tensor(
                        xh_(oc), bank, obs[:, oc:oc + 1], xh_(oc),
                        op0=ALU.add, op1=ALU.add),
                        waits=[(eT, tg)])
                t_arena_free = eT.n
                return txn, t_sc_last, t_arena_free

            txl = layernorm(xtok)
            if stage == "ln1":
                tst = []
                for dc in range(NDC):
                    t = vop(lambda e, dc=dc: e.tensor_copy(xh_(dc), xln_(dc)),
                            waits=[(eV, txl[dc])])
                    tst.append(t)
                txn = tst
            elif stage == "betaonly":
                txn, t_sc_last, t_ar_free = attention(
                    txl, extra_arena=[], t_wAB_free=0, sub="beta")
            else:
                txn, t_ffn_last = ffn(txl)
                if stage not in ("ffn1",):
                    txl2 = layernorm([(eV, t) for t in txn],
                                     extra_xbf=[(eT, t_ffn_last)])
                    asub = {"attnp": "proj", "attns": "score", "attnb": "beta",
                            "attnq": "qk", "attnl": "ln2"}.get(stage, "all")
                    txn, t_sc_last, t_ar_free = attention(
                        txl2, extra_arena=[(eT, t_ffn_last)],
                        t_wAB_free=t_ffn_last, sub=asub)
                    if stage == "full" and asub == "all":
                        txl3 = layernorm([(eV, t) for t in txn],
                                         extra_xbf=[(eT, t_sc_last)])
                        txn, _ = ffn(txl3,
                                     extra_arena=[(eT, t_ar_free)],
                                     extra_wb3=[(eT, t_ar_free)],
                                     extra_wAB=[(eT, t_ar_free),
                                                (sem["sCA"], sem["sCA"].n),
                                                (sem["sCB"], sem["sCB"].n)])
            tst = []
            for dc in range(NDC):
                t = dma("gpsimd", y_out[ts_(dc), h * HALF:(h + 1) * HALF], xh_(dc),
                        dP, waits=[(eV, txn[dc])])
                tst.append(t)
            return tst

        halfprog(0)
        halfprog(1)
        b.emit("gpsimd", lambda e: e.nop(), waits=[(dP, dP.n)])

        b.replay(E(nc.Block()))

    return nc


def _blocked_lhsT(W):
    o, i = W.shape
    return np.ascontiguousarray(
        W.reshape(o // 128, 128, i // 128, 128).transpose(0, 3, 2, 1)
        .reshape(o, i))


def _bias_cols(v):
    return np.ascontiguousarray(v.reshape(NDC, 128).T)


def _mask2():
    m = np.zeros((128, 128), np.float32)
    tri = np.triu(np.ones((CH, CH), np.float32))
    m[0:CH, 0:CH] = tri
    m[CH:128, CH:128] = tri
    return m


def host_inputs_forward(core, x_full, w12ag, w3ag, attn):
    xs = x_full.reshape(8192, DM)[core * TOK:(core + 1) * TOK, :]
    m = {
        "x_in": np.ascontiguousarray(xs.T),
        "wq": _blocked_lhsT(attn["q_w"]), "wk": _blocked_lhsT(attn["k_w"]),
        "wv": _blocked_lhsT(attn["v_w"]), "wog": _blocked_lhsT(attn["og_w"]),
        "wo": _blocked_lhsT(attn["o_w"]),
        "wbt": np.ascontiguousarray(attn["b_w"].T),
        "qb": _bias_cols(attn["q_b"]), "kb": _bias_cols(attn["k_b"]),
        "vb": _bias_cols(attn["v_b"]), "ob": _bias_cols(attn["o_b"]),
        "bb": attn["b_b"].reshape(1, NH),
        "cmask": _mask2(),
        "cones_col": np.ones((128, 1), bfdt),
        "cones_row": np.ones((1, 128), np.float32),
        "cident": np.eye(128, dtype=bfdt),
    }
    if w12ag is not None:
        m["wag12"] = w12ag
        m["wag3"] = w3ag
    return m


def ag_layout_w12(w1deq, w2deq):
    out = np.empty((8 * W12_SH, DM), bfdt)
    for r in range(4):
        for src, base in ((w1deq, 0), (w2deq, 4)):
            sh = src[W12_SH * r:W12_SH * (r + 1), :]
            t = sh.T
            for fc in range(W12_FC):
                blk = np.empty((128, DM), bfdt)
                for dc in range(NDC):
                    blk[:, dc * 128:(dc + 1) * 128] = \
                        t[dc * 128:(dc + 1) * 128, fc * 128:(fc + 1) * 128]
                out[(fc * 8 + base + r) * 128:(fc * 8 + base + r) * 128 + 128,
                    :] = blk
    return out


def _ag_layout_w12_old(w1deq, w2deq):
    out = np.empty((8 * W12_SH, DM), bfdt)
    for r in range(4):
        for src, base in ((w1deq, 0), (w2deq, 4)):
            sh = src[W12_SH * r:W12_SH * (r + 1), :]
            t = sh.T
            blk = t.reshape(NDC, 128, W12_FC, 128)
            d = np.empty((W12_SH, DM), bfdt)
            for fc in range(W12_FC):
                for dc in range(NDC):
                    d[fc * 128:(fc + 1) * 128, dc * 128:(dc + 1) * 128] = \
                        t[dc * 128:(dc + 1) * 128, fc * 128:(fc + 1) * 128]
            out[(base + r) * W12_SH:(base + r + 1) * W12_SH, :] = d
    return out


def ag_layout_w3(w3deq):
    out = np.empty((8 * W3_SH, DFF), bfdt)
    t = w3deq.T
    for r in range(8):
        for dcl in range(2):
            dglob = r * 256 + dcl * 128
            blk = np.empty((128, DFF), bfdt)
            for fc in range(NFC):
                blk[:, fc * 128:(fc + 1) * 128] = \
                    t[fc * 128:(fc + 1) * 128, dglob:dglob + 128]
            out[(dcl * 8 + r) * 128:(dcl * 8 + r) * 128 + 128, :] = blk
    return out


def _ag_layout_w3_old(w3deq):
    out = np.empty((8 * W3_SH, DFF), bfdt)
    t = w3deq.T
    for r in range(8):
        blk = np.empty((W3_SH, DFF), bfdt)
        for dcl in range(2):
            dglob = r * 256 + dcl * 128
            for fc in range(NFC):
                blk[dcl * 128:(dcl + 1) * 128, fc * 128:(fc + 1) * 128] = \
                    t[fc * 128:(fc + 1) * 128, dglob:dglob + 128]
        out[r * W3_SH:(r + 1) * W3_SH, :] = blk
    return out




GAP = [NF4[j + 1] - NF4[j] for j in range(15)]


def build_quant(fused=False):
    nc = bass.Bass("TRN2", target_bir_lowering=False)
    wfull = nc.dram_tensor("wfull", [DFF, DM], BF16, kind="ExternalInput").ap()
    w3full = nc.dram_tensor("w3full", [DM, DFF], BF16, kind="ExternalInput").ap()
    rw12 = nc.dram_tensor("rw12", [W12_SH, DM], BF16, kind="ExternalInput").ap()
    rw3 = nc.dram_tensor("rw3", [W3_SH, DFF], BF16, kind="ExternalInput").ap()
    qident = nc.dram_tensor("qident", [128, 128], BF16, kind="ExternalInput").ap()
    crow88 = nc.dram_tensor("crow88", [1, 88], F32, kind="ExternalInput").ap()
    cone88 = nc.dram_tensor("cone88", [88, 1], F32, kind="ExternalInput").ap()
    ag1s = nc.dram_tensor("ag1s", [W12_SH, DM], BF16, kind="ExternalOutput").ap()
    ag2s = nc.dram_tensor("ag2s", [W3_SH, DFF], BF16, kind="ExternalOutput").ap()
    dq12 = nc.dram_tensor("dq12", [W12_SH, 32], BF16).ap()
    rc12 = nc.dram_tensor("rc12", [W12_SH, 32], F32).ap()
    dq3 = nc.dram_tensor("dq3", [W3_SH, 88], BF16).ap()
    rc3 = nc.dram_tensor("rc3", [W3_SH, 88], F32).ap()

    with ExitStack() as ctx:
        E = ctx.enter_context
        qt = E(nc.sbuf_tensor("qt", [88, 2 * 16384], BF16))
        scalf = E(nc.sbuf_tensor("scalf", [88, 2048], F32))
        scalf3 = E(nc.sbuf_tensor("scalf3", [88, 2048], F32))
        sq = E(nc.sbuf_tensor("sq", [88, 8 * 256], F32))
        dqs = E(nc.sbuf_tensor("dqs", [88, 256], BF16))
        cents = E(nc.sbuf_tensor("cents", [88, 256], BF16))
        smalls = E(nc.sbuf_tensor("smalls", [88, 16], F32))
        s11 = E(nc.sbuf_tensor("s11", [1, 8], F32))
        s11b = E(nc.sbuf_tensor("s11b", [1, 2], BF16))
        ident = E(nc.sbuf_tensor("ident", [128, 128], BF16))
        r88 = E(nc.sbuf_tensor("r88", [1, 88], F32))
        c88 = E(nc.sbuf_tensor("c88", [88, 1], F32))
        wrow = E(nc.sbuf_tensor("wrow", [128, 2 * DFF], BF16))
        rbf = E(nc.sbuf_tensor("rbf", [128, DFF], BF16))
        accf = E(nc.sbuf_tensor("accf", [128, DFF], F32))
        tmpf = E(nc.sbuf_tensor("tmpf", [128, DFF], F32))
        wdq = E(nc.sbuf_tensor("wdq", [128, DFF], BF16))
        slab = E(nc.sbuf_tensor("slab", [128, DFF], BF16))
        dqrow = E(nc.sbuf_tensor("dqrow", [128, 2 * 88], BF16))
        rcrow = E(nc.sbuf_tensor("rcrow", [128, 2 * 88], F32))
        PM = E(nc.psum_tensor("PM", [128, 512], F32))
        PT = E(nc.psum_tensor("PT", [128, 2048], BF16))

        b = B(nc)
        semnames = ("dC", "sQ0", "sQ1", "sR0", "sR1", "dQW", "dO",
                    "eT", "eV", "eA", "eP")
        sem = {n: Sem(E(nc.semaphore(n)), n) for n in semnames}
        dC, dQW, dO = sem["dC"], sem["dQW"], sem["dO"]
        sQ = (sem["sQ0"], sem["sQ1"])
        sR = (sem["sR0"], sem["sR1"])
        eT, eV, eA, eP = sem["eT"], sem["eV"], sem["eA"], sem["eP"]

        def dma(queue, out, in_, s, waits=()):
            return b.emit(queue, lambda e, o=out, i=in_: e.dma_start(out=o, in_=i),
                          waits=waits, inc=s, k=16)

        def mm(out, lhsT, rhs, waits=(), inc=eT):
            return b.emit("tensor",
                          lambda e, o=out, l=lhsT, r=rhs:
                          e.matmul(o, l, r, start=True, stop=True),
                          waits=waits, inc=inc)

        def vop(fn, waits=()):
            return b.emit("vector", fn, waits=waits, inc=eV)

        def pop(fn, waits=()):
            return b.emit("gpsimd", fn, waits=waits, inc=eP)

        dma("sync", ident[:], qident, dC)
        dma("sync", r88[:], crow88, dC)
        dma("sync", c88[:], cone88, dC)
        cbar = [(dC, dC.n)]
        for _e in ENGINES:
            b.emit(_e, lambda e: e.nop(), waits=cbar)

        def fullpass(src, scaldst, t0=None):
            toks = []
            tprev = [t0, t0]
            for t in range(8):
                par = t % 2
                w = [(eV, tprev[par])] if tprev[par] is not None else []
                tl = dma("sync", qt[:, par * 16384:(par + 1) * 16384],
                         src.rearrange("a bx -> (a bx)")
                         .rearrange("(t p f) -> t p f", t=8, p=88)[t, :, :],
                         sQ[par], waits=w)
                tv = vop(lambda e, par=par, t=t: e.tensor_reduce(
                    scaldst[:, t * 256:(t + 1) * 256],
                    qt[:, par * 16384:(par + 1) * 16384]
                    .rearrange("p (c z) -> p c z", z=64),
                    axis=mybir.AxisListType.X, op=ALU.max,
                    apply_absolute_value=True), waits=[(sQ[par], tl)])
                tprev[par] = tv
                toks.append(tv)
            return toks[-1], toks

        tf12, _ = fullpass(wfull, scalf)
        tf3, _ = fullpass(w3full, scalf3, t0=tf12)

        def mean_of(scaldst, tfull, mcol_off, s11_off):
            rs = smalls[:, mcol_off + 1:mcol_off + 2]
            t1 = vop(lambda e: e.tensor_reduce(rs, scaldst[:],
                                               axis=mybir.AxisListType.X,
                                               op=ALU.add), waits=[(eV, tfull)])
            t2 = mm(PM[0:1, 0:1], c88[:], rs, waits=[(eV, t1)])
            t3 = vop(lambda e: e.tensor_scalar(s11[0:1, s11_off:s11_off + 1],
                                               PM[0:1, 0:1], 1.0 / 180224.0, None,
                                               op0=ALU.mult), waits=[(eT, t2)])
            t4 = vop(lambda e: e.tensor_copy(s11b[0:1, s11_off:s11_off + 1],
                                             s11[0:1, s11_off:s11_off + 1]),
                     waits=[(eV, t3)])
            t5 = vop(lambda e: e.tensor_copy(s11[0:1, 4 + s11_off:5 + s11_off],
                                             s11b[0:1, s11_off:s11_off + 1]),
                     waits=[(eV, t4)])
            t6 = mm(PM[0:88, 1:2], r88[:], s11[0:1, 4 + s11_off:5 + s11_off],
                    waits=[(eV, t5)])
            t7 = vop(lambda e: e.tensor_copy(smalls[:, mcol_off:mcol_off + 1],
                                             PM[0:88, 1:2]), waits=[(eT, t6)])
            return t7

        tm12 = mean_of(scalf, tf12, 0, 0)
        tm3 = mean_of(scalf3, tf3, 2, 1)

        def scaler_stage(rsrc, ntiles, mcol_off, tmean, dq_d, rc_d, tprevload):
            mcol = smalls[:, mcol_off:mcol_off + 1]
            S = lambda a: sq[:, a * 256:(a + 1) * 256]
            scal, yv, tv, Fv, bb2, rcs = S(0), S(1), S(2), S(3), S(4), S(5)
            m1 = smalls[:, 4:5]
            qf = smalls[:, 5:6]
            rqf = smalls[:, 6:7]
            tlast = tprevload
            tEnd = None
            for st in range(ntiles):
                par = st % 2
                w = [(eV, tlast)] if tlast is not None else []
                tl = dma("sync", qt[:, par * 16384:(par + 1) * 16384],
                         rsrc.rearrange("a bx -> (a bx)")
                         .rearrange("(t p f) -> t p f", t=ntiles, p=88)[st, :, :],
                         sQ[par], waits=w)
                t1 = vop(lambda e, par=par: e.tensor_reduce(
                    scal, qt[:, par * 16384:(par + 1) * 16384]
                    .rearrange("p (c z) -> p c z", z=64),
                    axis=mybir.AxisListType.X, op=ALU.max,
                    apply_absolute_value=True),
                    waits=[(sQ[par], tl), (dQW, dQW.n)])
                t2 = vop(lambda e: e.tensor_scalar(cents[:], scal, mcol, None,
                                                   op0=ALU.subtract),
                         waits=[(eV, t1), (eV, tmean)])
                t3 = vop(lambda e: e.tensor_reduce(m1, cents[:],
                                                   axis=mybir.AxisListType.X,
                                                   op=ALU.max,
                                                   apply_absolute_value=True),
                         waits=[(eV, t2)])
                t4 = vop(lambda e: e.reciprocal(qf, m1), waits=[(eV, t3)])
                t5 = vop(lambda e: e.tensor_scalar(qf, qf, 128.0, None,
                                                   op0=ALU.mult), waits=[(eV, t4)])
                t6 = vop(lambda e: e.tensor_scalar(yv, cents[:], qf, None,
                                                   op0=ALU.mult), waits=[(eV, t5)])
                t7 = vop(lambda e: e.tensor_scalar(tv, yv, 128.5, None,
                                                   op0=ALU.add), waits=[(eV, t6)])
                t8 = vop(lambda e: e.memset(Fv, 0.0), waits=[(eV, t7)])
                tk = t8
                for k in range(8, -1, -1):
                    ka = vop(lambda e, k=k: e.tensor_scalar(
                        bb2, tv, float(2 ** k), None, op0=ALU.is_ge),
                        waits=[(eV, tk)])
                    kb = vop(lambda e, k=k: e.scalar_tensor_tensor(
                        tv, bb2, float(-(2 ** k)), tv, op0=ALU.mult, op1=ALU.add),
                        waits=[(eV, ka)])
                    tk = vop(lambda e, k=k: e.scalar_tensor_tensor(
                        Fv, bb2, float(2 ** k), Fv, op0=ALU.mult, op1=ALU.add),
                        waits=[(eV, kb)])
                t9 = vop(lambda e: e.tensor_scalar(Fv, Fv, -128.0, 127.0,
                                                   op0=ALU.add, op1=ALU.min),
                         waits=[(eV, tk)])
                t10 = vop(lambda e: e.tensor_scalar(Fv, Fv, -128.0, None,
                                                    op0=ALU.max), waits=[(eV, t9)])
                t11 = vop(lambda e: e.reciprocal(rqf, qf), waits=[(eV, t10)])
                t12 = vop(lambda e: e.tensor_scalar(dqs[:], Fv, rqf, None,
                                                    op0=ALU.mult), waits=[(eV, t11)])
                t13 = vop(lambda e: e.tensor_scalar(dqs[:], dqs[:], mcol, None,
                                                    op0=ALU.add), waits=[(eV, t12)])
                t14 = vop(lambda e: e.reciprocal(rcs, scal), waits=[(eV, t13)])
                nrows = 16384 // (dq_d.shape[1] * 64)
                td1 = dma("gpsimd", dq_d.rearrange("a bx -> (a bx)")
                          [st * 22528:(st + 1) * 22528]
                          .rearrange("(p c) -> p c", p=88), dqs[:], dQW,
                          waits=[(eV, t13)])
                td2 = dma("gpsimd", rc_d.rearrange("a bx -> (a bx)")
                          [st * 22528:(st + 1) * 22528]
                          .rearrange("(p c) -> p c", p=88), rcs, dQW,
                          waits=[(eV, t14)])
                tlast = t14
                tEnd = t14
            return tEnd, dQW.n

        tq12, _ = scaler_stage(rw12, 2, 0, tm12, dq12, rc12, tf3)
        tq3, ndqw = scaler_stage(rw3, 1, 2, tm3, dq3, rc3, tq12)

        def row_stage(rsrc, nrt, width, dq_d, rc_d, agdst, t_prev, s_prev=None):
            nbl = width // 64
            ndc = width // 128
            tev_bank = [t_prev, t_prev]
            tstores = [s_prev] if s_prev else []
            tlw = [t_prev, t_prev]
            for t in range(nrt):
                par = t % 2
                w = [(dQW, ndqw)]
                if tlw[par] is not None:
                    w.append((eV, tlw[par]))
                tl1 = dma("sync", wrow[:, par * width:(par + 1) * width],
                          rsrc[128 * t:128 * (t + 1), :], sR[par], waits=w)
                tl2 = dma("sync", dqrow[:, par * nbl:(par + 1) * nbl],
                          dq_d[128 * t:128 * (t + 1), :], sR[par])
                tl3 = dma("sync", rcrow[:, par * nbl:(par + 1) * nbl],
                          rc_d[128 * t:128 * (t + 1), :], sR[par])
                tok = sR[par].n
                t1 = vop(lambda e, par=par: e.tensor_tensor(
                    rbf[:, 0:width].rearrange("p (c z) -> p c z", z=64),
                    wrow[:, par * width:(par + 1) * width]
                    .rearrange("p (c z) -> p c z", z=64),
                    rcrow[:, par * nbl:(par + 1) * nbl].unsqueeze(2)
                    .broadcast_to([128, nbl, 64]), op=ALU.mult),
                    waits=[(sR[par], tok)])
                t2 = vop(lambda e: e.memset(accf[:, 0:width], float(NF4[0])),
                         waits=[(eV, t1)])
                tk = t2
                for j in range(15):
                    ja = vop(lambda e, j=j: e.tensor_scalar(
                        tmpf[:, 0:width], rbf[:, 0:width], MID[j], GAP[j],
                        op0=ALU.is_gt, op1=ALU.mult), waits=[(eV, tk)])
                    tk = vop(lambda e, j=j: e.tensor_tensor(
                        accf[:, 0:width], accf[:, 0:width], tmpf[:, 0:width],
                        op=ALU.add), waits=[(eV, ja)])
                t3 = vop(lambda e, par=par: e.tensor_tensor(
                    wdq[:, 0:width].rearrange("p (c z) -> p c z", z=64),
                    accf[:, 0:width].rearrange("p (c z) -> p c z", z=64),
                    dqrow[:, par * nbl:(par + 1) * nbl].unsqueeze(2)
                    .broadcast_to([128, nbl, 64]), op=ALU.mult),
                    waits=[(eV, tk)])
                tlw[par] = t3
                tev = None
                first_evac = True
                for dc in range(ndc):
                    bank = dc % 2
                    slot = bank * 1024
                    w2 = [(eV, t3)]
                    if tev_bank[bank] is not None:
                        w2.append((eV, tev_bank[bank]))
                    tt_ = b.emit("tensor", lambda e, dc=dc, slot=slot: e.transpose(
                        PT[:, slot:slot + 128],
                        wdq[:, dc * 128:(dc + 1) * 128], ident[:]),
                        waits=w2, inc=eT)
                    w3_ = [(eT, tt_)]
                    if first_evac and tstores:
                        w3_.append((dO, tstores[-1]))
                        first_evac = False
                    tev = vop(lambda e, dc=dc, slot=slot: e.tensor_copy(
                        slab[:, dc * 128:(dc + 1) * 128], PT[:, slot:slot + 128]),
                        waits=w3_)
                    tev_bank[bank] = tev
                ts_ = dma("gpsimd", agdst[128 * t:128 * (t + 1), :],
                          slab[:, 0:width], dO, waits=[(eV, tev)])
                tstores.append(ts_)
            return tstores

        st1 = row_stage(rw12, W12_FC, DM, dq12, rc12, ag1s, tq3)
        st2 = row_stage(rw3, 2, DFF, dq3, rc3, ag2s, eV.n, s_prev=st1[-1])
        b.emit("gpsimd", lambda e: e.nop(), waits=[(dO, dO.n)])
        b.replay(E(nc.Block()))
    return nc


def host_inputs_quant(core, w1, w2, w3):
    wfull = w1 if core < 4 else w2
    r = core % 4
    return {
        "wfull": np.ascontiguousarray(wfull),
        "w3full": np.ascontiguousarray(w3),
        "rw12": np.ascontiguousarray(wfull[W12_SH * r:W12_SH * (r + 1), :]),
        "rw3": np.ascontiguousarray(w3[W3_SH * core:W3_SH * (core + 1), :]),
        "qident": np.eye(128, dtype=bfdt),
        "crow88": np.ones((1, 88), np.float32),
        "cone88": np.ones((88, 1), np.float32),
    }


FUSED = True


def kernel(**inputs):
    x = np.asarray(inputs["L0_attn_hidden_states"])
    w1 = np.ascontiguousarray(np.asarray(inputs["L0_ffn_weight1"]))
    w2 = np.ascontiguousarray(np.asarray(inputs["L0_ffn_weight2"]))
    w3 = np.ascontiguousarray(np.asarray(inputs["L0_ffn_weight3"]))
    attn = {k: np.ascontiguousarray(np.asarray(inputs["L0_attn_" + k]))
            for k in ("q_w", "q_b", "k_w", "k_b", "v_w", "v_b", "og_w",
                      "o_w", "o_b", "b_w", "b_b")}

    if FUSED:
        ncf = build_forward(stage="full", fused=True)
        fmaps = []
        for c in range(NCORES):
            m = host_inputs_forward(c, x, None, None, attn)
            q = host_inputs_quant(c, w1, w2, w3)
            for k in ("qident", "wfull", "w3full"):
                del q[k]
            m.update(q)
            m12 = np.zeros((1, 4), np.float32)
            m12[0, 0 if c < 4 else 1] = 1.0
            m3 = np.zeros((1, 4), np.float32)
            m3[0, 2] = 1.0
            m["cm12"] = m12
            m["cm3"] = m3
            m["cnmid"] = np.tile(-np.array(MID, np.float32), (128, 1))
            fmaps.append(m)
        resf = run_bass_kernel_spmd(ncf, fmaps, core_ids=list(range(NCORES)))
    else:
        ncq = build_quant()
        qmaps = [host_inputs_quant(c, w1, w2, w3) for c in range(NCORES)]
        resq = run_bass_kernel_spmd(ncq, qmaps, core_ids=list(range(NCORES)))
        a1 = np.stack([np.asarray(resq.results[c]["ag1s"])
                       for c in range(NCORES)])
        w12ag = np.ascontiguousarray(
            a1.reshape(8, W12_FC, 128, DM).transpose(1, 0, 2, 3)
            .reshape(8 * W12_SH, DM))
        a2 = np.stack([np.asarray(resq.results[c]["ag2s"])
                       for c in range(NCORES)])
        w3ag = np.ascontiguousarray(
            a2.reshape(8, 2, 128, DFF).transpose(1, 0, 2, 3)
            .reshape(8 * W3_SH, DFF))
        ncf = build_forward(stage="full")
        fmaps = [host_inputs_forward(c, x, w12ag, w3ag, attn)
                 for c in range(NCORES)]
        resf = run_bass_kernel_spmd(ncf, fmaps, core_ids=list(range(NCORES)))
    out = np.concatenate([np.asarray(resf.results[c]["y_out"]).T
                          for c in range(NCORES)], axis=0)
    return np.ascontiguousarray(out.reshape(2, 4096, DM))


# version 14
# speedup vs baseline: 1.0630x; 1.0525x over previous
import math
from contextlib import ExitStack

import numpy as np
import ml_dtypes

import concourse.bass as bass
import concourse.mybir as mybir
from concourse.bass_utils import run_bass_kernel_spmd

F32 = mybir.dt.float32
BF16 = mybir.dt.bfloat16
U8 = mybir.dt.uint8
AF = mybir.ActivationFunctionType
ALU = mybir.AluOpType
bfdt = ml_dtypes.bfloat16

NCORES = 8
DM = 2048
DFF = 5632
NDC = DM // 128
NFC = DFF // 128
TOK = 1024
HALF = 512
NSUB = HALF // 128
NH, DH, CH = 16, 128, 64
NCH = HALF // CH
EPS_LN = 1e-5
SCALE = float(DH) ** 0.5

NF4 = [float(bfdt(v)) for v in (-1.0, -0.6962, -0.5251, -0.3949, -0.2844, -0.1848,
                                -0.0911, 0.0, 0.0796, 0.1609, 0.2461, 0.3379,
                                0.4407, 0.5626, 0.723, 1.0)]
MID = [(NF4[j] + NF4[j + 1]) * 0.5 for j in range(15)]

W12_SH = DFF // 4
W12_FC = W12_SH // 128
W3_SH = DM // 8

ENGINES = ("sync", "tensor", "vector", "scalar", "gpsimd")


class Sem:
    __slots__ = ("h", "n", "name")

    def __init__(self, h, name):
        self.h = h
        self.n = 0
        self.name = name


class B:
    def __init__(self, nc):
        self.nc = nc
        self.q = {e: [] for e in ENGINES}
        self.last_wait = {e: {} for e in ENGINES}

    def emit(self, eng, fn, waits=(), inc=None, k=1):
        ws = []
        lw = self.last_wait[eng]
        for s, v in waits:
            if s is None or v is None or v <= 0:
                continue
            if lw.get(s.name, -1) >= v:
                continue
            lw[s.name] = v
            ws.append((s.h, v))
        tok = None
        if inc is not None:
            inc.n += k
            tok = inc.n
        self.q[eng].append((ws, fn, None if inc is None else inc.h, k))
        return tok

    def replay(self, block):
        for e in ENGINES:
            ops = self.q[e]

            def run(eng, ops=ops):
                for ws, fn, inch, k in ops:
                    for h, v in ws:
                        eng.wait_ge(h, v)
                    ins = fn(eng)
                    if inch is not None:
                        ins.then_inc(inch, k)

            getattr(block, e)(run)


def ts_(i, n=1):
    return slice(i * 128, (i + n) * 128)


NOACT = False
NOMM = False
GV_PLAIN_COPY = False
NO_TRANSPOSE = False
TR_MODE = "normal"
QK_TRANSPOSES = False
BETA_SUBS = 4
BETA_IN_Q2 = False
LN_SKIP = set()


def build_forward(stage="full", fused=False):
    nc = bass.Bass("TRN2", target_bir_lowering=False)

    x_in = nc.dram_tensor("x_in", [DM, TOK], F32, kind="ExternalInput").ap()
    if fused:
        rw12 = nc.dram_tensor("rw12", [W12_SH, DM], BF16, kind="ExternalInput").ap()
        cm12 = nc.dram_tensor("cm12", [1, 4], F32, kind="ExternalInput").ap()
        cnmid = nc.dram_tensor("cnmid", [128, 15], F32, kind="ExternalInput").ap()
        cm3 = nc.dram_tensor("cm3", [1, 4], F32, kind="ExternalInput").ap()
        ccin = nc.dram_tensor("ccin", [1, 4], F32).ap()
        ccout = nc.dram_tensor("ccout", [1, 4], F32).ap()
        rw3 = nc.dram_tensor("rw3", [W3_SH, DFF], BF16, kind="ExternalInput").ap()
        crow88 = nc.dram_tensor("crow88", [1, 88], F32, kind="ExternalInput").ap()
        cone88 = nc.dram_tensor("cone88", [88, 1], F32, kind="ExternalInput").ap()
        ag1s = nc.dram_tensor("ag1s", [W12_SH, DM], BF16).ap()
        ag2s = nc.dram_tensor("ag2s", [W3_SH, DFF], BF16).ap()
        wag12 = nc.dram_tensor("wag12", [8 * W12_SH, DM], BF16).ap()
        wag3 = nc.dram_tensor("wag3", [8 * W3_SH, DFF], BF16).ap()
        dq12 = nc.dram_tensor("dq12", [W12_SH, 32], BF16).ap()
        rc12 = nc.dram_tensor("rc12", [W12_SH, 32], F32).ap()
        dq3 = nc.dram_tensor("dq3", [W3_SH, 88], BF16).ap()
        rc3 = nc.dram_tensor("rc3", [W3_SH, 88], F32).ap()
    else:
        wag12 = nc.dram_tensor("wag12", [8 * W12_SH, DM],
                               BF16, kind="ExternalInput").ap()
        wag3 = nc.dram_tensor("wag3", [8 * W3_SH, DFF],
                              BF16, kind="ExternalInput").ap()
    wq = nc.dram_tensor("wq", [DM, DM], F32, kind="ExternalInput").ap()
    wk = nc.dram_tensor("wk", [DM, DM], F32, kind="ExternalInput").ap()
    wv = nc.dram_tensor("wv", [DM, DM], F32, kind="ExternalInput").ap()
    wog = nc.dram_tensor("wog", [DM, DM], F32, kind="ExternalInput").ap()
    wo = nc.dram_tensor("wo", [DM, DM], F32, kind="ExternalInput").ap()
    wbt = nc.dram_tensor("wbt", [DM, NH], F32, kind="ExternalInput").ap()
    qb = nc.dram_tensor("qb", [128, NDC], F32, kind="ExternalInput").ap()
    kb = nc.dram_tensor("kb", [128, NDC], F32, kind="ExternalInput").ap()
    vb = nc.dram_tensor("vb", [128, NDC], F32, kind="ExternalInput").ap()
    ob = nc.dram_tensor("ob", [128, NDC], F32, kind="ExternalInput").ap()
    bb = nc.dram_tensor("bb", [1, NH], F32, kind="ExternalInput").ap()
    cmask = nc.dram_tensor("cmask", [128, 128], F32, kind="ExternalInput").ap()
    cones_col = nc.dram_tensor("cones_col", [128, 1], BF16, kind="ExternalInput").ap()
    cones_row = nc.dram_tensor("cones_row", [1, 128], F32, kind="ExternalInput").ap()
    cident = nc.dram_tensor("cident", [128, 128], BF16, kind="ExternalInput").ap()

    y_out = nc.dram_tensor("y_out", [DM, TOK], F32, kind="ExternalOutput").ap()
    wcache = nc.dram_tensor("wcache", [5 * DM, DM], BF16).ap()

    with ExitStack() as ctx:
        E = ctx.enter_context

        bbs = E(nc.sbuf_tensor("bbs", [128, NH], F32))
        wbres = E(nc.sbuf_tensor("wbres", [128, NDC * NH], BF16))
        bscr = E(nc.sbuf_tensor("bscr", [128, NH], F32))
        bscr2 = E(nc.sbuf_tensor("bscr2", [128, NH], F32))
        beta = E(nc.sbuf_tensor("beta", [128, NSUB * NH], F32))
        xh = E(nc.sbuf_tensor("xh", [128, NDC * HALF], F32))
        xbf = E(nc.sbuf_tensor("xbf", [128, NDC * HALF], BF16))
        xln = E(nc.sbuf_tensor("xln", [128, NDC * HALF], BF16))
        arena = E(nc.sbuf_tensor("arena", [128, NFC * HALF], BF16))
        wbA = E(nc.sbuf_tensor("wbA", [128, 2 * DM], BF16))
        wbB = E(nc.sbuf_tensor("wbB", [128, 2 * DM], BF16))
        wb3 = E(nc.sbuf_tensor("wb3", [128, 2 * DFF], BF16))
        wf32 = E(nc.sbuf_tensor("wf32", [128, 2 * DM], F32))
        sc1 = E(nc.sbuf_tensor("sc1", [1, 4 * HALF], F32))
        bcB = E(nc.sbuf_tensor("bcB", [128, HALF], BF16))
        bcC = E(nc.sbuf_tensor("bcC", [128, HALF], BF16))
        lntmp = E(nc.sbuf_tensor("lntmp", [128, HALF], BF16))
        qbs = E(nc.sbuf_tensor("qbs", [128, NDC], F32))
        kbs = E(nc.sbuf_tensor("kbs", [128, NDC], F32))
        vbs = E(nc.sbuf_tensor("vbs", [128, NDC], F32))
        obs = E(nc.sbuf_tensor("obs", [128, NDC], F32))
        msk = E(nc.sbuf_tensor("msk", [128, 128], F32))
        onc = E(nc.sbuf_tensor("onc", [128, 1], BF16))
        onr = E(nc.sbuf_tensor("onr", [1, 128], F32))
        idn = E(nc.sbuf_tensor("idn", [128, 128], BF16))
        q2 = E(nc.sbuf_tensor("q2", [128, HALF], BF16))
        vt2 = E(nc.sbuf_tensor("vt2", [128, 2 * HALF], BF16))
        mskball = E(nc.sbuf_tensor("mskball", [128, NSUB * NH * 128], BF16))

        if fused:
            sqq = E(nc.sbuf_tensor("sqq", [88, 6 * 256], F32))
            dqs = E(nc.sbuf_tensor("dqs", [88, 256], BF16))
            cents = E(nc.sbuf_tensor("cents", [88, 256], BF16))
            smalls = E(nc.sbuf_tensor("smalls", [88, 16], F32))
            s11 = E(nc.sbuf_tensor("s11", [1, 8], F32))
            s11b = E(nc.sbuf_tensor("s11b", [1, 2], BF16))
            r88 = E(nc.sbuf_tensor("r88", [1, 88], F32))
            c88 = E(nc.sbuf_tensor("c88", [88, 1], F32))
            m12row = E(nc.sbuf_tensor("m12row", [1, 4], F32))
            m3row = E(nc.sbuf_tensor("m3row", [1, 4], F32))
            nmid = E(nc.sbuf_tensor("nmid", [128, 15], F32))
            rcr2 = E(nc.sbuf_tensor("rcr2", [128, 176], F32))
        PS1 = E(nc.psum_tensor("PS1", [128, 1024], F32))
        PS2 = E(nc.psum_tensor("PS2", [128, 1024], F32))
        PS3 = E(nc.psum_tensor("PS3", [128, 1024], F32))
        PT = E(nc.psum_tensor("PT", [128, 2048], BF16))

        b = B(nc)
        semnames = ("dX", "dC", "dP", "sBR", "sWA0", "sWA1", "sWB0", "sWB1",
                    "sW30", "sW31", "sWF0", "sWF1", "sCA", "sCB", "eT", "eV", "eA",
                    "sQ0", "sQ1", "sR0", "sR1", "dQW", "dO", "ccS", "eP",
                    "sWA2", "sWB2")
        sem = {n: Sem(E(nc.semaphore(n)), n) for n in semnames}
        dX, dC, dP, sBR = sem["dX"], sem["dC"], sem["dP"], sem["sBR"]
        sWA = (sem["sWA0"], sem["sWA1"])
        sWB = (sem["sWB0"], sem["sWB1"])
        sW3 = (sem["sW30"], sem["sW31"])
        sWF = (sem["sWF0"], sem["sWF1"])
        sCC = (sem["sCA"], sem["sCB"])
        eT, eV, eA = sem["eT"], sem["eV"], sem["eA"]
        sQ = (sem["sQ0"], sem["sQ1"])
        sR = (sem["sR0"], sem["sR1"])
        dQW, dO, ccS = sem["dQW"], sem["dO"], sem["ccS"]

        def dma(queue, out, in_, s, waits=()):
            return b.emit(queue, lambda e, o=out, i=in_: e.dma_start(out=o, in_=i),
                          waits=waits, inc=s, k=16)

        def mm(out, lhsT, rhs, start, stop, waits=(), inc=None):
            return b.emit("tensor",
                          lambda e, o=out, l=lhsT, r=rhs, st=start, sp=stop:
                          e.matmul(o, l, r, start=st, stop=sp),
                          waits=waits, inc=inc)

        def vop(fn, waits=()):
            return b.emit("vector", fn, waits=waits, inc=eV)

        def aop(fn, waits=()):
            return b.emit("scalar", fn, waits=waits, inc=eA)

        def xh_(dc):
            return xh[:, dc * HALF:(dc + 1) * HALF]

        def xbf_(dc):
            return xbf[:, dc * HALF:(dc + 1) * HALF]

        def xln_(dc):
            return xln[:, dc * HALF:(dc + 1) * HALF]

        def ar_(fc):
            return arena[:, fc * HALF:(fc + 1) * HALF]

        def slab_tile(buf, off, idx, width):
            return buf[:, off:off + NDC * width].rearrange(
                "p (c f) -> p c f", f=width)[:, idx, :]

        dma("sync", msk[:], cmask, dC)
        dma("sync", onc[:], cones_col, dC)
        dma("sync", onr[:], cones_row, dC)
        dma("sync", qbs[:], qb, dC)
        dma("sync", kbs[:], kb, dC)
        dma("sync", vbs[:], vb, dC)
        dma("sync", obs[:], ob, dC)
        dma("sync", idn[:], cident, dC)
        t_wbres = dma("gpsimd", wbres[:].rearrange("p (c n) -> p c n", n=NH),
                      wbt.rearrange("(c p) n -> p c n", p=128), sBR)
        t_bbrow = dma("sync", sc1[0:1, 0:NH], bb, dC)
        t_consts = t_bbrow
        for pi_, wsrc_ in enumerate((wq, wk, wog, wv, wo)):
            dma("gpsimd", wcache[DM * pi_:DM * (pi_ + 1), :], wsrc_,
                sem["sWF0"])
        cbar = [(dC, dC.n), (sBR, sBR.n)]
        for _e in ENGINES:
            b.emit(_e, lambda e: e.nop(), waits=cbar)
        t = mm(PS3[:, 0:NH], onr[:], sc1[0:1, 0:NH], True, True, inc=eT)
        t_bbs = vop(lambda e: e.tensor_copy(bbs[:], PS3[:, 0:NH]), waits=[(eT, t)])

        if fused:
            dma("sync", r88[:], crow88, dC)
            dma("sync", c88[:], cone88, dC)
            dma("sync", m12row[:], cm12, dC)
            dma("sync", m3row[:], cm3, dC)
            dma("sync", nmid[:], cnmid, dC)
            qt_ = (lambda: arena[0:88, 0:16384],
                   lambda: arena[0:88, 0:16384])
            scalfs = (wf32[0:88, 0:2048], wf32[0:88, 2048:4096])

            def q_fullpass(src, scaldst, t0=None, t0p=None):
                tprev = [t0, t0p]
                tv = tvp = None
                for t in range(8):
                    par = t % 2
                    w = ([(eV, tprev[par])] if tprev[par] is not None
                         else [(dC, dC.n)])
                    tl = dma("sync" if par == 0 else "scalar", qt_[par](),
                             src.rearrange("a bx -> (a bx)")
                             .rearrange("(t p f) -> t p f", t=8, p=88)[t, :, :],
                             sQ[par], waits=w)
                    fn = (lambda e, par=par, t=t: e.tensor_reduce(
                        scaldst[:, t * 256:(t + 1) * 256],
                        qt_[par]().rearrange("p (c z) -> p c z", z=64),
                        axis=mybir.AxisListType.X, op=ALU.max,
                        apply_absolute_value=True))
                    tv = vop(fn, waits=[(sQ[par], tl)])
                    tprev[par] = tv
                return tv, tvp

            def q_shard_scalers(rsrc, ntiles, dst, t0=None):
                tprev = t0
                tv = None
                for t in range(ntiles):
                    w = [(eV, tprev)] if tprev is not None else [(dC, dC.n)]
                    tl = dma("sync", qt_[0](),
                             rsrc.rearrange("a bx -> (a bx)")
                             .rearrange("(t p f) -> t p f", t=ntiles, p=88)[t, :, :],
                             sQ[0], waits=w)
                    tv = vop(lambda e, t=t: e.tensor_reduce(
                        dst[:, t * 256:(t + 1) * 256],
                        qt_[0]().rearrange("p (c z) -> p c z", z=64),
                        axis=mybir.AxisListType.X, op=ALU.max,
                        apply_absolute_value=True), waits=[(sQ[0], tl)])
                    tprev = tv
                return tv

            scal12k = wf32[0:88, 0:512]
            scal3k = wf32[0:88, 512:768]
            ts12 = q_shard_scalers(rw12, 2, scal12k)
            ts3 = q_shard_scalers(rw3, 1, scal3k, t0=ts12)

            rs12 = smalls[:, 12:13]
            rs3 = smalls[:, 13:14]
            t1 = vop(lambda e: e.tensor_reduce(rs12, scal12k,
                                               axis=mybir.AxisListType.X,
                                               op=ALU.add), waits=[(eV, ts12)])
            t1b = vop(lambda e: e.tensor_reduce(rs3, scal3k,
                                                axis=mybir.AxisListType.X,
                                                op=ALU.add), waits=[(eV, ts3)])
            t2 = mm(PS3[0:1, 0:1], c88[:], rs12, True, True,
                    waits=[(eV, t1)], inc=eT)
            t2b = mm(PS3[0:1, 1:2], c88[:], rs3, True, True,
                     waits=[(eV, t1b)], inc=eT)
            t3 = vop(lambda e: e.tensor_copy(s11[0:1, 0:1], PS3[0:1, 0:1]),
                     waits=[(eT, t2)])
            t3b = vop(lambda e: e.tensor_copy(s11[0:1, 1:2], PS3[0:1, 1:2]),
                      waits=[(eT, t2b)])
            t4 = vop(lambda e: e.tensor_scalar(s11[0:1, 4:8], m12row[:],
                                               s11[0:1, 0:1], None, op0=ALU.mult),
                     waits=[(eV, t3)])
            t5 = vop(lambda e: e.scalar_tensor_tensor(s11[0:1, 4:8], m3row[:],
                                                      s11[0:1, 1:2],
                                                      s11[0:1, 4:8],
                                                      op0=ALU.mult, op1=ALU.add),
                     waits=[(eV, t4), (eV, t3b)])
            tcw = dma("gpsimd", ccin, s11[0:1, 4:8], dQW, waits=[(eV, t5)])
            tcc = b.emit("gpsimd",
                         lambda e: e.collective_compute(
                             "AllReduce", ALU.add,
                             replica_groups=[list(range(8))],
                             ins=[ccin.opt()], outs=[ccout.opt()]),
                         waits=[(dQW, tcw)], inc=ccS, k=1)
            tcr = dma("gpsimd", s11[0:1, 4:8], ccout, dQW, waits=[(ccS, tcc)])

            def q_mean(mrow, mcol_off, s11_off):
                t6 = vop(lambda e: e.tensor_tensor(s11b[0:1, 0:4].bitcast(F32)
                                                   [0:1, 0:2][0:1, 0:2]
                                                   if False else
                                                   smalls[0:1, 8:12],
                                                   s11[0:1, 4:8], mrow[:],
                                                   op=ALU.mult),
                         waits=[(dQW, tcr)])
                t7 = vop(lambda e: e.tensor_reduce(s11[0:1, s11_off:s11_off + 1],
                                                   smalls[0:1, 8:12],
                                                   axis=mybir.AxisListType.X,
                                                   op=ALU.add), waits=[(eV, t6)])
                t8 = vop(lambda e: e.tensor_scalar(s11[0:1, s11_off:s11_off + 1],
                                                   s11[0:1, s11_off:s11_off + 1],
                                                   1.0 / 180224.0, None,
                                                   op0=ALU.mult), waits=[(eV, t7)])
                t9 = vop(lambda e: e.tensor_copy(s11b[0:1, s11_off:s11_off + 1],
                                                 s11[0:1, s11_off:s11_off + 1]),
                         waits=[(eV, t8)])
                t10 = vop(lambda e: e.tensor_copy(s11[0:1, 2 + s11_off:3 + s11_off],
                                                  s11b[0:1, s11_off:s11_off + 1]),
                          waits=[(eV, t9)])
                t11 = mm(PS3[0:88, 1:2], r88[:], s11[0:1, 2 + s11_off:3 + s11_off],
                         True, True, waits=[(eV, t10)], inc=eT)
                t12 = vop(lambda e: e.tensor_copy(smalls[:, mcol_off:mcol_off + 1],
                                                  PS3[0:88, 1:2]), waits=[(eT, t11)])
                return t12

            tm12 = q_mean(m12row, 0, 0)
            tm3 = q_mean(m3row, 2, 1)
            tf3 = tm3

            def q_scaler_stage(scal_res, ntiles, mcol_off, tmean, dq_d, rc_d,
                               tprevload):
                mcol = smalls[:, mcol_off:mcol_off + 1]
                S = lambda a: sqq[:, a * 256:(a + 1) * 256]
                yv, tv, Fv, bb2, rcs = S(1), S(2), S(3), S(4), S(5)
                m1 = smalls[:, 8:9]
                qf = smalls[:, 9:10]
                rqf = smalls[:, 10:11]
                tlast = tprevload
                for st in range(ntiles):
                    scal = scal_res[:, st * 256:(st + 1) * 256]
                    t1 = tlast
                    t2 = vop(lambda e, scal=scal: e.tensor_scalar(
                        cents[:], scal, mcol, None, op0=ALU.subtract),
                        waits=[(eV, tmean), (dQW, dQW.n)] +
                              ([(eV, tlast)] if tlast is not None else []))
                    t3 = vop(lambda e: e.tensor_reduce(m1, cents[:],
                                                       axis=mybir.AxisListType.X,
                                                       op=ALU.max,
                                                       apply_absolute_value=True),
                             waits=[(eV, t2)])
                    t4 = vop(lambda e: e.reciprocal(qf, m1), waits=[(eV, t3)])
                    t5 = vop(lambda e: e.tensor_scalar(qf, qf, 128.0, None,
                                                       op0=ALU.mult),
                             waits=[(eV, t4)])
                    t6 = vop(lambda e: e.tensor_scalar(yv, cents[:], qf, None,
                                                       op0=ALU.mult),
                             waits=[(eV, t5)])
                    t7 = vop(lambda e: e.tensor_scalar(tv, yv, 128.5, None,
                                                       op0=ALU.add),
                             waits=[(eV, t6)])
                    t8 = vop(lambda e: e.memset(Fv, 0.0), waits=[(eV, t7)])
                    tk = t8
                    for k in range(8, -1, -1):
                        ka = vop(lambda e, k=k: e.tensor_scalar(
                            bb2, tv, float(2 ** k), None, op0=ALU.is_ge),
                            waits=[(eV, tk)])
                        kb = vop(lambda e, k=k: e.scalar_tensor_tensor(
                            tv, bb2, float(-(2 ** k)), tv, op0=ALU.mult,
                            op1=ALU.add), waits=[(eV, ka)])
                        tk = vop(lambda e, k=k: e.scalar_tensor_tensor(
                            Fv, bb2, float(2 ** k), Fv, op0=ALU.mult,
                            op1=ALU.add), waits=[(eV, kb)])
                    t9 = vop(lambda e: e.tensor_scalar(Fv, Fv, -128.0, 127.0,
                                                       op0=ALU.add, op1=ALU.min),
                             waits=[(eV, tk)])
                    t10 = vop(lambda e: e.tensor_scalar(Fv, Fv, -128.0, None,
                                                        op0=ALU.max),
                              waits=[(eV, t9)])
                    t11 = vop(lambda e: e.reciprocal(rqf, qf), waits=[(eV, t10)])
                    t12 = vop(lambda e: e.tensor_scalar(dqs[:], Fv, rqf, None,
                                                        op0=ALU.mult),
                              waits=[(eV, t11)])
                    t13 = vop(lambda e: e.tensor_scalar(dqs[:], dqs[:], mcol, None,
                                                        op0=ALU.add),
                              waits=[(eV, t12)])
                    t14 = vop(lambda e, scal=scal: e.reciprocal(rcs, scal),
                              waits=[(eV, t13)])
                    dma("gpsimd", dq_d.rearrange("a bx -> (a bx)")
                        [st * 22528:(st + 1) * 22528]
                        .rearrange("(p c) -> p c", p=88), dqs[:], dQW,
                        waits=[(eV, t13)])
                    dma("gpsimd", rc_d.rearrange("a bx -> (a bx)")
                        [st * 22528:(st + 1) * 22528]
                        .rearrange("(p c) -> p c", p=88), rcs, dQW,
                        waits=[(eV, t14)])
                    tlast = t14
                return tlast

            tq12 = q_scaler_stage(scal12k, 2, 0, tm12, dq12, rc12, tm3)
            tq3 = q_scaler_stage(scal3k, 1, 2, tm3, dq3, rc3, tq12)
            ndqw = dQW.n

            eP = sem["eP"]

            def q_row_stage(rsrc, nrt, width, dq_d, rc_d, agdst, t_prev,
                            s_prev=None, tp_prev=None, agcb=None):
                nbl = width // 64
                ndc = width // 128
                hw = width // 2
                hb = nbl // 2
                wrow_ = lambda par: wb3[:, par * width:(par + 1) * width]
                rbf_ = xbf[:, 0:width]
                accf_ = arena[:, 0:width]
                tmpf_ = xln[:, 0:width]
                wdq_ = mskball[:, 0:width]
                slab_ = arena[:, width:2 * width]
                dqr_ = lambda par: q2[:, par * nbl:(par + 1) * nbl]
                rcr_ = lambda par: rcr2[:, par * nbl:(par + 1) * nbl]
                tev_bank = [t_prev, t_prev]
                tstores = [s_prev] if s_prev else []
                tlw = [t_prev, t_prev]
                tlwp = [tp_prev, tp_prev]

                C0 = float(NF4[0]) + sum(GAP) * 0.5
                sgn_ok = False

                def halfops(eng, inc_s, c0, c1, b0, b1, par, tok, w0=()):
                    emitq = (lambda fn, waits: b.emit(eng, fn, waits=waits,
                                                      inc=inc_s))
                    nb = b1 - b0
                    t1 = emitq(lambda e, par=par: e.tensor_tensor(
                        rbf_[:, c0:c1].rearrange("p (c z) -> p c z", z=64),
                        wrow_(par)[:, c0:c1].rearrange("p (c z) -> p c z", z=64),
                        rcr_(par)[:, b0:b1].unsqueeze(2)
                        .broadcast_to([128, nb, 64]), op=ALU.mult),
                        [(sR[par], tok)] + list(w0))
                    if sgn_ok:
                        tmpb = (xln[:, width:2 * width],
                                xln[:, 2 * width:3 * width])
                        t2 = emitq(lambda e: e.memset(accf_[:, c0:c1], C0),
                                   [(inc_s, t1)])
                        tk = t2
                        tstt = {}
                        tsg = {}
                        for j in range(15):
                            wsg = [(eV, t1)]
                            if j >= 2:
                                wsg.append((eV, tstt[j - 2]))
                            tsg[j] = aop(lambda e, j=j: e.activation(
                                tmpb[j % 2][:, c0:c1],
                                rbf_[:, c0:c1], AF.Sign,
                                bias=nmid[:, j:j + 1]),
                                waits=wsg)
                            tk = emitq(lambda e, j=j: e.scalar_tensor_tensor(
                                accf_[:, c0:c1],
                                tmpb[j % 2][:, c0:c1], GAP[j] * 0.5,
                                accf_[:, c0:c1], op0=ALU.mult, op1=ALU.add),
                                [(eA, tsg[j]), (inc_s, tk)])
                            tstt[j] = tk
                    else:
                        t2 = emitq(lambda e: e.memset(accf_[:, c0:c1],
                                                      float(NF4[0])),
                                   [(inc_s, t1)])
                        tk = t2
                        for j in range(15):
                            ja = emitq(lambda e, j=j: e.tensor_scalar(
                                tmpf_[:, c0:c1], rbf_[:, c0:c1], MID[j], GAP[j],
                                op0=ALU.is_gt, op1=ALU.mult), [(inc_s, tk)])
                            tk = emitq(lambda e, j=j: e.tensor_tensor(
                                accf_[:, c0:c1], accf_[:, c0:c1],
                                tmpf_[:, c0:c1], op=ALU.add),
                                [(inc_s, ja)])
                    t3 = emitq(lambda e, par=par: e.tensor_tensor(
                        wdq_[:, c0:c1].rearrange("p (c z) -> p c z", z=64),
                        accf_[:, c0:c1].rearrange("p (c z) -> p c z", z=64),
                        dqr_(par)[:, b0:b1].unsqueeze(2)
                        .broadcast_to([128, nb, 64]), op=ALU.mult), [(inc_s, tk)])
                    return t3

                for t in range(nrt):
                    par = t % 2
                    w = [(dQW, ndqw)]
                    if t == 0:
                        w.append((dO, dO.n))
                    if tlw[par] is not None:
                        w.append((eV, tlw[par]))
                    if tlwp[par] is not None:
                        w.append((eP, tlwp[par]))
                    dma("sync", wrow_(par), rsrc[128 * t:128 * (t + 1), :],
                        sR[par], waits=w)
                    dma("sync", dqr_(par), dq_d[128 * t:128 * (t + 1), :], sR[par])
                    dma("sync", rcr_(par), rc_d[128 * t:128 * (t + 1), :], sR[par])
                    tok = sR[par].n
                    t3v = halfops("vector", eV, 0, width, 0, nbl, par, tok,
                                  w0=([(dO, dO.n)] if t == 0 else ()))
                    t3p = None
                    tlw[par] = t3v
                    tlwp[par] = None
                    tev = None
                    first_evac = True
                    for dc in range(ndc):
                        bank = dc % 2
                        slot = bank * 1024
                        w2 = [(eV, t3v)]
                        if tev_bank[bank] is not None:
                            w2.append((eV, tev_bank[bank]))
                        tt_ = b.emit("tensor",
                                     lambda e, dc=dc, slot=slot: e.transpose(
                                         PT[:, slot:slot + 128],
                                         wdq_[:, dc * 128:(dc + 1) * 128], idn[:]),
                                     waits=w2, inc=eT)
                        w3_ = [(eT, tt_)]
                        if first_evac and tstores:
                            w3_.append((dO, tstores[-1]))
                            first_evac = False
                        tev = vop(lambda e, dc=dc, slot=slot: e.tensor_copy(
                            slab_[:, dc * 128:(dc + 1) * 128],
                            PT[:, slot:slot + 128]), waits=w3_)
                        tev_bank[bank] = tev
                    ts_ = dma("gpsimd", agdst[128 * t:128 * (t + 1), :],
                              slab_[:, 0:width], dO, waits=[(eV, tev)])
                    tstores.append(ts_)
                    if agcb is not None:
                        agcb(t, ts_)
                return tstores, tlw, tlwp

            def ag1cb(t, ts_):
                b.emit("gpsimd",
                       lambda e, t=t: e.collective_compute(
                           "AllGather", ALU.bypass,
                           replica_groups=[list(range(8))],
                           ins=[ag1s[128 * t:128 * (t + 1), :].opt()],
                           outs=[wag12[1024 * t:1024 * (t + 1), :].opt()]),
                       waits=[(dO, ts_)], inc=ccS, k=1)

            def ag2cb(t, ts_):
                b.emit("gpsimd",
                       lambda e, t=t: e.collective_compute(
                           "AllGather", ALU.bypass,
                           replica_groups=[list(range(8))],
                           ins=[ag2s[128 * t:128 * (t + 1), :].opt()],
                           outs=[wag3[1024 * t:1024 * (t + 1), :].opt()]),
                       waits=[(dO, ts_)], inc=ccS, k=1)

            st1, tl1, tp1 = q_row_stage(rw12, W12_FC, DM, dq12, rc12, ag1s, tq3,
                                        agcb=ag1cb)
            q_row_stage(rw3, 2, DFF, dq3, rc3, ag2s, eV.n, s_prev=st1[-1],
                        tp_prev=max(t for t in tp1 if t is not None)
                        if any(tp1) else None, agcb=ag2cb)
            qbar = [(ss_, ss_.n) for ss_ in sem.values()]
            for _e in ENGINES:
                b.emit(_e, lambda e: e.nop(), waits=qbar)



        def halfprog(h):
            if h > 0:
                bar = [(ss, ss.n) for ss in sem.values()]
                for eng in ("sync", "tensor", "vector", "scalar", "gpsimd"):
                    b.emit(eng, lambda e: e.nop(), waits=bar)
            for dc in range(NDC):
                dma("sync", xh_(dc), x_in[ts_(dc), h * HALF:(h + 1) * HALF], dX)
            xtok = [(dX, dX.n)] * NDC

            def layernorm(xtok, extra_xbf=()):
                tb = []
                for dc in range(NDC):
                    w = [xtok[dc]] + list(extra_xbf)
                    tb.append(vop(lambda e, dc=dc: e.tensor_copy(xbf_(dc), xh_(dc)),
                                  waits=w))
                if "stats" in LN_SKIP:
                    tstat = b.emit("tensor", lambda e: e.nop(),
                                   waits=[(eV, tb[-1])], inc=eT)
                else:
                    tsq_consumed = None
                    for dc in range(NDC):
                        w = [(eV, tb[dc])]
                        if tsq_consumed is not None:
                            w.append((eT, tsq_consumed))
                        tq = vop(lambda e, dc=dc: e.tensor_tensor(
                            lntmp[:], xbf_(dc), xbf_(dc), op=ALU.mult), waits=w)
                        mm(PS1[0:1, 0:512], onc[:], xbf_(dc), dc == 0, dc == NDC - 1,
                           waits=[(eV, tb[dc])])
                        tsq_consumed = mm(PS2[0:1, 0:512], onc[:], lntmp[:],
                                          dc == 0, dc == NDC - 1,
                                          waits=[(eV, tq)], inc=eT)
                    tstat = tsq_consumed
                S = lambda a: sc1[0:1, a * 512:(a + 1) * 512]
                mean, ex2, m2, ve = S(0), S(1), S(2), S(3)
                r, t_, mrs = S(2), S(1), S(1)
                if "scal" in LN_SKIP:
                    tz0 = vop(lambda e: e.memset(bcB[:], 1.0), waits=[(eT, tstat)])
                    tBc = tCc = vop(lambda e: e.memset(bcC[:], 0.0),
                                    waits=[(eV, tz0)])
                    txl = []
                    if "apply" in LN_SKIP:
                        return tb
                    prev = tCc
                    for dc in range(NDC):
                        ta = vop(lambda e, dc=dc: e.tensor_tensor(
                            lntmp[:], xbf_(dc), bcB[:], op=ALU.mult),
                            waits=[(eV, tBc), (eV, prev)])
                        prev = vop(lambda e, dc=dc: e.tensor_tensor(
                            xln_(dc), lntmp[:], bcC[:], op=ALU.subtract),
                            waits=[(eV, ta)])
                        txl.append(prev)
                    return txl
                t1 = vop(lambda e: e.tensor_copy(mean, PS1[0:1, 0:512]),
                         waits=[(eT, tstat)])
                t2 = vop(lambda e: e.tensor_copy(ex2, PS2[0:1, 0:512]),
                         waits=[(eT, tstat)])
                t3 = vop(lambda e: e.tensor_scalar(mean, mean, 1.0 / DM, None,
                                                   op0=ALU.mult), waits=[(eV, t1)])
                t4 = vop(lambda e: e.tensor_scalar(ex2, ex2, 1.0 / DM, None,
                                                   op0=ALU.mult), waits=[(eV, t2)])
                t5 = vop(lambda e: e.tensor_tensor(m2, mean, mean, op=ALU.mult),
                         waits=[(eV, t3)])
                t6 = vop(lambda e: e.scalar_tensor_tensor(ve, ex2, EPS_LN, m2,
                                                          op0=ALU.add,
                                                          op1=ALU.subtract),
                         waits=[(eV, t4), (eV, t5)])
                if "act" in LN_SKIP:
                    t8v = vop(lambda e: e.memset(r, 1.0), waits=[(eV, t6)])
                    last = (eV, t8v)
                else:
                    t7 = aop(lambda e: e.activation(t_, ve, AF.Ln), waits=[(eV, t6)])
                    t8 = aop(lambda e: e.activation(r, t_, AF.Exp, scale=-0.5),
                             waits=[(eA, t7)])
                    last = (eA, t8)
                for _ in range(0 if "newton" in LN_SKIP else 2):
                    ta = vop(lambda e: e.tensor_tensor(t_, r, r, op=ALU.mult),
                             waits=[last])
                    tb2 = vop(lambda e: e.tensor_tensor(t_, t_, ve, op=ALU.mult),
                              waits=[(eV, ta)])
                    tc = vop(lambda e: e.tensor_scalar(t_, t_, -0.5, 1.5,
                                                       op0=ALU.mult, op1=ALU.add),
                             waits=[(eV, tb2)])
                    td = vop(lambda e: e.tensor_tensor(r, r, t_, op=ALU.mult),
                             waits=[(eV, tc)])
                    last = (eV, td)
                tm = vop(lambda e: e.tensor_tensor(mrs, mean, r, op=ALU.mult),
                         waits=[last])
                tB = mm(PS3[:, 0:512], onr[:], r, True, True,
                        waits=[(eV, tm)], inc=eT)
                tC = mm(PS3[:, 512:1024], onr[:], mrs, True, True,
                        waits=[(eV, tm)], inc=eT)
                tBc = vop(lambda e: e.tensor_copy(bcB[:], PS3[:, 0:512]),
                          waits=[(eT, tB)])
                tCc = vop(lambda e: e.tensor_copy(bcC[:], PS3[:, 512:1024]),
                          waits=[(eT, tC)])
                txl = []
                prev = tCc
                for dc in range(NDC):
                    ta = vop(lambda e, dc=dc: e.tensor_tensor(lntmp[:], xbf_(dc),
                                                              bcB[:], op=ALU.mult),
                             waits=[(eV, tBc), (eV, prev)])
                    prev = vop(lambda e, dc=dc: e.tensor_tensor(
                        xln_(dc), lntmp[:], bcC[:], op=ALU.subtract),
                        waits=[(eV, ta)])
                    txl.append(prev)
                return txl

            def ffn(txl, extra_arena=(), extra_wb3=(), extra_wAB=()):
                tslabA = [None, None]
                tsilu = [None] * NFC
                tmmA = [None] * NFC
                for fc in range(NFC):
                    r_, lfc = fc // W12_FC, fc % W12_FC
                    par = fc % 2
                    w = ([(eT, tmmA[fc - 2])] if fc >= 2 else
                         list(extra_wAB) + [(sCC[0], sCC[0].n)])
                    tslabA[par] = dma("sync", wbA[:, par * DM:(par + 1) * DM],
                                      wag12[(lfc * 8 + r_) * 128:
                                            (lfc * 8 + r_) * 128 + 128, :],
                                      sWA[par], waits=w)
                    bank = PS1[:, par * 512:(par + 1) * 512]
                    for dc in range(NDC):
                        wts = [(sWA[par], tslabA[par]), (eV, txl[dc])]
                        if dc == 0 and fc >= 2 and tsilu[fc - 2] is not None:
                            wts.append((eA, tsilu[fc - 2]))
                        tk = mm(bank, slab_tile(wbA, par * DM, dc, 128),
                                xln_(dc), dc == 0, dc == NDC - 1, waits=wts,
                                inc=eT if dc == NDC - 1 else None)
                    tmmA[fc] = tk
                    wts = [(eT, tmmA[fc])] + list(extra_arena)
                    tsilu[fc] = aop(lambda e, fc=fc, bank=bank: e.activation(
                        ar_(fc), bank, AF.Silu), waits=wts)
                tmmB = [None] * NFC
                th = [None] * NFC
                for fc in range(NFC):
                    r_, lfc = fc // W12_FC, fc % W12_FC
                    par = fc % 2
                    w = ([(eT, tmmB[fc - 2])] if fc >= 2 else
                         list(extra_wAB) + [(sCC[1], sCC[1].n)])
                    tsb = dma("sync", wbB[:, par * DM:(par + 1) * DM],
                              wag12[(lfc * 8 + r_ + 4) * 128:
                                    (lfc * 8 + r_ + 4) * 128 + 128, :],
                              sWB[par], waits=w)
                    bank = PS2[:, par * 512:(par + 1) * 512]
                    for dc in range(NDC):
                        wts = [(sWB[par], tsb), (eV, txl[dc])]
                        if dc == 0 and fc >= 2 and th[fc - 2] is not None:
                            wts.append((eV, th[fc - 2]))
                        tk = mm(bank, slab_tile(wbB, par * DM, dc, 128),
                                xln_(dc), dc == 0, dc == NDC - 1, waits=wts,
                                inc=eT if dc == NDC - 1 else None)
                    tmmB[fc] = tk
                    th[fc] = vop(lambda e, fc=fc, bank=bank: e.tensor_tensor(
                        ar_(fc), ar_(fc), bank, op=ALU.mult),
                        waits=[(eT, tmmB[fc]), (eA, tsilu[fc])])
                tmmC = [None] * NDC
                txn = [None] * NDC
                for dc in range(NDC):
                    r_, dcl = dc // 2, dc % 2
                    par = dc % 2
                    w = ([(eT, tmmC[dc - 2])] if dc >= 2 else []) + list(extra_wb3)
                    ts3 = dma("sync", wb3[:, par * DFF:(par + 1) * DFF],
                              wag3[(dcl * 8 + r_) * 128:
                                   (dcl * 8 + r_) * 128 + 128, :],
                              sW3[par], waits=w)
                    bank = PS3[:, par * 512:(par + 1) * 512]
                    for fc in range(NFC):
                        wts = [(sW3[par], ts3), (eV, th[fc])]
                        if fc == 0 and dc >= 2 and txn[dc - 2] is not None:
                            wts.append((eV, txn[dc - 2]))
                        tk = mm(bank,
                                wb3[:, par * DFF:(par + 1) * DFF].rearrange(
                                    "p (c f) -> p c f", f=128)[:, fc, :],
                                ar_(fc), fc == 0, fc == NFC - 1, waits=wts,
                                inc=eT if fc == NFC - 1 else None)
                    tmmC[dc] = tk
                    txn[dc] = vop(lambda e, dc=dc, bank=bank: e.scalar_tensor_tensor(
                        xh_(dc), bank, 0.5, xh_(dc), op0=ALU.mult, op1=ALU.add),
                        waits=[(eT, tmmC[dc])])
                return txn, tmmC[NDC - 1]

            def attention(txl, extra_arena, t_wAB_free, sub="all"):
                if sub == "ln2":
                    tdump = []
                    for dc in range(NDC):
                        tdump.append(vop(lambda e, dc=dc: e.tensor_copy(
                            xh_(dc), xln_(dc)),
                            waits=[(eV, eV.n), (eA, eA.n), (eT, eT.n)]))
                    return tdump, eT.n, eT.n
                qn_ = lambda oc: arena[:, 0 * 8192 + oc * HALF: 0 * 8192 + (oc + 1) * HALF]
                gate_ = lambda oc: arena[:, 8192 + oc * HALF: 8192 + (oc + 1) * HALF]
                kn_ = lambda oc: xbf_(oc)
                gv_ = lambda sub: wb3[:, sub * DM:(sub + 1) * DM]

                tbeta = []
                tprev = None
                for bsub in range(min(NSUB, BETA_SUBS)):
                    if not NOMM:
                        for dc in range(NDC):
                            w = [(eV, txl[dc])]
                            if dc == 0 and tprev is not None:
                                w.append((eV, tprev))
                            tz = mm(PS3[:, 0:NH],
                                    xln[:, dc * HALF + bsub * 128:dc * HALF + (bsub + 1) * 128],
                                    wbres[:].rearrange("p (c n) -> p c n", n=NH)[:, dc, :],
                                    dc == 0, dc == NDC - 1,
                                    waits=w, inc=eT if dc == NDC - 1 else None)
                    bsl = beta[:, bsub * NH:(bsub + 1) * NH]
                    if NOMM:
                        t0 = vop(lambda e, bsl=bsl: e.memset(bsl, 0.25),
                                 waits=([(eV, tprev)] if tprev else []))
                    else:
                        t0 = vop(lambda e, bsl=bsl: e.tensor_tensor(
                            bsl, PS3[:, 0:NH], bbs[:], op=ALU.add),
                            waits=[(eT, tz), (eV, t_bbs)])
                    if NOACT:
                        tprev = t0
                        tbeta.append(tprev)
                        continue
                    ta1 = aop(lambda e, bsl=bsl: e.activation(bscr[:], bsl, AF.Abs),
                              waits=[(eV, t0)])
                    ta2 = aop(lambda e: e.activation(bscr2[:], bscr[:], AF.Exp,
                                                     scale=-1.0), waits=[(eA, ta1)])
                    ta3 = aop(lambda e: e.activation(bscr[:], bscr2[:], AF.Ln,
                                                     bias=1.0), waits=[(eA, ta2)])
                    tprev = vop(lambda e, bsl=bsl: e.scalar_tensor_tensor(
                        bsl, bsl, 0.0, bscr[:], op0=ALU.max, op1=ALU.add),
                        waits=[(eA, ta3)])
                    tbeta.append(tprev)
                t_beta_all = tprev

                if sub == "beta":
                    tdump = []
                    for dc in range(NDC):
                        tdump.append(vop(lambda e, dc=dc: e.tensor_copy(
                            xh_(dc), xln_(dc)),
                            waits=[(eV, eV.n), (eA, eA.n), (eT, eT.n)]))
                    return tdump, eT.n, eT.n

                passes = [("q", wq, qbs, qn_), ("k", wk, kbs, kn_),
                          ("og", wog, None, gate_), ("v", wv, vbs, None)]
                if sub == "qk":
                    passes = passes[:2]
                SLAB4 = (lambda: wbA[:, 0:DM], lambda: wbA[:, DM:2 * DM],
                         lambda: wbB[:, 0:DM], lambda: wbB[:, DM:2 * DM])
                SSEM4 = (sWA[0], sWA[1], sWB[0], sWB[1])
                BANK4 = (lambda: PS2[:, 0:512], lambda: PS2[:, 512:1024],
                         lambda: PS1[:, 0:512], lambda: PS1[:, 512:1024])
                tslab = [None] * 4
                tconv = [None] * 4
                tgrp = [None] * 4
                tevac = [None] * 4
                normq = {}
                last_tss = [None]
                last_tn = [None]
                tgv = [None] * (NSUB * NH)
                tnorm_done = {}
                for pi, (pname, wsrc, bias, dst) in enumerate(passes):
                    for oc in range(NDC):
                        par = oc % 4
                        wpar = oc % 2
                        gi = pi * NDC + oc
                        w = [(eT, tgrp[par]) if tgrp[par] is not None
                             else (eT, t_wAB_free),
                             (sem["sWF0"], 80)]
                        tl = dma("scalar" if par < 2 else "sync", SLAB4[par](),
                                 wcache[DM * pi + 128 * oc:
                                        DM * pi + 128 * (oc + 1), :],
                                 SSEM4[par], waits=w)
                        slab = SLAB4[par]()
                        tslab_tok = (SSEM4[par], tl)
                        bank = BANK4[par]()
                        for kc in range(NDC):
                            wts = [tslab_tok, (eV, txl[kc])]
                            if kc == 0 and tevac[par] is not None:
                                wts.append(tevac[par])
                            tg = mm(bank, slab.rearrange("p (c f) -> p c f",
                                                         f=128)[:, kc, :],
                                    xln_(kc), kc == 0, kc == NDC - 1, waits=wts,
                                    inc=eT if kc == NDC - 1 else None)
                        tgrp[par] = tg
                        if pname in ("q", "k"):
                            dtile = dst(oc)
                            wts = [(eT, tg)]
                            if pname == "q":
                                wts += list(extra_arena)
                            te = aop(lambda e, dtile=dtile, bank=bank, bias=bias,
                                     oc=oc: e.activation(
                                dtile, bank, AF.Identity, bias=bias[:, oc:oc + 1]),
                                waits=wts)

                            def norm_stage1(st):
                                tss = mm(PS3[0:1, 512:1024], onc[:], q2[:],
                                         True, True,
                                         waits=[(eV, st["tq2"])], inc=eT)
                                st["tss"] = tss
                                S = lambda a: sc1[0:1, a * 512:(a + 1) * 512]
                                ss, tt = S(0), S(1)
                                rr = S(2 + st["oc"] % 2)
                                t1 = vop(lambda e, ss=ss: e.tensor_copy(
                                    ss, PS3[0:1, 512:1024]), waits=[(eT, tss)])
                                t7 = aop(lambda e, ss=ss, tt=tt: e.activation(
                                    tt, ss, AF.Ln), waits=[(eV, t1)])
                                t8 = aop(lambda e, rr=rr, tt=tt: e.activation(
                                    rr, tt, AF.Exp, scale=-0.5), waits=[(eA, t7)])
                                last = (eA, t8)
                                for _ in range(2):
                                    ta = vop(lambda e, rr=rr, tt=tt:
                                             e.tensor_tensor(tt, rr, rr,
                                                             op=ALU.mult),
                                             waits=[last])
                                    tb2 = vop(lambda e, ss=ss, tt=tt:
                                              e.tensor_tensor(tt, tt, ss,
                                                              op=ALU.mult),
                                              waits=[(eV, ta)])
                                    tc2 = vop(lambda e, tt=tt: e.tensor_scalar(
                                        tt, tt, -0.5, 1.5, op0=ALU.mult,
                                        op1=ALU.add), waits=[(eV, tb2)])
                                    td = vop(lambda e, rr=rr, tt=tt:
                                             e.tensor_tensor(rr, rr, tt,
                                                             op=ALU.mult),
                                             waits=[(eV, tc2)])
                                    last = (eV, td)
                                st["rr"] = rr
                                st["tnewt"] = last

                            def norm_stage2(st):
                                w_ = [st["tnewt"]]
                                if last_tn[0] is not None:
                                    w_.append((eV, last_tn[0]))
                                tfb = mm(PS3[:, 0:512], onr[:], st["rr"],
                                         True, True, waits=w_, inc=eT)
                                tn = vop(lambda e, dtile=st["dtile"]:
                                         e.tensor_tensor(
                                             dtile, dtile, PS3[:, 0:512],
                                             op=ALU.mult), waits=[(eT, tfb)])
                                tnorm_done[(st["pname"], st["oc"])] = tn
                                last_tn[0] = tn
                                return tn

                            if oc >= 1:
                                norm_stage1(normq[oc - 1])
                                last_tss[0] = normq[oc - 1]["tss"]
                            tq2 = vop(lambda e, dtile=dtile: e.tensor_tensor(
                                q2[:], dtile, dtile, op=ALU.mult),
                                waits=[(eA, te)] +
                                      ([(eT, last_tss[0])]
                                       if last_tss[0] is not None else []))
                            normq[oc] = {"dtile": dtile, "tq2": tq2, "oc": oc,
                                         "pname": pname}
                            if oc >= 2:
                                tevac[(oc - 2) % 4] = (eV,
                                                       norm_stage2(normq[oc - 2]))
                            if oc == NDC - 1:
                                norm_stage1(normq[oc])
                                last_tss[0] = normq[oc]["tss"]
                                tevac[(oc - 1) % 4] = (eV,
                                                       norm_stage2(normq[oc - 1]))
                                tevac[oc % 4] = (eV, norm_stage2(normq[oc]))
                                normq.clear()
                        elif pname == "og":
                            te = aop(lambda e, oc=oc, bank=bank: e.activation(
                                gate_(oc), bank, AF.Silu),
                                waits=[(eT, tg)] + list(extra_arena))
                            tevac[par] = (eA, te)
                        else:
                            vp = oc % 2
                            te = aop(lambda e, bank=bank, oc=oc, vp=vp: e.activation(
                                vt2[:, vp * HALF:(vp + 1) * HALF], bank, AF.Identity,
                                bias=vbs[:, oc:oc + 1]), waits=[(eT, tg)])
                            for vsub in range(NSUB):
                                g = oc * NSUB + vsub
                                slot = (g % 2) * 1024
                                w2 = [(eA, te)]
                                if g >= 2:
                                    w2.append((eV, tgv[g - 2]))
                                if NO_TRANSPOSE:
                                    tt_ = b.emit("tensor", lambda e: e.nop(),
                                                 waits=w2, inc=eT)
                                else:
                                    if TR_MODE == "fixed_in":
                                        tt_ = b.emit("tensor",
                                                     lambda e, slot=slot:
                                                     e.transpose(PT[:, slot:slot + 128],
                                                                 idn[:], idn[:]),
                                                     waits=w2, inc=eT)
                                    elif TR_MODE == "slot0":
                                        tt_ = b.emit("tensor",
                                                     lambda e, vsub=vsub, vp=vp:
                                                     e.transpose(
                                                         PT[:, 0:128],
                                                         vt2[:, vp * HALF + vsub * 128:
                                                             vp * HALF + (vsub + 1) * 128],
                                                         idn[:]),
                                                     waits=w2, inc=eT)
                                    else:
                                        tt_ = b.emit("tensor",
                                                     lambda e, vsub=vsub, vp=vp, slot=slot:
                                                     e.transpose(
                                                         PT[:, slot:slot + 128],
                                                         vt2[:, vp * HALF + vsub * 128:
                                                             vp * HALF + (vsub + 1) * 128],
                                                         idn[:]),
                                                     waits=w2, inc=eT)
                                if GV_PLAIN_COPY:
                                    tgv[oc * NSUB + vsub] = vop(
                                        lambda e, vsub=vsub, oc=oc, slot=slot:
                                        e.tensor_copy(
                                            gv_(vsub)[:, oc * 128:(oc + 1) * 128],
                                            PT[:, slot:slot + 128]),
                                        waits=[(eT, tt_), (eV, t_beta_all)])
                                else:
                                    tgv[oc * NSUB + vsub] = vop(
                                        lambda e, vsub=vsub, oc=oc, slot=slot:
                                        e.tensor_scalar(
                                            gv_(vsub)[:, oc * 128:(oc + 1) * 128],
                                            PT[:, slot:slot + 128],
                                            beta[:, vsub * NH + oc: vsub * NH + oc + 1],
                                            None, op0=ALU.mult),
                                        waits=[(eT, tt_), (eV, t_beta_all)])
                            tevac[par] = (eA, te)
                t_gv_all = eV.n
                t_kq_all = eV.n

                if sub == "qk" and QK_TRANSPOSES:
                    tlast = None
                    for i_ in range(8):
                        tlast = b.emit("tensor",
                                       lambda e, i_=i_: e.transpose(
                                           PT[:, i_ * 128:(i_ + 1) * 128],
                                           idn[:], idn[:]),
                                       waits=[(eT, eT.n), (eV, eV.n), (eA, eA.n)],
                                       inc=eT)
                if sub in ("proj", "qk"):
                    tdump = []
                    for dc in range(NDC):
                        tdump.append(vop(lambda e, dc=dc: e.tensor_copy(
                            xh_(dc), qn_(dc)),
                            waits=[(eV, eV.n), (eA, eA.n), (eT, eT.n)]))
                    return tdump, eT.n, eT.n

                t_projV, t_projA = eV.n, eA.n
                tmsk = [[None, None] for _ in range(NSUB)]
                for sp in range(NSUB):
                    for ic in range(2):
                        ps = PS1 if ic == 0 else PS2
                        for hh in range(NH):
                            w = [(eV, tnorm_done[("q", hh)]),
                                 (eV, tnorm_done[("k", hh)])]
                            if hh == 0 and sp == 0:
                                w += [(eV, t_projV), (eA, t_projA)]
                            if hh == 0 and sp >= 1:
                                w.append((eV, tmsk[sp - 1][ic]))
                            tsc = mm(ps[:, hh * CH:(hh + 1) * CH],
                                     kn_(hh)[:, sp * 128:(sp + 1) * 128],
                                     qn_(hh)[:, sp * 128 + ic * CH:
                                             sp * 128 + (ic + 1) * CH],
                                     True, True, waits=w,
                                     inc=eT if hh == NH - 1 else None)
                        mb = msk[:, ic * CH:(ic + 1) * CH].unsqueeze(1) \
                            .broadcast_to([128, NH, CH])
                        tmsk[sp][ic] = vop(
                            lambda e, sp=sp, ic=ic, ps=ps, mb=mb: e.tensor_tensor(
                                mskball[:, sp * NH * 128:(sp + 1) * NH * 128]
                                .rearrange("p (n i) -> p n i", i=128)
                                [:, :, ic * CH:(ic + 1) * CH],
                                ps[:, 0:NH * CH].rearrange("p (n i) -> p n i", i=CH),
                                mb, op=ALU.mult), waits=[(eT, tsc)])
                t_sc_last = eT.n

                tgat = [None] * NH
                for hh in range(NH):
                    par = hh % 2
                    bank = PS3[:, par * 512:(par + 1) * 512]
                    for sp in range(NSUB):
                        w = [(eV, tmsk[sp][0]), (eV, tmsk[sp][1]), (eV, t_gv_all)]
                        if sp == 0 and hh < 2:
                            w.append((eV, t_projV))
                        if sp == 0 and hh >= 2 and tgat[hh - 2] is not None:
                            w.append((eV, tgat[hh - 2]))
                        ta = mm(bank[:, sp * 128:(sp + 1) * 128],
                                gv_(sp)[:, hh * 128:(hh + 1) * 128],
                                mskball[:, sp * NH * 128 + hh * 128:
                                        sp * NH * 128 + (hh + 1) * 128],
                                True, True, waits=w,
                                inc=eT if sp == NSUB - 1 else None)
                    tgat[hh] = vop(lambda e, hh=hh, bank=bank: e.tensor_tensor(
                        gate_(hh), bank, gate_(hh), op=ALU.mult),
                        waits=[(eT, ta)])

                if sub == "score":
                    tdump = []
                    for dc in range(NDC):
                        tdump.append(vop(lambda e, dc=dc: e.tensor_copy(
                            xh_(dc), gate_(dc)),
                            waits=[(eV, eV.n), (eA, eA.n), (eT, eT.n)]))
                    return tdump, eT.n, eT.n

                tslab2 = [None, None]
                tgrp2 = [None] * 4
                txn = [None] * NDC
                for oc in range(NDC):
                    par = oc % 4
                    wpar = oc % 2
                    w = [(eT, tgrp2[par]) if tgrp2[par] is not None
                         else (eT, tgrp[par])]
                    tl = dma("scalar" if par < 2 else "sync", SLAB4[par](),
                             wcache[DM * 4 + 128 * oc:DM * 4 + 128 * (oc + 1), :],
                             SSEM4[par], waits=w)
                    slab = SLAB4[par]()
                    ttok = (SSEM4[par], tl)
                    bank = BANK4[par]()
                    for kc in range(NDC):
                        w = [ttok, (eV, tgat[kc])]
                        if kc == 0 and oc >= 4 and txn[oc - 4] is not None:
                            w.append((eV, txn[oc - 4]))
                        tg = mm(bank, slab.rearrange("p (c f) -> p c f",
                                                     f=128)[:, kc, :],
                                gate_(kc), kc == 0, kc == NDC - 1, waits=w,
                                inc=eT if kc == NDC - 1 else None)
                    tgrp2[par] = tg
                    txn[oc] = vop(lambda e, oc=oc, bank=bank: e.scalar_tensor_tensor(
                        xh_(oc), bank, obs[:, oc:oc + 1], xh_(oc),
                        op0=ALU.add, op1=ALU.add),
                        waits=[(eT, tg)])
                t_arena_free = eT.n
                return txn, t_sc_last, t_arena_free

            txl = layernorm(xtok)
            if stage == "ln1":
                tst = []
                for dc in range(NDC):
                    t = vop(lambda e, dc=dc: e.tensor_copy(xh_(dc), xln_(dc)),
                            waits=[(eV, txl[dc])])
                    tst.append(t)
                txn = tst
            elif stage == "betaonly":
                txn, t_sc_last, t_ar_free = attention(
                    txl, extra_arena=[], t_wAB_free=0, sub="beta")
            else:
                txn, t_ffn_last = ffn(txl)
                if stage not in ("ffn1",):
                    txl2 = layernorm([(eV, t) for t in txn],
                                     extra_xbf=[(eT, t_ffn_last)])
                    asub = {"attnp": "proj", "attns": "score", "attnb": "beta",
                            "attnq": "qk", "attnl": "ln2"}.get(stage, "all")
                    txn, t_sc_last, t_ar_free = attention(
                        txl2, extra_arena=[(eT, t_ffn_last)],
                        t_wAB_free=t_ffn_last, sub=asub)
                    if stage == "full" and asub == "all":
                        txl3 = layernorm([(eV, t) for t in txn],
                                         extra_xbf=[(eT, t_sc_last)])
                        txn, _ = ffn(txl3,
                                     extra_arena=[(eT, t_ar_free)],
                                     extra_wb3=[(eT, t_ar_free)],
                                     extra_wAB=[(eT, t_ar_free),
                                                (sem["sCA"], sem["sCA"].n),
                                                (sem["sCB"], sem["sCB"].n)])
            tst = []
            for dc in range(NDC):
                t = dma("gpsimd", y_out[ts_(dc), h * HALF:(h + 1) * HALF], xh_(dc),
                        dP, waits=[(eV, txn[dc])])
                tst.append(t)
            return tst

        halfprog(0)
        halfprog(1)
        b.emit("gpsimd", lambda e: e.nop(), waits=[(dP, dP.n)])

        b.replay(E(nc.Block()))

    return nc


def _blocked_lhsT(W):
    o, i = W.shape
    return np.ascontiguousarray(
        W.reshape(o // 128, 128, i // 128, 128).transpose(0, 3, 2, 1)
        .reshape(o, i))


def _bias_cols(v):
    return np.ascontiguousarray(v.reshape(NDC, 128).T)


def _mask2():
    m = np.zeros((128, 128), np.float32)
    tri = np.triu(np.ones((CH, CH), np.float32))
    m[0:CH, 0:CH] = tri
    m[CH:128, CH:128] = tri
    return m


def host_inputs_forward(core, x_full, w12ag, w3ag, attn):
    xs = x_full.reshape(8192, DM)[core * TOK:(core + 1) * TOK, :]
    m = {
        "x_in": np.ascontiguousarray(xs.T),
        "wq": _blocked_lhsT(attn["q_w"]), "wk": _blocked_lhsT(attn["k_w"]),
        "wv": _blocked_lhsT(attn["v_w"]), "wog": _blocked_lhsT(attn["og_w"]),
        "wo": _blocked_lhsT(attn["o_w"]),
        "wbt": np.ascontiguousarray(attn["b_w"].T),
        "qb": _bias_cols(attn["q_b"]), "kb": _bias_cols(attn["k_b"]),
        "vb": _bias_cols(attn["v_b"]), "ob": _bias_cols(attn["o_b"]),
        "bb": attn["b_b"].reshape(1, NH),
        "cmask": _mask2(),
        "cones_col": np.ones((128, 1), bfdt),
        "cones_row": np.ones((1, 128), np.float32),
        "cident": np.eye(128, dtype=bfdt),
    }
    if w12ag is not None:
        m["wag12"] = w12ag
        m["wag3"] = w3ag
    return m


def ag_layout_w12(w1deq, w2deq):
    out = np.empty((8 * W12_SH, DM), bfdt)
    for r in range(4):
        for src, base in ((w1deq, 0), (w2deq, 4)):
            sh = src[W12_SH * r:W12_SH * (r + 1), :]
            t = sh.T
            for fc in range(W12_FC):
                blk = np.empty((128, DM), bfdt)
                for dc in range(NDC):
                    blk[:, dc * 128:(dc + 1) * 128] = \
                        t[dc * 128:(dc + 1) * 128, fc * 128:(fc + 1) * 128]
                out[(fc * 8 + base + r) * 128:(fc * 8 + base + r) * 128 + 128,
                    :] = blk
    return out


def _ag_layout_w12_old(w1deq, w2deq):
    out = np.empty((8 * W12_SH, DM), bfdt)
    for r in range(4):
        for src, base in ((w1deq, 0), (w2deq, 4)):
            sh = src[W12_SH * r:W12_SH * (r + 1), :]
            t = sh.T
            blk = t.reshape(NDC, 128, W12_FC, 128)
            d = np.empty((W12_SH, DM), bfdt)
            for fc in range(W12_FC):
                for dc in range(NDC):
                    d[fc * 128:(fc + 1) * 128, dc * 128:(dc + 1) * 128] = \
                        t[dc * 128:(dc + 1) * 128, fc * 128:(fc + 1) * 128]
            out[(base + r) * W12_SH:(base + r + 1) * W12_SH, :] = d
    return out


def ag_layout_w3(w3deq):
    out = np.empty((8 * W3_SH, DFF), bfdt)
    t = w3deq.T
    for r in range(8):
        for dcl in range(2):
            dglob = r * 256 + dcl * 128
            blk = np.empty((128, DFF), bfdt)
            for fc in range(NFC):
                blk[:, fc * 128:(fc + 1) * 128] = \
                    t[fc * 128:(fc + 1) * 128, dglob:dglob + 128]
            out[(dcl * 8 + r) * 128:(dcl * 8 + r) * 128 + 128, :] = blk
    return out


def _ag_layout_w3_old(w3deq):
    out = np.empty((8 * W3_SH, DFF), bfdt)
    t = w3deq.T
    for r in range(8):
        blk = np.empty((W3_SH, DFF), bfdt)
        for dcl in range(2):
            dglob = r * 256 + dcl * 128
            for fc in range(NFC):
                blk[dcl * 128:(dcl + 1) * 128, fc * 128:(fc + 1) * 128] = \
                    t[fc * 128:(fc + 1) * 128, dglob:dglob + 128]
        out[r * W3_SH:(r + 1) * W3_SH, :] = blk
    return out




GAP = [NF4[j + 1] - NF4[j] for j in range(15)]


def build_quant(fused=False):
    nc = bass.Bass("TRN2", target_bir_lowering=False)
    wfull = nc.dram_tensor("wfull", [DFF, DM], BF16, kind="ExternalInput").ap()
    w3full = nc.dram_tensor("w3full", [DM, DFF], BF16, kind="ExternalInput").ap()
    rw12 = nc.dram_tensor("rw12", [W12_SH, DM], BF16, kind="ExternalInput").ap()
    rw3 = nc.dram_tensor("rw3", [W3_SH, DFF], BF16, kind="ExternalInput").ap()
    qident = nc.dram_tensor("qident", [128, 128], BF16, kind="ExternalInput").ap()
    crow88 = nc.dram_tensor("crow88", [1, 88], F32, kind="ExternalInput").ap()
    cone88 = nc.dram_tensor("cone88", [88, 1], F32, kind="ExternalInput").ap()
    ag1s = nc.dram_tensor("ag1s", [W12_SH, DM], BF16, kind="ExternalOutput").ap()
    ag2s = nc.dram_tensor("ag2s", [W3_SH, DFF], BF16, kind="ExternalOutput").ap()
    dq12 = nc.dram_tensor("dq12", [W12_SH, 32], BF16).ap()
    rc12 = nc.dram_tensor("rc12", [W12_SH, 32], F32).ap()
    dq3 = nc.dram_tensor("dq3", [W3_SH, 88], BF16).ap()
    rc3 = nc.dram_tensor("rc3", [W3_SH, 88], F32).ap()

    with ExitStack() as ctx:
        E = ctx.enter_context
        qt = E(nc.sbuf_tensor("qt", [88, 2 * 16384], BF16))
        scalf = E(nc.sbuf_tensor("scalf", [88, 2048], F32))
        scalf3 = E(nc.sbuf_tensor("scalf3", [88, 2048], F32))
        sq = E(nc.sbuf_tensor("sq", [88, 8 * 256], F32))
        dqs = E(nc.sbuf_tensor("dqs", [88, 256], BF16))
        cents = E(nc.sbuf_tensor("cents", [88, 256], BF16))
        smalls = E(nc.sbuf_tensor("smalls", [88, 16], F32))
        s11 = E(nc.sbuf_tensor("s11", [1, 8], F32))
        s11b = E(nc.sbuf_tensor("s11b", [1, 2], BF16))
        ident = E(nc.sbuf_tensor("ident", [128, 128], BF16))
        r88 = E(nc.sbuf_tensor("r88", [1, 88], F32))
        c88 = E(nc.sbuf_tensor("c88", [88, 1], F32))
        wrow = E(nc.sbuf_tensor("wrow", [128, 2 * DFF], BF16))
        rbf = E(nc.sbuf_tensor("rbf", [128, DFF], BF16))
        accf = E(nc.sbuf_tensor("accf", [128, DFF], F32))
        tmpf = E(nc.sbuf_tensor("tmpf", [128, DFF], F32))
        wdq = E(nc.sbuf_tensor("wdq", [128, DFF], BF16))
        slab = E(nc.sbuf_tensor("slab", [128, DFF], BF16))
        dqrow = E(nc.sbuf_tensor("dqrow", [128, 2 * 88], BF16))
        rcrow = E(nc.sbuf_tensor("rcrow", [128, 2 * 88], F32))
        PM = E(nc.psum_tensor("PM", [128, 512], F32))
        PT = E(nc.psum_tensor("PT", [128, 2048], BF16))

        b = B(nc)
        semnames = ("dC", "sQ0", "sQ1", "sR0", "sR1", "dQW", "dO",
                    "eT", "eV", "eA", "eP")
        sem = {n: Sem(E(nc.semaphore(n)), n) for n in semnames}
        dC, dQW, dO = sem["dC"], sem["dQW"], sem["dO"]
        sQ = (sem["sQ0"], sem["sQ1"])
        sR = (sem["sR0"], sem["sR1"])
        eT, eV, eA, eP = sem["eT"], sem["eV"], sem["eA"], sem["eP"]

        def dma(queue, out, in_, s, waits=()):
            return b.emit(queue, lambda e, o=out, i=in_: e.dma_start(out=o, in_=i),
                          waits=waits, inc=s, k=16)

        def mm(out, lhsT, rhs, waits=(), inc=eT):
            return b.emit("tensor",
                          lambda e, o=out, l=lhsT, r=rhs:
                          e.matmul(o, l, r, start=True, stop=True),
                          waits=waits, inc=inc)

        def vop(fn, waits=()):
            return b.emit("vector", fn, waits=waits, inc=eV)

        def pop(fn, waits=()):
            return b.emit("gpsimd", fn, waits=waits, inc=eP)

        dma("sync", ident[:], qident, dC)
        dma("sync", r88[:], crow88, dC)
        dma("sync", c88[:], cone88, dC)
        cbar = [(dC, dC.n)]
        for _e in ENGINES:
            b.emit(_e, lambda e: e.nop(), waits=cbar)

        def fullpass(src, scaldst, t0=None):
            toks = []
            tprev = [t0, t0]
            for t in range(8):
                par = t % 2
                w = [(eV, tprev[par])] if tprev[par] is not None else []
                tl = dma("sync", qt[:, par * 16384:(par + 1) * 16384],
                         src.rearrange("a bx -> (a bx)")
                         .rearrange("(t p f) -> t p f", t=8, p=88)[t, :, :],
                         sQ[par], waits=w)
                tv = vop(lambda e, par=par, t=t: e.tensor_reduce(
                    scaldst[:, t * 256:(t + 1) * 256],
                    qt[:, par * 16384:(par + 1) * 16384]
                    .rearrange("p (c z) -> p c z", z=64),
                    axis=mybir.AxisListType.X, op=ALU.max,
                    apply_absolute_value=True), waits=[(sQ[par], tl)])
                tprev[par] = tv
                toks.append(tv)
            return toks[-1], toks

        tf12, _ = fullpass(wfull, scalf)
        tf3, _ = fullpass(w3full, scalf3, t0=tf12)

        def mean_of(scaldst, tfull, mcol_off, s11_off):
            rs = smalls[:, mcol_off + 1:mcol_off + 2]
            t1 = vop(lambda e: e.tensor_reduce(rs, scaldst[:],
                                               axis=mybir.AxisListType.X,
                                               op=ALU.add), waits=[(eV, tfull)])
            t2 = mm(PM[0:1, 0:1], c88[:], rs, waits=[(eV, t1)])
            t3 = vop(lambda e: e.tensor_scalar(s11[0:1, s11_off:s11_off + 1],
                                               PM[0:1, 0:1], 1.0 / 180224.0, None,
                                               op0=ALU.mult), waits=[(eT, t2)])
            t4 = vop(lambda e: e.tensor_copy(s11b[0:1, s11_off:s11_off + 1],
                                             s11[0:1, s11_off:s11_off + 1]),
                     waits=[(eV, t3)])
            t5 = vop(lambda e: e.tensor_copy(s11[0:1, 4 + s11_off:5 + s11_off],
                                             s11b[0:1, s11_off:s11_off + 1]),
                     waits=[(eV, t4)])
            t6 = mm(PM[0:88, 1:2], r88[:], s11[0:1, 4 + s11_off:5 + s11_off],
                    waits=[(eV, t5)])
            t7 = vop(lambda e: e.tensor_copy(smalls[:, mcol_off:mcol_off + 1],
                                             PM[0:88, 1:2]), waits=[(eT, t6)])
            return t7

        tm12 = mean_of(scalf, tf12, 0, 0)
        tm3 = mean_of(scalf3, tf3, 2, 1)

        def scaler_stage(rsrc, ntiles, mcol_off, tmean, dq_d, rc_d, tprevload):
            mcol = smalls[:, mcol_off:mcol_off + 1]
            S = lambda a: sq[:, a * 256:(a + 1) * 256]
            scal, yv, tv, Fv, bb2, rcs = S(0), S(1), S(2), S(3), S(4), S(5)
            m1 = smalls[:, 4:5]
            qf = smalls[:, 5:6]
            rqf = smalls[:, 6:7]
            tlast = tprevload
            tEnd = None
            for st in range(ntiles):
                par = st % 2
                w = [(eV, tlast)] if tlast is not None else []
                tl = dma("sync", qt[:, par * 16384:(par + 1) * 16384],
                         rsrc.rearrange("a bx -> (a bx)")
                         .rearrange("(t p f) -> t p f", t=ntiles, p=88)[st, :, :],
                         sQ[par], waits=w)
                t1 = vop(lambda e, par=par: e.tensor_reduce(
                    scal, qt[:, par * 16384:(par + 1) * 16384]
                    .rearrange("p (c z) -> p c z", z=64),
                    axis=mybir.AxisListType.X, op=ALU.max,
                    apply_absolute_value=True),
                    waits=[(sQ[par], tl), (dQW, dQW.n)])
                t2 = vop(lambda e: e.tensor_scalar(cents[:], scal, mcol, None,
                                                   op0=ALU.subtract),
                         waits=[(eV, t1), (eV, tmean)])
                t3 = vop(lambda e: e.tensor_reduce(m1, cents[:],
                                                   axis=mybir.AxisListType.X,
                                                   op=ALU.max,
                                                   apply_absolute_value=True),
                         waits=[(eV, t2)])
                t4 = vop(lambda e: e.reciprocal(qf, m1), waits=[(eV, t3)])
                t5 = vop(lambda e: e.tensor_scalar(qf, qf, 128.0, None,
                                                   op0=ALU.mult), waits=[(eV, t4)])
                t6 = vop(lambda e: e.tensor_scalar(yv, cents[:], qf, None,
                                                   op0=ALU.mult), waits=[(eV, t5)])
                t7 = vop(lambda e: e.tensor_scalar(tv, yv, 128.5, None,
                                                   op0=ALU.add), waits=[(eV, t6)])
                t8 = vop(lambda e: e.memset(Fv, 0.0), waits=[(eV, t7)])
                tk = t8
                for k in range(8, -1, -1):
                    ka = vop(lambda e, k=k: e.tensor_scalar(
                        bb2, tv, float(2 ** k), None, op0=ALU.is_ge),
                        waits=[(eV, tk)])
                    kb = vop(lambda e, k=k: e.scalar_tensor_tensor(
                        tv, bb2, float(-(2 ** k)), tv, op0=ALU.mult, op1=ALU.add),
                        waits=[(eV, ka)])
                    tk = vop(lambda e, k=k: e.scalar_tensor_tensor(
                        Fv, bb2, float(2 ** k), Fv, op0=ALU.mult, op1=ALU.add),
                        waits=[(eV, kb)])
                t9 = vop(lambda e: e.tensor_scalar(Fv, Fv, -128.0, 127.0,
                                                   op0=ALU.add, op1=ALU.min),
                         waits=[(eV, tk)])
                t10 = vop(lambda e: e.tensor_scalar(Fv, Fv, -128.0, None,
                                                    op0=ALU.max), waits=[(eV, t9)])
                t11 = vop(lambda e: e.reciprocal(rqf, qf), waits=[(eV, t10)])
                t12 = vop(lambda e: e.tensor_scalar(dqs[:], Fv, rqf, None,
                                                    op0=ALU.mult), waits=[(eV, t11)])
                t13 = vop(lambda e: e.tensor_scalar(dqs[:], dqs[:], mcol, None,
                                                    op0=ALU.add), waits=[(eV, t12)])
                t14 = vop(lambda e: e.reciprocal(rcs, scal), waits=[(eV, t13)])
                nrows = 16384 // (dq_d.shape[1] * 64)
                td1 = dma("gpsimd", dq_d.rearrange("a bx -> (a bx)")
                          [st * 22528:(st + 1) * 22528]
                          .rearrange("(p c) -> p c", p=88), dqs[:], dQW,
                          waits=[(eV, t13)])
                td2 = dma("gpsimd", rc_d.rearrange("a bx -> (a bx)")
                          [st * 22528:(st + 1) * 22528]
                          .rearrange("(p c) -> p c", p=88), rcs, dQW,
                          waits=[(eV, t14)])
                tlast = t14
                tEnd = t14
            return tEnd, dQW.n

        tq12, _ = scaler_stage(rw12, 2, 0, tm12, dq12, rc12, tf3)
        tq3, ndqw = scaler_stage(rw3, 1, 2, tm3, dq3, rc3, tq12)

        def row_stage(rsrc, nrt, width, dq_d, rc_d, agdst, t_prev, s_prev=None):
            nbl = width // 64
            ndc = width // 128
            tev_bank = [t_prev, t_prev]
            tstores = [s_prev] if s_prev else []
            tlw = [t_prev, t_prev]
            for t in range(nrt):
                par = t % 2
                w = [(dQW, ndqw)]
                if tlw[par] is not None:
                    w.append((eV, tlw[par]))
                tl1 = dma("sync", wrow[:, par * width:(par + 1) * width],
                          rsrc[128 * t:128 * (t + 1), :], sR[par], waits=w)
                tl2 = dma("sync", dqrow[:, par * nbl:(par + 1) * nbl],
                          dq_d[128 * t:128 * (t + 1), :], sR[par])
                tl3 = dma("sync", rcrow[:, par * nbl:(par + 1) * nbl],
                          rc_d[128 * t:128 * (t + 1), :], sR[par])
                tok = sR[par].n
                t1 = vop(lambda e, par=par: e.tensor_tensor(
                    rbf[:, 0:width].rearrange("p (c z) -> p c z", z=64),
                    wrow[:, par * width:(par + 1) * width]
                    .rearrange("p (c z) -> p c z", z=64),
                    rcrow[:, par * nbl:(par + 1) * nbl].unsqueeze(2)
                    .broadcast_to([128, nbl, 64]), op=ALU.mult),
                    waits=[(sR[par], tok)])
                t2 = vop(lambda e: e.memset(accf[:, 0:width], float(NF4[0])),
                         waits=[(eV, t1)])
                tk = t2
                for j in range(15):
                    ja = vop(lambda e, j=j: e.tensor_scalar(
                        tmpf[:, 0:width], rbf[:, 0:width], MID[j], GAP[j],
                        op0=ALU.is_gt, op1=ALU.mult), waits=[(eV, tk)])
                    tk = vop(lambda e, j=j: e.tensor_tensor(
                        accf[:, 0:width], accf[:, 0:width], tmpf[:, 0:width],
                        op=ALU.add), waits=[(eV, ja)])
                t3 = vop(lambda e, par=par: e.tensor_tensor(
                    wdq[:, 0:width].rearrange("p (c z) -> p c z", z=64),
                    accf[:, 0:width].rearrange("p (c z) -> p c z", z=64),
                    dqrow[:, par * nbl:(par + 1) * nbl].unsqueeze(2)
                    .broadcast_to([128, nbl, 64]), op=ALU.mult),
                    waits=[(eV, tk)])
                tlw[par] = t3
                tev = None
                first_evac = True
                for dc in range(ndc):
                    bank = dc % 2
                    slot = bank * 1024
                    w2 = [(eV, t3)]
                    if tev_bank[bank] is not None:
                        w2.append((eV, tev_bank[bank]))
                    tt_ = b.emit("tensor", lambda e, dc=dc, slot=slot: e.transpose(
                        PT[:, slot:slot + 128],
                        wdq[:, dc * 128:(dc + 1) * 128], ident[:]),
                        waits=w2, inc=eT)
                    w3_ = [(eT, tt_)]
                    if first_evac and tstores:
                        w3_.append((dO, tstores[-1]))
                        first_evac = False
                    tev = vop(lambda e, dc=dc, slot=slot: e.tensor_copy(
                        slab[:, dc * 128:(dc + 1) * 128], PT[:, slot:slot + 128]),
                        waits=w3_)
                    tev_bank[bank] = tev
                ts_ = dma("gpsimd", agdst[128 * t:128 * (t + 1), :],
                          slab[:, 0:width], dO, waits=[(eV, tev)])
                tstores.append(ts_)
            return tstores

        st1 = row_stage(rw12, W12_FC, DM, dq12, rc12, ag1s, tq3)
        st2 = row_stage(rw3, 2, DFF, dq3, rc3, ag2s, eV.n, s_prev=st1[-1])
        b.emit("gpsimd", lambda e: e.nop(), waits=[(dO, dO.n)])
        b.replay(E(nc.Block()))
    return nc


def host_inputs_quant(core, w1, w2, w3):
    wfull = w1 if core < 4 else w2
    r = core % 4
    return {
        "wfull": np.ascontiguousarray(wfull),
        "w3full": np.ascontiguousarray(w3),
        "rw12": np.ascontiguousarray(wfull[W12_SH * r:W12_SH * (r + 1), :]),
        "rw3": np.ascontiguousarray(w3[W3_SH * core:W3_SH * (core + 1), :]),
        "qident": np.eye(128, dtype=bfdt),
        "crow88": np.ones((1, 88), np.float32),
        "cone88": np.ones((88, 1), np.float32),
    }


FUSED = True


def kernel(**inputs):
    x = np.asarray(inputs["L0_attn_hidden_states"])
    w1 = np.ascontiguousarray(np.asarray(inputs["L0_ffn_weight1"]))
    w2 = np.ascontiguousarray(np.asarray(inputs["L0_ffn_weight2"]))
    w3 = np.ascontiguousarray(np.asarray(inputs["L0_ffn_weight3"]))
    attn = {k: np.ascontiguousarray(np.asarray(inputs["L0_attn_" + k]))
            for k in ("q_w", "q_b", "k_w", "k_b", "v_w", "v_b", "og_w",
                      "o_w", "o_b", "b_w", "b_b")}

    if FUSED:
        ncf = build_forward(stage="full", fused=True)
        fmaps = []
        for c in range(NCORES):
            m = host_inputs_forward(c, x, None, None, attn)
            q = host_inputs_quant(c, w1, w2, w3)
            for k in ("qident", "wfull", "w3full"):
                del q[k]
            m.update(q)
            m12 = np.zeros((1, 4), np.float32)
            m12[0, 0 if c < 4 else 1] = 1.0
            m3 = np.zeros((1, 4), np.float32)
            m3[0, 2] = 1.0
            m["cm12"] = m12
            m["cm3"] = m3
            m["cnmid"] = np.tile(-np.array(MID, np.float32), (128, 1))
            fmaps.append(m)
        resf = run_bass_kernel_spmd(ncf, fmaps, core_ids=list(range(NCORES)))
    else:
        ncq = build_quant()
        qmaps = [host_inputs_quant(c, w1, w2, w3) for c in range(NCORES)]
        resq = run_bass_kernel_spmd(ncq, qmaps, core_ids=list(range(NCORES)))
        a1 = np.stack([np.asarray(resq.results[c]["ag1s"])
                       for c in range(NCORES)])
        w12ag = np.ascontiguousarray(
            a1.reshape(8, W12_FC, 128, DM).transpose(1, 0, 2, 3)
            .reshape(8 * W12_SH, DM))
        a2 = np.stack([np.asarray(resq.results[c]["ag2s"])
                       for c in range(NCORES)])
        w3ag = np.ascontiguousarray(
            a2.reshape(8, 2, 128, DFF).transpose(1, 0, 2, 3)
            .reshape(8 * W3_SH, DFF))
        ncf = build_forward(stage="full")
        fmaps = [host_inputs_forward(c, x, w12ag, w3ag, attn)
                 for c in range(NCORES)]
        resf = run_bass_kernel_spmd(ncf, fmaps, core_ids=list(range(NCORES)))
    out = np.concatenate([np.asarray(resf.results[c]["y_out"]).T
                          for c in range(NCORES)], axis=0)
    return np.ascontiguousarray(out.reshape(2, 4096, DM))
